# version 02
# baseline (speedup 1.0000x reference)
import contextlib
import numpy as np
import ml_dtypes
import concourse.bass as bass
import concourse.mybir as mybir
from concourse.bass_utils import run_bass_kernel_spmd

F32 = mybir.dt.float32
F16 = mybir.dt.float16
BF16 = mybir.dt.bfloat16
I32 = mybir.dt.int32
AF = mybir.ActivationFunctionType
OP = mybir.AluOpType
AX = mybir.AxisListType

T = 2048
D = 1024
NCH = 8
DH = 4096
HEADS = 16
HD = 64
CC = 64
NDC = 32
LN_EPS = 1e-5
WA = 256
WB = 256
WF = 128

STAGE = 4
DBG = False
DBGCH = 0
BSTRIPS = None
BCUT = None
DCUT = None
BAR = None


def _hilo(a64):
    hi = a64.astype(np.float32)
    lo = (a64 - hi.astype(np.float64)).astype(np.float32)
    return hi, lo


def build_constants():
    c = {}
    c["IDT"] = np.eye(128, dtype=np.float32)
    c["IDTB"] = np.eye(128, dtype=np.float32)
    c["ONES"] = np.ones((128, 128), dtype=np.float32)
    s16 = np.zeros((128, NCH * 16), dtype=np.float32)
    s16t = np.zeros((16, NCH * 128), dtype=np.float32)
    for ch in range(NCH):
        for p in range(128):
            h = 2 * ch + (1 if p >= 64 else 0)
            s16[p, ch * 16 + h] = 1.0
            s16t[h, ch * 128 + p] = 1.0
    c["S16"] = s16
    c["S16T"] = s16t
    tril = np.tril(np.ones((CC, CC)), -1)
    c["NMASK"] = np.tile(-tril, (1, 2)).astype(np.float32)
    c["NMASKT"] = np.tile(-tril.T, (1, 2)).astype(np.float32)
    c["I64S"] = np.tile(np.eye(CC), (1, 2)).astype(np.float32)
    a_all = np.zeros((NDC, CC))
    w_all = np.zeros((NDC, CC))
    c1_all = np.zeros(NDC)
    c2_all = np.zeros(NDC)
    tmT_all = np.zeros((NDC, CC, CC))
    Cf = float(CC)
    for i in range(NDC):
        ts = float(i * CC)
        idx = np.arange(1, CC + 1, dtype=np.float64) + ts
        cs = np.cumsum(idx)
        weighted_idx = idx + idx.sum() - cs
        w_all[i] = weighted_idx / ((1 + ts + Cf) * (ts + Cf) / 2)
        c1_all[i] = (ts / (ts + Cf)) * ((1 + ts) / (1 + ts + Cf)) if i > 0 else 0.0
        c2_all[i] = (Cf / (ts + Cf)) * ((ts * 2 + 1 + Cf) / (1 + ts + Cf))
        tri0 = (1 + ts) * ts / 2
        a_all[i] = tri0 / (tri0 + cs)
        r = cs[:, None]
        j_idx = (cs - idx)[None, :]
        tmT_all[i] = np.where(r >= j_idx, (r - j_idx) / (tri0 + r), 0.0).T
    ab = np.tile(a_all.reshape(1, NDC * CC), (128, 1))
    c["AB_hi"], c["AB_lo"] = _hilo(ab)
    wcol = np.concatenate([w_all.T, w_all.T], axis=0)
    c["WCOL_hi"], c["WCOL_lo"] = _hilo(wcol)
    c["C1_hi"], c["C1_lo"] = _hilo(np.tile(c1_all[None, :], (128, 1)))
    c["C2_hi"], c["C2_lo"] = _hilo(np.tile(c2_all[None, :], (128, 1)))
    tmt = np.zeros((64, NDC * CC))
    for i in range(NDC):
        tmt[:, i * CC:(i + 1) * CC] = tmT_all[i]
    c["TMT_hi"], c["TMT_lo"] = _hilo(tmt)
    return c



class Sem:
    CAP = 1_000_000_000

    def __init__(self, nc, stack, name):
        self.nc, self.stack, self.name = nc, stack, name
        self.idx = 0
        self.h = stack.enter_context(nc.semaphore(f"{name}0"))
        self.n = 0

    def add(self, k):
        if self.n + k > self.CAP:
            self.idx += 1
            self.h = self.stack.enter_context(self.nc.semaphore(f"{self.name}{self.idx}"))
            self.n = 0
        self.n += k
        return (self.h, self.n)

    def cur(self):
        return (self.h, self.n)


class Prog:
    ENGS = ("t", "v", "s", "g", "y")

    def __init__(self, nc, stack):
        self.nc = nc
        self.stack = stack
        self.ops = {e: [] for e in self.ENGS}
        self.sem = {e: Sem(nc, stack, f"c{e}") for e in self.ENGS}
        self.dmas = {q: Sem(nc, stack, f"d{q}") for q in ("y", "s", "g")}
        self.waited = {e: {} for e in self.ENGS}
        self._nsem = [0]

    def new_sem(self, name=None):
        self._nsem[0] += 1
        return Sem(self.nc, self.stack, name or f"x{self._nsem[0]}")

    def wait(self, e, hv):
        if hv is None:
            return
        h, v = hv
        seen = self.waited[e].get(id(h), -1)
        if v <= seen:
            return
        self.waited[e][id(h)] = v
        self.ops[e].append(lambda eng, h=h, v=v: eng.wait_ge(h, v))

    def op(self, e, fn, drain=True):
        hv = self.sem[e].add(1)
        def run(eng, fn=fn, hv=hv, drain=drain):
            fn(eng).then_inc(hv[0], 1)
            if drain:
                eng.drain()
        self.ops[e].append(run)
        return hv

    def top(self, fn):
        hv = self.sem["t"].add(1)
        self.ops["t"].append(lambda eng, fn=fn, hv=hv: fn(eng).then_inc(hv[0], 1))
        return hv

    def dma(self, q, dst, src, slow=False, sem=None):
        hv = (sem if sem is not None else self.dmas[q]).add(16)
        if slow:
            self.ops[q].append(lambda eng, d=dst, s=src, hv=hv: eng.dma_start(
                d, s, allow_slow_non_contiguous=True).then_inc(hv[0], 16))
        else:
            self.ops[q].append(lambda eng, d=dst, s=src, hv=hv: eng.dma_start(d, s).then_inc(hv[0], 16))
        return hv

    def emit(self, block):
        decs = {"t": block.tensor, "v": block.vector, "s": block.scalar,
                "g": block.gpsimd, "y": block.sync}
        for e in self.ENGS:
            ops = self.ops[e]
            @decs[e]
            def _(eng, ops=ops):
                for fn in ops:
                    fn(eng)


class Banks:
    def __init__(self, nc, stack):
        self.b = [stack.enter_context(nc.psum_tensor(f"bank{i}", [128, 512], F32))
                  for i in range(8)]
        self.cons = [[] for _ in range(8)]
        self.prod = [None] * 8

    def acquire(self, p, i):
        for hv in self.cons[i]:
            p.wait("t", hv)
        self.cons[i] = []
        return self.b[i]

    def produced(self, p, i, hv):
        self.prod[i] = hv

    def consume(self, p, i, e):
        if self.prod[i] is not None:
            p.wait(e, self.prod[i])

    def consumed(self, p, i, hv):
        self.cons[i].append(hv)



def build_program(stage=None):
    if stage is None:
        stage = STAGE
    stack = contextlib.ExitStack()
    nc = bass.Bass("TRN2", target_bir_lowering=False)

    dx = nc.dram_tensor("x_in", [T, D], F32, kind="ExternalInput").ap()
    dy = nc.dram_tensor("y_out", [T, D], F32, kind="ExternalOutput").ap()
    ddbg = [nc.dram_tensor(f"dbg{i}", [128, 2048], F32, kind="ExternalOutput").ap()
            for i in range(6)]
    dbg_done = [False]
    def dump_dbg(aps):
        if dbg_done[0]:
            return
        dbg_done[0] = True
        p.wait("g", p.sem["v"].cur())
        p.wait("g", p.sem["s"].cur())
        hv = None
        sem_dbg = p.new_sem("dbg")
        for i, ap in aps:
            n = int(ap.ap[-1][1])
            np_ = int(ap.ap[0][1])
            hv = p.dma("g", ddbg[i][:np_, 0:n], ap, sem=sem_dbg)
        p.wait("v", hv)
        p.wait("s", hv)
        p.wait("t", hv)

    din = {}
    for name, shape in [
        ("aWqkv", [3 * D, D]), ("abqkv", [3 * D]), ("aWproj", [D, D]), ("abproj", [D]),
        ("ascale", [1]),
        ("bWk", [D, D]), ("bbk", [D]), ("bWq", [D, D]), ("bbq", [D]),
        ("bWv", [D, D]), ("bbv", [D]), ("bWbeta", [HEADS, D]), ("bbbeta", [HEADS]),
        ("bconvk", [D, 4]), ("bconvq", [D, 4]), ("bconvv", [D, 4]),
        ("brms", [HD]), ("bWout", [D, D]), ("bbout", [D]),
        ("fw1", [DH, D]), ("fconvw", [DH, 3]), ("fconvb", [DH]), ("fw2", [D, DH]),
    ]:
        din[name] = nc.dram_tensor(name, shape, F32, kind="ExternalInput").ap()

    CONSTS = build_constants()
    dcon = {}
    for k, v in CONSTS.items():
        dt = BF16 if k == "IDTB" else F32
        if k == "IDTB":
            dcon[k] = nc.dram_tensor("c_" + k, list(v.shape), BF16, kind="ExternalInput").ap()
        else:
            dcon[k] = nc.dram_tensor("c_" + k, list(v.shape), F32, kind="ExternalInput").ap()

    scr = {}
    scr["qkv"] = nc.dram_tensor("s_qkv", [24, 8, 128, 128], F32, kind="Internal").ap()
    scr["proj"] = nc.dram_tensor("s_proj", [8, 8, 128, 128], F32, kind="Internal").ap()
    for w in ("k", "q", "v", "out"):
        scr[w] = nc.dram_tensor(f"s_{w}", [8, 8, 128, 128], F32, kind="Internal").ap()
    scr["t1"] = nc.dram_tensor("s_t1", [32, 128, 1024], BF16, kind="Internal").ap()
    scr["t2"] = nc.dram_tensor("s_t2", [32, 8, 128, 128], BF16, kind="Internal").ap()

    cur = [16512]
    offs = {}

    def P_(name, shape, dtype=F32, at=None, alias=None):
        save = None
        if alias is not None:
            save = cur[0]
            cur[0] = offs[alias]
        elif at is not None:
            cur[0] = at
        nbytes = int(np.prod(shape[1:])) * mybir.dt.size(dtype)
        off = (cur[0] + 31) // 32 * 32
        h = nc.alloc_sbuf_tensor_at(name, shape, dtype, offset=off)
        offs[name] = off
        cur[0] = off + nbytes
        if save is not None and save > cur[0]:
            cur[0] = save
        return h

    xfm = P_("xfm", [128, NCH * T])
    cIDT = P_("cIDT", [128, 128])
    cIDTB = P_("cIDTB", [128, 128], BF16)
    cONES = P_("cONES", [128, 128])
    cS16 = P_("cS16", [128, NCH * 16])
    cS16T = P_("cS16T", [16, NCH * 128])
    cNMASK = P_("cNMASK", [64, 2 * CC])
    cNMASKT = P_("cNMASKT", [64, 2 * CC])
    cI64S = P_("cI64S", [64, 2 * CC])
    cAB = P_("cAB", [128, NDC * CC])
    cTMT = P_("cTMT", [64, NDC * CC])
    cWCOL = P_("cWCOL", [128, NDC])
    cC1 = P_("cC1", [128, NDC])
    cC2 = P_("cC2", [128, NDC])
    bias_qkv = P_("b_qkv", [128, 24])
    bias_proj = P_("b_proj", [128, 8])
    bias_k = P_("b_k", [128, 8]); bias_q = P_("b_q", [128, 8])
    bias_v = P_("b_v", [128, 8]); bias_out = P_("b_out", [128, 8])
    bias_beta = P_("b_beta", [16, 1])
    convk = P_("convk", [128, NCH * 4]); convq = P_("convq", [128, NCH * 4])
    convv = P_("convv", [128, NCH * 4])
    fcw = P_("fcw", [128, 32 * 3]); fcb = P_("fcb", [128, 32])
    rmsw = P_("rmsw", [128, 1])
    scale_a = P_("scale_a", [128, 1])
    cw1t = P_("cw1t", [128, 1]); cw1r = P_("cw1r", [128, 1])
    cw2t = P_("cw2t", [128, 1]); cw2r = P_("cw2r", [128, 1])
    cWbT = P_("cWbT", [128, NCH * 16])
    carry = P_("carry", [128, NCH * 2])
    PERM_END = cur[0]
    AR = (PERM_END + 63) // 64 * 64
    assert AR < 110500, f"perm too big: {AR}"

    wnat = P_("wnat", [128, 2 * DH], at=AR)
    pstage = P_("pstage", [128, 8 * 128])
    pstageb = P_("pstageb", [128, 8 * 128], BF16)
    pval = P_("pval", [128, 512])
    pint = P_("pint", [128, 512], I32)
    pf = P_("pf", [128, 512])
    pd = P_("pd", [128, 512])
    xld = P_("xld", [128, 2 * 1024])
    lnA = P_("lnA", [128, NCH * WA], at=AR)
    qkvA = P_("qkvA", [128, 24 * WA])
    kvA = P_("kvA", [128, NCH * WA])
    kcA = P_("kcA", [128, NCH * WA])
    wstrA = P_("wstrA", [128, 2 * 1024])
    smallA = P_("smallA", [128, 1024])
    stripA = P_("stripA", [128, 1024])
    mbA = P_("mbA", [128, WA])
    sbA = P_("sbA", [128, WA])
    czA = P_("czA", [128, WA])
    aintA = P_("aintA", [128, 512], I32)
    t1T = P_("t1T", [128, 32 * 1024], BF16, at=AR)
    hbuf = P_("hbuf", [128, 32 * 131])
    hc8 = P_("hc8", [128, 8 * 132])
    hsave = P_("hsave", [128, 32])
    m1b = P_("m1b", [128, 8 * WF], BF16)
    m2b = P_("m2b", [128, 32 * 256], BF16)
    t2buf = P_("t2buf", [128, 8 * 128], BF16)
    fsc = P_("fsc", [128, 8 * 128])
    fstats = P_("fstats", [128, 1024], alias="hc8")
    fval = P_("fval", [128, 512]); fint = P_("fint", [128, 512], I32)
    ff = P_("ff", [128, 512]); fd = P_("fd", [128, 512])
    famx = P_("famx", [128, 8 * 129], alias="fval")
    fsm = P_("fsm", [128, 384])
    FF_END = cur[0]
    assert FF_END <= 229344, f"ffn map too big: {FF_END}"
    lnB = P_("lnB", [128, NCH * WB], at=AR)
    bk0 = P_("bk0", [128, NCH * 264])
    bq0 = P_("bq0", [128, NCH * 264])
    bv0 = P_("bv0", [128, NCH * 264])
    bkh = P_("bkh", [128, NCH * WB])
    bfm = P_("bfmq", [128, NCH * WB])
    betab = P_("betab", [128, NCH * WB])
    nbfm = P_("nbfm", [128, NCH * WB])
    wstrB = P_("wstrB", [128, 2 * 1024])
    smallB = P_("smallB", [128, 1024])
    stripB = P_("stripB", [128, 1024])
    mbB = P_("mbB", [128, WB]); sbB = P_("sbB", [128, WB])
    leadsv = P_("leadsv", [128, NCH * 3 * 3])
    dW = P_("dW", [128, NCH * CC])
    dWavg = P_("dWavg", [128, NCH * CC])
    dPows = P_("dPows", [64, 12 * 2 * CC])
    dP = P_("dP", [64, 2 * CC])
    dG = P_("dG", [64, 2 * CC])
    dU = P_("dU", [64, 2 * CC])
    dUw = P_("dUw", [64, 2 * CC])
    dWmix = P_("dWmix", [128, CC])
    dT1 = P_("dT1", [64, 2 * CC])
    dT2 = P_("dT2", [128, CC])
    dKts = P_("dKts", [64, 2 * CC])
    dCmb = P_("dCmb", [128, CC])
    AB_END = cur[0]
    assert AB_END <= 229344, f"attn_b map too big: {AB_END}"

    p = Prog(nc, stack)
    bk = Banks(nc, stack)

    def xbar(pid):
        if BAR != pid:
            return
        hv_t, hv_v, hv_s = p.sem["t"].cur(), p.sem["v"].cur(), p.sem["s"].cur()
        for e in ("t", "v", "s", "g", "y"):
            p.wait(e, hv_t)
            p.wait(e, hv_v)
            p.wait(e, hv_s)

    def xc(c, t0, w):
        return xfm.ap()[:, c * T + t0: c * T + t0 + w]

    POW = {"N1": 0, "NS1": 1, "N2": 2, "NS2": 3, "N4": 4, "NS4": 5, "N8": 6,
           "NS8": 7, "N16": 8, "NS16": 9, "N32": 10, "PI": 11}

    def pw(nm):
        i = POW[nm]
        return dPows.ap()[:64, i * 2 * CC:(i + 1) * 2 * CC]

    def pwh(nm, h):
        i = POW[nm]
        return dPows.ap()[:64, i * 2 * CC + h * CC:i * 2 * CC + (h + 1) * CC]

    def transpose_group(tiles, bank_i, evac_fn, evac_engine="s", ident=None):
        idn = ident if ident is not None else cIDT.ap()[:, :]
        b = bk.acquire(p, bank_i)
        hv = None
        for j, src in enumerate(tiles):
            hv = p.top(lambda e, s=src, d=b[:, j * 128:(j + 1) * 128], idn=idn:
                       e.transpose(d, s, idn))
        bk.produced(p, bank_i, hv)
        bk.consume(p, bank_i, evac_engine)
        hv2 = evac_fn(b)
        bk.consumed(p, bank_i, hv2)
        return hv2

    def emit_round(val_ap, int_ap, f_ap, d_ap, out_ap):
        p.wait("s", p.sem["v"].cur())
        p.op("s", lambda e: e.copy(int_ap, val_ap))
        hv = p.op("s", lambda e: e.copy(f_ap, int_ap))
        p.wait("v", hv)
        p.op("v", lambda e: e.tensor_tensor(d_ap, val_ap, f_ap, OP.subtract))
        p.op("v", lambda e: e.scalar_tensor_tensor(f_ap, d_ap, 0.5, f_ap, OP.is_ge, OP.add))
        p.op("v", lambda e: e.tensor_scalar(d_ap, d_ap, -0.5, None, OP.is_le))
        return p.op("v", lambda e: e.tensor_tensor(out_ap, f_ap, d_ap, OP.subtract))

    def emit_rsqrt(out_ap, in_ap, eps, t1_ap, t2_ap, ln_ap):
        p.op("v", lambda e: e.tensor_scalar(t1_ap, in_ap, float(eps), None, OP.add))
        p.wait("s", p.sem["v"].cur())
        p.op("s", lambda e: e.activation(ln_ap, t1_ap, AF.Ln))
        hv = p.op("s", lambda e: e.activation(out_ap, ln_ap, AF.Exp, scale=-0.5))
        p.wait("v", hv)
        p.op("v", lambda e: e.tensor_tensor(t2_ap, out_ap, out_ap, OP.mult))
        p.op("v", lambda e: e.tensor_tensor(t2_ap, t2_ap, t1_ap, OP.mult))
        p.op("v", lambda e: e.tensor_scalar(t2_ap, t2_ap, -0.5, 1.5, OP.mult, OP.add))
        return p.op("v", lambda e: e.tensor_tensor(out_ap, out_ap, t2_ap, OP.mult))

    def emit_sqrt(out_ap, in_ap, t1_ap, t2_ap, ln_ap):
        p.op("v", lambda e: e.tensor_copy(t2_ap, in_ap))
        p.wait("s", p.sem["v"].cur())
        p.op("s", lambda e: e.activation(ln_ap, t2_ap, AF.Ln))
        hv = p.op("s", lambda e: e.activation(out_ap, ln_ap, AF.Exp, scale=0.5))
        p.wait("v", hv)
        p.op("v", lambda e: e.reciprocal(t1_ap, out_ap))
        p.op("v", lambda e: e.tensor_tensor(t1_ap, t1_ap, t2_ap, OP.mult))
        p.op("v", lambda e: e.tensor_tensor(t2_ap, out_ap, t1_ap, OP.add))
        return p.op("v", lambda e: e.tensor_scalar(out_ap, t2_ap, 0.5, None, OP.mult))

    sem_const = p.new_sem("const")
    for name, dst in [("IDT", cIDT), ("IDTB", cIDTB), ("ONES", cONES), ("S16", cS16),
                      ("S16T", cS16T), ("NMASK", cNMASK), ("NMASKT", cNMASKT),
                      ("I64S", cI64S)]:
        sh = dst.shape
        p.dma("y", dst.ap()[:sh[0], :], dcon[name], sem=sem_const)
    prev_add = None
    for hi, lo, dst in [("AB_hi", "AB_lo", cAB), ("WCOL_hi", "WCOL_lo", cWCOL),
                        ("C1_hi", "C1_lo", cC1), ("C2_hi", "C2_lo", cC2),
                        ("TMT_hi", "TMT_lo", cTMT)]:
        w = dst.shape[1]
        np_ = dst.shape[0]
        if prev_add is not None:
            p.wait("y", prev_add)
        hv0 = p.dma("y", dst.ap()[:, :], dcon[hi], sem=sem_const)
        hv = p.dma("y", wnat.ap()[:np_, :w], dcon[lo], sem=sem_const)
        p.wait("v", hv)
        prev_add = p.op("v", lambda e, d=dst, w=w, np_=np_: e.tensor_tensor(
            d.ap()[:, :], d.ap()[:, :], wnat.ap()[:np_, :w], OP.add))
    p.wait("y", prev_add)
    for nm, dst in [("abqkv", bias_qkv), ("abproj", bias_proj), ("bbk", bias_k),
                    ("bbq", bias_q), ("bbv", bias_v), ("bbout", bias_out)]:
        p.dma("y", dst.ap()[:, :], din[nm].rearrange("(c p) -> p c", p=128), slow=True, sem=sem_const)
    p.dma("y", bias_beta.ap()[:16, :], din["bbbeta"].rearrange("(h o) -> h o", o=1), slow=True, sem=sem_const)
    for nm, dst in [("bconvk", convk), ("bconvq", convq), ("bconvv", convv)]:
        p.dma("y", dst.ap()[:, :].rearrange("p (c k) -> p c k", k=4),
              din[nm].rearrange("(c p) k -> p c k", p=128), slow=True, sem=sem_const)
    p.dma("y", fcw.ap()[:, :].rearrange("p (c k) -> p c k", k=3),
          din["fconvw"].rearrange("(c p) k -> p c k", p=128), slow=True, sem=sem_const)
    p.dma("y", fcb.ap()[:, :], din["fconvb"].rearrange("(c p) -> p c", p=128), slow=True, sem=sem_const)
    p.dma("y", rmsw.ap()[:64, :], din["brms"].rearrange("(d o) -> d o", o=1), slow=True, sem=sem_const)
    p.dma("y", rmsw.ap()[64:128, :], din["brms"].rearrange("(d o) -> d o", o=1), slow=True, sem=sem_const)
    p.dma("y", scale_a.ap()[:1, :], din["ascale"].rearrange("(a b) -> a b", b=1), slow=True, sem=sem_const)
    p.wait("t", sem_const.cur())
    b0 = bk.acquire(p, 0)
    hv = p.top(lambda e: e.matmul(b0[:, 0:1], cONES.ap()[:1, :], scale_a.ap()[:1, :],
                                  start=True, stop=True))
    bk.produced(p, 0, hv)
    bk.consume(p, 0, "v")
    hv = p.op("v", lambda e: e.tensor_copy(scale_a.ap()[:, :], b0[:, 0:1]))
    bk.consumed(p, 0, hv)

    sem_wld = [p.new_sem("wld0"), p.new_sem("wld1")]
    sem_pst = p.new_sem("pst")
    last_tr = [None, None]
    last_wr = [None, None]

    def stage_fp32_weight(dsrc, dscr, MCH):
        for m in range(MCH):
            buf = wnat.ap()[:, (m % 2) * D:(m % 2) * D + D]
            if last_tr[m % 2] is not None:
                p.wait("y", last_tr[m % 2])
            if last_wr[m % 2] is not None:
                p.wait("y", last_wr[m % 2])
            hv = p.dma("y", buf, dsrc[m * 128:(m + 1) * 128, :], sem=sem_wld[m % 2])
            last_wr[m % 2] = hv
            p.wait("t", hv)
            for half in range(2):
                tiles = [buf[:, (half * 4 + k) * 128:(half * 4 + k + 1) * 128] for k in range(4)]
                def evac(b, half=half):
                    return p.op("v", lambda e: e.tensor_copy(
                        pstage.ap()[:, half * 512:(half + 1) * 512], b[:, 0:512]))
                transpose_group(tiles, 2 + half, evac, evac_engine="v")
            last_tr[m % 2] = p.sem["t"].cur()
            hvv = p.sem["v"].cur()
            p.wait("s", hvv)
            hv = p.dma("s", dscr[m, :, :, :].rearrange("k a b -> a k b"),
                       pstage.ap()[:, :].rearrange("a (k b) -> a k b", b=128), sem=sem_pst)
            p.wait("v", hv)

    stage_fp32_weight(din["aWqkv"], scr["qkv"], 24)
    stage_fp32_weight(din["aWproj"], scr["proj"], 8)
    stage_fp32_weight(din["bWk"], scr["k"], 8)
    stage_fp32_weight(din["bWq"], scr["q"], 8)
    stage_fp32_weight(din["bWv"], scr["v"], 8)
    stage_fp32_weight(din["bWout"], scr["out"], 8)

    if last_tr[0] is not None:
        p.wait("y", last_tr[0])
    if last_wr[0] is not None:
        p.wait("y", last_wr[0])
    hv = p.dma("y", wnat.ap()[:16, 0:D], din["bWbeta"], sem=sem_wld[0])
    last_wr[0] = hv
    p.wait("t", hv)
    b4 = bk.acquire(p, 4)
    hv = None
    for k in range(8):
        hv = p.top(lambda e, k=k: e.transpose(b4[:, k * 16:(k + 1) * 16],
                                              wnat.ap()[:16, k * 128:(k + 1) * 128],
                                              cIDT.ap()[:16, :16]))
    bk.produced(p, 4, hv)
    last_tr[0] = hv
    last_tr[1] = hv
    bk.consume(p, 4, "v")
    hv = p.op("v", lambda e: e.tensor_copy(cWbT.ap()[:, :], b4[:, 0:NCH * 16]))
    bk.consumed(p, 4, hv)

    def ffn_stage2(dsrc, MCH, KW, cwt, cwr, dscr, kmajor):
        sem_f = [p.new_sem(), p.new_sem()]
        sem_fs = p.new_sem()
        for g in (last_tr[0], last_tr[1], last_wr[0], last_wr[1]):
            if g is not None:
                p.wait("y", g)
        p.op("v", lambda e: e.memset(pval.ap()[:, 0:1], 0.0))
        prev_red = [last_tr[0], last_tr[1]]
        for m in range(MCH):
            buf = wnat.ap()[:, (m % 2) * KW:(m % 2) * KW + KW]
            if prev_red[m % 2] is not None:
                p.wait("y", prev_red[m % 2])
            if last_wr[m % 2] is not None:
                p.wait("y", last_wr[m % 2])
            hv = p.dma("y", buf, dsrc[m * 128:(m + 1) * 128, :], sem=sem_f[m % 2])
            last_wr[m % 2] = hv
            p.wait("v", hv)
            p.op("v", lambda e, buf=buf: e.tensor_reduce(
                pval.ap()[:, 1:2], buf, AX.X, OP.add, apply_absolute_value=True))
            hvv = p.op("v", lambda e: e.tensor_tensor(pval.ap()[:, 0:1], pval.ap()[:, 0:1],
                                                      pval.ap()[:, 1:2], OP.add))
            if m + 1 < MCH:
                p.wait("y", hvv)
        p.wait("t", p.sem["v"].cur())
        b5 = bk.acquire(p, 5)
        hv = p.top(lambda e: e.matmul(b5[:1, 0:1], cONES.ap()[:, 0:1], pval.ap()[:, 0:1],
                                      start=True, stop=True))
        bk.produced(p, 5, hv)
        bk.consume(p, 5, "v")
        p.op("v", lambda e: e.tensor_scalar(pval.ap()[:1, 2:3], b5[:1, 0:1],
                                            1.0 / (MCH * 128 * KW), 1e-5, OP.mult, OP.max))
        hv = p.op("v", lambda e: e.reciprocal(pval.ap()[:1, 3:4], pval.ap()[:1, 2:3]))
        bk.consumed(p, 5, hv)
        p.wait("t", hv)
        b5 = bk.acquire(p, 5)
        hv = p.top(lambda e: e.matmul(b5[:, 0:2], cONES.ap()[:1, :], pval.ap()[:1, 2:4],
                                      start=True, stop=True))
        bk.produced(p, 5, hv)
        bk.consume(p, 5, "v")
        p.op("v", lambda e: e.tensor_copy(cwt.ap()[:, :], b5[:, 0:1]))
        hv = p.op("v", lambda e: e.tensor_copy(cwr.ap()[:, :], b5[:, 1:2]))
        bk.consumed(p, 5, hv)
        last = None
        hvv_last = p.sem["v"].cur()
        qgate = [hvv_last, hvv_last]
        for m in range(MCH):
            buf = wnat.ap()[:, (m % 2) * KW:(m % 2) * KW + KW]
            if qgate[m % 2] is not None:
                p.wait("y", qgate[m % 2])
            if last_wr[m % 2] is not None:
                p.wait("y", last_wr[m % 2])
            hv = p.dma("y", buf, dsrc[m * 128:(m + 1) * 128, :], sem=sem_f[m % 2])
            last_wr[m % 2] = hv
            p.wait("v", hv)
            p.wait("s", hv)
            for grp in range(KW // 512):
                g0 = grp * 512
                qgate[m % 2] = p.op("v", lambda e, buf=buf, g0=g0: e.tensor_scalar(
                    pval.ap()[:, :], buf[:, g0:g0 + 512], cwr.ap()[:, 0:1], None, OP.mult))
                emit_round(pval.ap()[:, :], pint.ap()[:, :], pf.ap()[:, :], pd.ap()[:, :],
                           pf.ap()[:, :])
                p.op("v", lambda e, g0=g0: e.tensor_scalar(
                    pstage.ap()[:, (g0 % 1024):(g0 % 1024) + 512],
                    pf.ap()[:, :], 1.0, -1.0, OP.min, OP.max))
                if (g0 % 1024) == 512 or KW == 512:
                    blk = g0 // 1024
                    hvq = p.sem["v"].cur()
                    p.wait("t", hvq)
                    for half in range(2):
                        tiles = [pstage.ap()[:, (half * 4 + k) * 128:(half * 4 + k + 1) * 128]
                                 for k in range(4)]
                        def evac(b, half=half):
                            return p.op("v", lambda e, half=half: e.tensor_copy(
                                pstageb.ap()[:, half * 512:(half + 1) * 512], b[:, 0:512]))
                        transpose_group(tiles, 2 + half, evac, evac_engine="v")
                    hvv = p.sem["v"].cur()
                    p.wait("g", hvv)
                    if last is not None:
                        p.wait("g", last)
                    if kmajor:
                        kbase = blk * 8
                        last = p.dma("g", dscr[kbase:kbase + 8, m, :, :].rearrange("k a b -> a k b"),
                                     pstageb.ap()[:, :].rearrange("a (k b) -> a k b", b=128),
                                     sem=sem_fs)
                    else:
                        last = p.dma("g", dscr[m, :, :], pstageb.ap()[:, :], sem=sem_fs)
                    p.wait("v", last)
        last_tr[0] = qgate[0]
        last_tr[1] = qgate[1]
        return

    if stage >= 2:
        ffn_stage2(din["fw1"], 32, D, cw1t, cw1r, scr["t1"], kmajor=False)
        ffn_stage2(din["fw2"], 8, DH, cw2t, cw2r, scr["t2"], kmajor=True)

    sem_xld = [p.new_sem(), p.new_sem()]
    xld_free = [None, None]
    for j in range(16):
        buf = xld.ap()[:, (j % 2) * 1024:(j % 2) * 1024 + 1024]
        if xld_free[j % 2] is not None:
            p.wait("y", xld_free[j % 2])
        hv = p.dma("y", buf, dx[j * 128:(j + 1) * 128, :], sem=sem_xld[j % 2])
        p.wait("t", hv)
        for half in range(2):
            tiles = [buf[:, (half * 4 + c) * 128:(half * 4 + c + 1) * 128] for c in range(4)]
            def evac(b, half=half, j=j):
                hv2 = None
                for q in range(4):
                    c = half * 4 + q
                    hv2 = p.op("s", lambda e, c=c, q=q, b=b:
                               e.copy(xfm.ap()[:, c * T + j * 128: c * T + (j + 1) * 128],
                                      b[:, q * 128:(q + 1) * 128]))
                return hv2
            transpose_group(tiles, 6 + half, evac, evac_engine="s")
        xld_free[j % 2] = p.sem["t"].cur()

    def ln_stats(t0, W, mb_t, sb_t, small_t, strip_t, bankbase=6, amax_dst=None):
        ntile = W // 128
        for j in range(ntile):
            tj = t0 + j * 128
            xt = strip_t.ap()[:, 0:1024]
            for half in range(2):
                tiles = [xc(half * 4 + c, tj, 128) for c in range(4)]
                def evac(b, half=half, xt=xt):
                    hv2 = None
                    for q in range(4):
                        c = half * 4 + q
                        hv2 = p.op("s", lambda e, c=c, q=q, b=b, xt=xt:
                                   e.copy(xt[:, c * 128:(c + 1) * 128],
                                          b[:, q * 128:(q + 1) * 128]))
                    return hv2
                transpose_group(tiles, bankbase + half, evac, evac_engine="s")
            hvs = p.sem["s"].cur()
            p.wait("v", hvs)
            sm = small_t.ap()
            p.op("v", lambda e, xt=xt, sm=sm: e.tensor_reduce(sm[:, 0:1], xt[:, 0:D],
                                                              AX.X, OP.add))
            p.op("v", lambda e, sm=sm: e.tensor_scalar(sm[:, 0:1], sm[:, 0:1], 1.0 / D,
                                                       None, OP.mult))
            if amax_dst is not None:
                p.op("v", lambda e, xt=xt, sm=sm: e.tensor_scalar(
                    xt[:, 0:D], xt[:, 0:D], sm[:, 0:1], None, OP.subtract))
                hvr = p.op("v", lambda e, xt=xt, j=j: e.tensor_reduce(
                    amax_dst[:, j:j + 1], xt[:, 0:D], AX.X, OP.max, apply_absolute_value=True))
                p.wait("s", hvr)
                hv = p.op("s", lambda e, xt=xt: e.activation(xt[:, 0:D], xt[:, 0:D], AF.Square))
                p.wait("v", hv)
                p.op("v", lambda e, xt=xt, sm=sm: e.tensor_reduce(sm[:, 1:2], xt[:, 0:D],
                                                                  AX.X, OP.add))
                p.op("v", lambda e, sm=sm: e.tensor_scalar(sm[:, 1:2], sm[:, 1:2], 1.0 / D,
                                                           None, OP.mult))
            else:
                p.wait("s", p.sem["v"].cur())
                hv = p.op("s", lambda e, xt=xt: e.activation(xt[:, 0:D], xt[:, 0:D], AF.Square))
                p.wait("v", hv)
                p.op("v", lambda e, xt=xt, sm=sm: e.tensor_reduce(sm[:, 1:2], xt[:, 0:D],
                                                                  AX.X, OP.add))
                p.op("v", lambda e, sm=sm: e.tensor_scalar(sm[:, 1:2], sm[:, 1:2], 1.0 / D,
                                                           None, OP.mult))
                p.op("v", lambda e, sm=sm: e.tensor_tensor(sm[:, 2:3], sm[:, 0:1],
                                                           sm[:, 0:1], OP.mult))
                p.op("v", lambda e, sm=sm: e.tensor_tensor(sm[:, 1:2], sm[:, 1:2],
                                                           sm[:, 2:3], OP.subtract))
            emit_rsqrt(sm[:, 1:2], sm[:, 1:2], LN_EPS, sm[:, 2:3], sm[:, 3:4], sm[:, 4:5])
            p.op("v", lambda e, sm=sm: e.tensor_scalar(sm[:, 128:256], cONES.ap()[:, :],
                                                       sm[:, 0:1], None, OP.mult))
            hv = p.op("v", lambda e, sm=sm: e.tensor_scalar(sm[:, 256:384], cONES.ap()[:, :],
                                                            sm[:, 1:2], None, OP.mult))
            p.wait("t", hv)
            bb = bk.acquire(p, bankbase)
            p.top(lambda e, sm=sm: e.transpose(bb[:, 0:128], sm[:, 128:256], cIDT.ap()[:, :]))
            hv = p.top(lambda e, sm=sm: e.transpose(bb[:, 128:256], sm[:, 256:384],
                                                    cIDT.ap()[:, :]))
            bk.produced(p, bankbase, hv)
            bk.consume(p, bankbase, "s")
            p.op("s", lambda e, j=j: e.copy(mb_t.ap()[:, j * 128:(j + 1) * 128], bb[:, 0:128]))
            hv = p.op("s", lambda e, j=j: e.copy(sb_t.ap()[:, j * 128:(j + 1) * 128],
                                                 bb[:, 128:256]))
            bk.consumed(p, bankbase, hv)

    sem_wsA = [p.new_sem(), p.new_sem()]

    def emit_attn_a():
        p.op("v", lambda e: e.memset(carry.ap()[:, :], 0.0))
        p.op("v", lambda e: e.memset(czA.ap()[:, :], 0.0))
        p.wait("t", p.sem["s"].cur())
        p.wait("t", p.sem["v"].cur())
        p.wait("y", p.sem["t"].cur())
        p.wait("y", p.sem["s"].cur())
        p.wait("g", p.sem["t"].cur())
        wfree = [None, None]
        for s in range(T // WA):
            t0 = s * WA
            ln_stats(t0, WA, mbA, sbA, smallA, stripA, bankbase=6)
            p.wait("v", p.sem["s"].cur())
            for c in range(NCH):
                ln = lnA.ap()[:, c * WA:(c + 1) * WA]
                p.op("v", lambda e, c=c, ln=ln, t0=t0: e.tensor_tensor(
                    ln, xc(c, t0, WA), mbA.ap()[:, :], OP.subtract))
                p.op("v", lambda e, ln=ln: e.tensor_tensor(ln, ln, sbA.ap()[:, :], OP.mult))
            hv_ln = p.sem["v"].cur()
            p.wait("t", hv_ln)
            if s == 0 and DBG == 1:
                dump_dbg([(0, lnA.ap()[:, 0:2048]), (1, mbA.ap()[:, 0:WA]),
                          (2, sbA.ap()[:, 0:WA]), (3, xfm.ap()[:, 0:2048])])
            for m in range(24):
                wb = wstrA.ap()[:, (m % 2) * 1024:(m % 2) * 1024 + 1024]
                if wfree[m % 2] is not None:
                    p.wait("y", wfree[m % 2])
                hvw = p.dma("y", wb.rearrange("a (k b) -> a k b", b=128),
                            scr["qkv"][m, :, :, :].rearrange("k a b -> a k b"),
                            sem=sem_wsA[m % 2])
                p.wait("t", hvw)
                bi = m % 2
                b = bk.acquire(p, bi)
                hv = None
                for k in range(8):
                    hv = p.top(lambda e, b=b, wb=wb, k=k:
                               e.matmul(b[:, 0:WA], wb[:, k * 128:(k + 1) * 128],
                                        lnA.ap()[:, k * WA:(k + 1) * WA],
                                        start=(k == 0), stop=(k == 7)))
                wfree[m % 2] = hv
                bk.produced(p, bi, hv)
                bk.consume(p, bi, "v")
                hv = p.op("v", lambda e, m=m, b=b:
                          e.tensor_scalar(qkvA.ap()[:, m * WA:(m + 1) * WA], b[:, 0:WA],
                                          bias_qkv.ap()[:, m:m + 1], None, OP.add))
                bk.consumed(p, bi, hv)
            for m in range(16):
                u = qkvA.ap()[:, m * WA:(m + 1) * WA]
                mn = smallA.ap()[:, 0:WA]
                mx = smallA.ap()[:, WA:2 * WA]
                p.op("v", lambda e, u=u, mn=mn: e.tensor_scalar(mn, u, 0.0, None, OP.min))
                hv = p.op("v", lambda e, u=u, mx=mx: e.tensor_scalar(mx, u, 0.0, None, OP.max))
                p.wait("s", hv)
                hv = p.op("s", lambda e, mn=mn: e.activation(mn, mn, AF.Exp,
                                                             scale=scale_a.ap()[:, 0:1]))
                p.wait("v", hv)
                hv = p.op("v", lambda e, u=u, mn=mn, mx=mx:
                          e.scalar_tensor_tensor(u, mx, scale_a.ap()[:, 0:1], mn,
                                                 OP.mult, OP.add))
                p.wait("s", hv)
            for c in range(NCH):
                kq = qkvA.ap()[:, (8 + c) * WA:(9 + c) * WA]
                vv = qkvA.ap()[:, (16 + c) * WA:(17 + c) * WA]
                p.op("v", lambda e, c=c, kq=kq, vv=vv:
                     e.tensor_tensor(kvA.ap()[:, c * WA:(c + 1) * WA], kq, vv, OP.mult))
            for c in range(NCH):
                kq = qkvA.ap()[:, (8 + c) * WA:(9 + c) * WA]
                kc = kcA.ap()[:, c * WA:(c + 1) * WA]
                p.op("v", lambda e, c=c: e.tensor_copy(czA.ap()[:, 0:1],
                                                       carry.ap()[:, 2 * c:2 * c + 1]))
                p.op("v", lambda e, kq=kq, kc=kc:
                     e.tensor_tensor_scan(kc, kq, czA.ap()[:, 0:WA], 0.0, OP.add, OP.add))
                p.op("v", lambda e, c=c, kc=kc:
                     e.tensor_copy(carry.ap()[:, 2 * c:2 * c + 1], kc[:, WA - 1:WA]))
                kv = kvA.ap()[:, c * WA:(c + 1) * WA]
                p.op("v", lambda e, c=c: e.tensor_copy(czA.ap()[:, 0:1],
                                                       carry.ap()[:, 2 * c + 1:2 * c + 2]))
                p.op("v", lambda e, kv=kv:
                     e.tensor_tensor_scan(kv, kv, czA.ap()[:, 0:WA], 0.0, OP.add, OP.add))
                p.op("v", lambda e, c=c, kv=kv:
                     e.tensor_copy(carry.ap()[:, 2 * c + 1:2 * c + 2], kv[:, WA - 1:WA]))
            b2 = bk.acquire(p, 2)
            prev_mm = None
            for c in range(NCH):
                qq = qkvA.ap()[:, c * WA:(c + 1) * WA]
                kc = kcA.ap()[:, c * WA:(c + 1) * WA]
                if prev_mm is not None:
                    p.wait("v", prev_mm)
                hv = p.op("v", lambda e, qq=qq, kc=kc:
                          e.tensor_tensor(smallA.ap()[:, 0:WA], qq, kc, OP.mult))
                p.wait("t", hv)
                prev_mm = hv = p.top(lambda e, c=c, b2=b2: e.matmul(
                    b2[:16, 0:WA], cS16.ap()[:, c * 16:(c + 1) * 16], smallA.ap()[:, 0:WA],
                    start=(c == 0), stop=(c == 7)))
            bk.produced(p, 2, hv)
            bk.consume(p, 2, "v")
            den = stripA.ap()[:16, 0:WA]
            p.op("v", lambda e, b2=b2: e.tensor_scalar(den, b2[:16, 0:WA], 1e-3, None, OP.max))
            hv = p.op("v", lambda e: e.reciprocal(den, den))
            bk.consumed(p, 2, hv)
            p.wait("t", hv)
            for c in range(NCH):
                b3 = bk.acquire(p, 3)
                hv = p.top(lambda e, c=c, b3=b3: e.matmul(
                    b3[:, 0:WA], cS16T.ap()[:16, c * 128:(c + 1) * 128], den,
                    start=True, stop=True))
                bk.produced(p, 3, hv)
                qq = qkvA.ap()[:, c * WA:(c + 1) * WA]
                kv = kvA.ap()[:, c * WA:(c + 1) * WA]
                yy = lnA.ap()[:, c * WA:(c + 1) * WA]
                p.op("v", lambda e, qq=qq, kv=kv, yy=yy: e.tensor_tensor(yy, qq, kv, OP.mult))
                bk.consume(p, 3, "v")
                hv = p.op("v", lambda e, yy=yy, b3=b3: e.tensor_tensor(yy, yy, b3[:, 0:WA],
                                                                       OP.mult))
                bk.consumed(p, 3, hv)
            hv_y = p.sem["v"].cur()
            p.wait("t", hv_y)
            if s == 0 and DBG == 2:
                dump_dbg([(0, lnA.ap()[:, 0:2048]), (1, qkvA.ap()[:, 0:2048]),
                          (2, qkvA.ap()[:, 8 * WA:8 * WA + 2048]),
                          (3, kcA.ap()[:, 0:2048]), (4, kvA.ap()[:, 0:2048]),
                          (5, stripA.ap()[:, 0:1024])])
            for m in range(8):
                wb = wstrA.ap()[:, (m % 2) * 1024:(m % 2) * 1024 + 1024]
                if wfree[m % 2] is not None:
                    p.wait("y", wfree[m % 2])
                hvw = p.dma("y", wb.rearrange("a (k b) -> a k b", b=128),
                            scr["proj"][m, :, :, :].rearrange("k a b -> a k b"),
                            sem=sem_wsA[m % 2])
                p.wait("t", hvw)
                bi = 4 + (m % 2)
                b = bk.acquire(p, bi)
                hv = None
                for k in range(8):
                    hv = p.top(lambda e, b=b, wb=wb, k=k:
                               e.matmul(b[:, 0:WA], wb[:, k * 128:(k + 1) * 128],
                                        lnA.ap()[:, k * WA:(k + 1) * WA],
                                        start=(k == 0), stop=(k == 7)))
                wfree[m % 2] = hv
                bk.produced(p, bi, hv)
                bk.consume(p, bi, "v")
                hv = p.op("v", lambda e, m=m, b=b: e.tensor_scalar(
                    smallA.ap()[:, 0:WA], b[:, 0:WA], bias_proj.ap()[:, m:m + 1], None, OP.add))
                bk.consumed(p, bi, hv)
                p.op("v", lambda e, m=m, t0=t0: e.tensor_tensor(
                    xc(m, t0, WA), xc(m, t0, WA), smallA.ap()[:, 0:WA], OP.add))

    if stage >= 1:
        emit_attn_a()

    def ln_stats_ffn(t0, sm):
        xt = fstats.ap()[:, 0:1024]
        for half in range(2):
            tiles = [xc(half * 4 + c, t0, 128) for c in range(4)]
            def evac(b, half=half):
                hv2 = None
                for q in range(4):
                    c = half * 4 + q
                    hv2 = p.op("s", lambda e, c=c, q=q, b=b:
                               e.copy(xt[:, c * 128:(c + 1) * 128], b[:, q * 128:(q + 1) * 128]))
                return hv2
            transpose_group(tiles, 6 + half, evac, evac_engine="s")
        p.wait("v", p.sem["s"].cur())
        p.op("v", lambda e: e.tensor_reduce(sm[:, 0:1], xt[:, 0:D], AX.X, OP.add))
        p.op("v", lambda e: e.tensor_scalar(sm[:, 0:1], sm[:, 0:1], 1.0 / D, None, OP.mult))
        p.op("v", lambda e: e.tensor_scalar(xt[:, 0:D], xt[:, 0:D], sm[:, 0:1], None,
                                            OP.subtract))
        hvr = p.op("v", lambda e: e.tensor_reduce(sm[:, 5:6], xt[:, 0:D], AX.X, OP.max,
                                            apply_absolute_value=True))
        p.wait("s", hvr)
        hv = p.op("s", lambda e: e.activation(xt[:, 0:D], xt[:, 0:D], AF.Square))
        p.wait("v", hv)
        p.op("v", lambda e: e.tensor_reduce(sm[:, 1:2], xt[:, 0:D], AX.X, OP.add))
        p.op("v", lambda e: e.tensor_scalar(sm[:, 1:2], sm[:, 1:2], 1.0 / D, None, OP.mult))
        emit_rsqrt(sm[:, 1:2], sm[:, 1:2], LN_EPS, sm[:, 2:3], sm[:, 3:4], sm[:, 4:5])

    bcast_gate = [None]

    def bcast_col(src_col_ap, dst_ap, sm, bank=6):
        if bcast_gate[0] is not None:
            p.wait("v", bcast_gate[0])
        hv = p.op("v", lambda e: e.tensor_scalar(sm[:, 128:256], cONES.ap()[:, :],
                                                 src_col_ap, None, OP.mult))
        p.wait("t", hv)
        bb = bk.acquire(p, bank)
        hv = p.top(lambda e, bb=bb: e.transpose(bb[:, 0:128], sm[:, 128:256], cIDT.ap()[:, :]))
        bcast_gate[0] = hv
        bk.produced(p, bank, hv)
        bk.consume(p, bank, "s")
        hv = p.op("s", lambda e, bb=bb, dst_ap=dst_ap: e.copy(dst_ap, bb[:, 0:128]))
        bk.consumed(p, bank, hv)
        return hv

    def emit_ffn():
        sem_t1 = p.new_sem()
        sem_t2 = p.new_sem()
        p.wait("y", p.sem["t"].cur())
        p.wait("y", p.sem["v"].cur())
        p.wait("y", p.sem["s"].cur())
        p.wait("g", p.sem["t"].cur())
        hv_t1 = p.dma("y", t1T.ap()[:, :].rearrange("a (m w) -> a m w", w=1024),
                      scr["t1"].rearrange("m a w -> a m w"), sem=sem_t1)
        p.wait("t", hv_t1)
        h3 = hbuf.ap()[:, :].rearrange("a (c w) -> a c w", w=131)
        m23 = m2b.ap()[:, :].rearrange("a (c w) -> a c w", w=256)
        p.op("v", lambda e: e.memset(hsave.ap()[:, :], 0.0))
        p.op("v", lambda e, h3=h3: e.memset(h3[:, :, 0:2], 0.0))
        p.op("v", lambda e, h3=h3: e.memset(h3[:, :, 130:131], 0.0))
        sm = fsm.ap()
        burst = {"tb0": -1, "bc": 0, "last_mm": None}

        def mm2_flush():
            if burst["bc"] == 0:
                return
            tb0 = burst["tb0"]
            sc = 1 if tb0 < 0 else 0
            ncol = burst["bc"] - sc
            out0 = tb0 + sc
            p.wait("t", p.sem["v"].cur())
            bs = [bk.acquire(p, i) for i in range(8)]
            hv = None
            t2gate = None
            for k in range(32):
                if t2gate is not None:
                    p.wait("g", t2gate)
                hvw = p.dma("g", t2buf.ap()[:, :].rearrange("a (m b) -> a m b", b=128),
                            scr["t2"][k, :, :, :].rearrange("m a b -> a m b"), sem=sem_t2)
                p.wait("t", hvw)
                for m in range(8):
                    b = bs[m]
                    hv = p.top(lambda e, b=b, m=m, k=k, ncol=ncol, sc=sc:
                               e.matmul(b[:, 0:ncol],
                                        t2buf.ap()[:, m * 128:(m + 1) * 128],
                                        m2b.ap()[:, k * 256 + sc:k * 256 + sc + ncol],
                                        start=(k == 0), stop=(k == 31)))
                t2gate = hv
            for i in range(8):
                bk.produced(p, i, hv)
            for m in range(8):
                b = bs[m]
                bk.consume(p, m, "v")
                p.op("v", lambda e, b=b, ncol=ncol, sc=sc:
                     e.tensor_tensor(fval.ap()[:, 0:ncol], b[:, 0:ncol],
                                     fsc.ap()[:, 512 + sc:512 + sc + ncol], OP.mult))
                hv2 = p.op("v", lambda e, m=m, out0=out0, ncol=ncol:
                           e.tensor_tensor(xc(m, out0, ncol), xc(m, out0, ncol),
                                           fval.ap()[:, 0:ncol], OP.add))
                bk.consumed(p, m, hv2)
            burst["last_mm"] = hv
            burst["bc"] = 0
            burst["tb0"] = out0 + ncol

        for s in range(T // WF):
            t0 = s * WF
            last_strip = (s == T // WF - 1)
            n = 129 if last_strip else 128
            if s > 0:
                p.op("v", lambda e, h3=h3: e.tensor_copy(
                    h3[:, :, 0:1], hsave.ap()[:, :].rearrange("a (c o) -> a c o", o=1)))
                p.op("v", lambda e, h3=h3: e.tensor_copy(h3[:, :, 1:2], h3[:, :, 129:130]))
            ln_stats_ffn(t0, sm)
            p.op("v", lambda e: e.tensor_tensor(sm[:, 6:7], sm[:, 5:6], sm[:, 1:2], OP.mult))
            p.op("v", lambda e: e.tensor_scalar(sm[:, 6:7], sm[:, 6:7], 1e-5, None, OP.max))
            p.op("v", lambda e: e.reciprocal(sm[:, 7:8], sm[:, 6:7]))
            p.op("v", lambda e: e.tensor_scalar(sm[:, 7:8], sm[:, 7:8], 127.0, None, OP.mult))
            p.op("v", lambda e: e.tensor_tensor(sm[:, 8:9], sm[:, 1:2], sm[:, 7:8], OP.mult))
            p.op("v", lambda e: e.tensor_scalar(sm[:, 9:10], sm[:, 6:7], 1.0 / 127.0,
                                                None, OP.mult))
            p.op("v", lambda e: e.tensor_tensor(sm[:, 9:10], sm[:, 9:10], cw1t.ap()[:, 0:1],
                                                OP.mult))
            bcast_col(sm[:, 0:1], fsc.ap()[:, 0:128], sm)
            bcast_col(sm[:, 8:9], fsc.ap()[:, 128:256], sm)
            bcast_col(sm[:, 9:10], fsc.ap()[:, 256:384], sm)
            p.wait("v", p.sem["s"].cur())
            if burst["last_mm"] is not None:
                p.wait("v", burst["last_mm"])
                p.wait("s", burst["last_mm"])
            if burst.get("last_mm1") is not None:
                p.wait("v", burst["last_mm1"])
                p.wait("s", burst["last_mm1"])
            for c in range(NCH):
                dst = fval.ap()[:, (c % 4) * 128:(c % 4) * 128 + 128]
                p.op("v", lambda e, c=c, dst=dst, t0=t0: e.tensor_tensor(
                    dst, xc(c, t0, WF), fsc.ap()[:, 0:128], OP.subtract))
                p.op("v", lambda e, dst=dst: e.tensor_tensor(
                    dst, dst, fsc.ap()[:, 128:256], OP.mult))
                if c % 4 == 3:
                    emit_round(fval.ap()[:, :], fint.ap()[:, :], ff.ap()[:, :],
                               fd.ap()[:, :],
                               m1b.ap()[:, (c - 3) * 128:(c + 1) * 128])
            hv_m1 = p.sem["v"].cur()
            p.wait("t", hv_m1)
            for m in range(32):
                bi = 4 + (m % 2)
                b = bk.acquire(p, bi)
                hv = None
                for k in range(8):
                    hv = p.top(lambda e, b=b, m=m, k=k: e.matmul(
                        b[:, 0:WF], t1T.ap()[:, m * 1024 + k * 128: m * 1024 + (k + 1) * 128],
                        m1b.ap()[:, k * WF:(k + 1) * WF], start=(k == 0), stop=(k == 7)))
                burst["last_mm1"] = hv
                bk.produced(p, bi, hv)
                bk.consume(p, bi, "v")
                hv = p.op("v", lambda e, b=b, m=m, h3=h3: e.tensor_tensor(
                    h3[:, m, 2:130], b[:, 0:WF], fsc.ap()[:, 256:384], OP.mult))
                bk.consumed(p, bi, hv)
            p.op("v", lambda e, h3=h3: e.tensor_copy(
                hsave.ap()[:, :].rearrange("a (c o) -> a c o", o=1), h3[:, :, 128:129]))
            hcv = hc8.ap()[:, :].rearrange("a (c w) -> a c w", w=132)
            for grp in range(4):
                for q in range(8):
                    m = grp * 8 + q
                    hvt1 = p.op("v", lambda e, m=m, q=q, n=n, h3=h3, hcv=hcv: e.tensor_scalar(
                        hcv[:, q, 0:n], h3[:, m, 1:1 + n], fcw.ap()[:, m * 3 + 1:m * 3 + 2],
                        fcb.ap()[:, m:m + 1], OP.mult, OP.add))
                    p.op("v", lambda e, m=m, q=q, n=n, h3=h3, hcv=hcv:
                         e.scalar_tensor_tensor(
                        hcv[:, q, 0:n], h3[:, m, 0:n], fcw.ap()[:, m * 3:m * 3 + 1],
                        hcv[:, q, 0:n], OP.mult, OP.add))
                    p.op("v", lambda e, m=m, q=q, n=n, h3=h3, hcv=hcv:
                         e.scalar_tensor_tensor(
                        hcv[:, q, 0:n], h3[:, m, 2:2 + n], fcw.ap()[:, m * 3 + 2:m * 3 + 3],
                        hcv[:, q, 0:n], OP.mult, OP.add))
                hvv = p.sem["v"].cur()
                p.wait("s", hvv)
                hv = p.op("s", lambda e, grp=grp, n=n, h3=h3, hcv=hcv: e.activation(
                    h3[:, grp * 8:(grp + 1) * 8, 1:1 + n], hcv[:, 0:8, 0:n], AF.Silu))
                p.wait("v", hv)
                p.wait("g", hv)
            amx = hcv
            hperm = hbuf.ap()[:, :].rearrange("a (c w) -> a w c", w=131)
            p.op("v", lambda e, hperm=hperm, amx=amx, n=n: e.tensor_reduce(
                amx[:, 0, 0:n], hperm[:, 1:1 + n, :], AX.X, OP.max,
                apply_absolute_value=True))
            hv = p.sem["v"].cur()
            p.wait("t", hv)
            bb = bk.acquire(p, 6)
            hv = p.top(lambda e, bb=bb, amx=amx: e.transpose(bb[:, 0:128], amx[:, 0, 0:128],
                                                             cIDT.ap()[:, :]))
            bk.produced(p, 6, hv)
            bk.consume(p, 6, "v")
            hv2 = p.op("v", lambda e, bb=bb: e.tensor_reduce(
                sm[:, 20:21], bb[:, 0:128], AX.X, OP.max, apply_absolute_value=True))
            bk.consumed(p, 6, hv2)
            if last_strip:
                bb = bk.acquire(p, 6)
                hv = p.top(lambda e, bb=bb, amx=amx: e.transpose(
                    bb[:1, 0:128], amx[:, 0:1, 128:129], cIDT.ap()[:, :]))
                bk.produced(p, 6, hv)
                bk.consume(p, 6, "v")
                hv2 = p.op("v", lambda e, bb=bb: e.tensor_reduce(
                    sm[:1, 24:25], bb[:1, 0:128], AX.X, OP.max, apply_absolute_value=True))
                bk.consumed(p, 6, hv2)
                p.op("v", lambda e: e.tensor_scalar(sm[:1, 24:25], sm[:1, 24:25], 1e-5,
                                                    None, OP.max))
                p.op("v", lambda e: e.reciprocal(sm[:1, 25:26], sm[:1, 24:25]))
                p.op("v", lambda e: e.tensor_scalar(sm[:1, 25:26], sm[:1, 25:26], 127.0,
                                                    None, OP.mult))
                p.op("v", lambda e: e.tensor_scalar(sm[:1, 26:27], sm[:1, 24:25],
                                                    1.0 / 127.0, None, OP.mult))
                p.op("v", lambda e: e.tensor_tensor(sm[:1, 26:27], sm[:1, 26:27],
                                                    cw2t.ap()[:1, 0:1], OP.mult))
            p.op("v", lambda e: e.tensor_scalar(sm[:, 20:21], sm[:, 20:21], 1e-5, None, OP.max))
            p.op("v", lambda e: e.reciprocal(sm[:, 21:22], sm[:, 20:21]))
            p.op("v", lambda e: e.tensor_scalar(sm[:, 21:22], sm[:, 21:22], 127.0, None,
                                                OP.mult))
            p.op("v", lambda e: e.tensor_scalar(sm[:, 22:23], sm[:, 20:21], 1.0 / 127.0,
                                                None, OP.mult))
            p.op("v", lambda e: e.tensor_tensor(sm[:, 22:23], sm[:, 22:23], cw2t.ap()[:, 0:1],
                                                OP.mult))
            bcast_col(sm[:, 21:22], fsc.ap()[:, 384:512], sm)
            bc = burst["bc"]
            bcast_col(sm[:, 22:23], fsc.ap()[:, 512 + bc:512 + bc + 128], sm)
            if last_strip:
                p.wait("t", p.sem["v"].cur())
                bb = bk.acquire(p, 6)
                p.top(lambda e, bb=bb: e.matmul(bb[:, 0:1], cONES.ap()[:1, :], sm[:1, 25:26],
                                                start=True, stop=True))
                hv = p.top(lambda e, bb=bb: e.matmul(bb[:, 1:2], cONES.ap()[:1, :],
                                                     sm[:1, 26:27], start=True, stop=True,
                                                     skip_group_check=True))
                bk.produced(p, 6, hv)
                bk.consume(p, 6, "s")
                p.op("s", lambda e, bb=bb: e.copy(fsc.ap()[:, 1000:1001], bb[:, 0:1]))
                hv = p.op("s", lambda e, bb=bb, bc=bc: e.copy(
                    fsc.ap()[:, 512 + bc + 128:512 + bc + 129], bb[:, 1:2]))
                bk.consumed(p, 6, hv)
            p.wait("v", p.sem["s"].cur())
            for grp in range(8):
                for q in range(4):
                    c = grp * 4 + q
                    p.op("v", lambda e, c=c, q=q, h3=h3: e.tensor_tensor(
                        fval.ap()[:, q * 128:(q + 1) * 128], h3[:, c, 1:129],
                        fsc.ap()[:, 384:512], OP.mult))
                emit_round(fval.ap()[:, :], fint.ap()[:, :], ff.ap()[:, :], fd.ap()[:, :],
                           m23[:, grp * 4:(grp + 1) * 4, bc:bc + 128])
            if last_strip:
                for grp in range(8):
                    for q in range(4):
                        c = grp * 4 + q
                        p.op("v", lambda e, c=c, q=q, grp=grp, h3=h3: e.tensor_tensor(
                            fval.ap()[:, grp * 4 + q:grp * 4 + q + 1], h3[:, c, 129:130],
                            fsc.ap()[:, 1000:1001], OP.mult))
                emit_round(fval.ap()[:, 0:32], fint.ap()[:, 0:32], ff.ap()[:, 0:32],
                           fd.ap()[:, 0:32],
                           m23[:, 0:32, bc + 128:bc + 129].rearrange("a c o -> a (c o)"))
            if s == 15 and DBG == 3:
                dump_dbg([(0, hbuf.ap()[:, 0:2048]), (1, fsc.ap()[:, 0:512])])
            burst["bc"] = bc + n
            if (s % 2 == 1 and s < 14) or s >= 14:
                mm2_flush()

    def emit_attn_b():
        sem_wB = [p.new_sem(), p.new_sem()]
        p.wait("y", p.sem["t"].cur())
        p.wait("y", p.sem["v"].cur())
        p.wait("y", p.sem["s"].cur())
        p.wait("g", p.sem["t"].cur())
        xbar(0)
        p.op("v", lambda e: e.memset(dW.ap()[:, :], 0.0))
        p.op("v", lambda e: e.memset(dWavg.ap()[:, :], 0.0))
        for raw in (bk0, bq0, bv0):
            p.op("v", lambda e, raw=raw: e.memset(
                raw.ap()[:, :].rearrange("a (c w) -> a c w", w=264)[:, :, 0:3], 0.0))
        wfree = [None, None]

        def stream_mm(dscr_w, rhs_fn, out_fn, nbank0=0):
            for m in range(8):
                wb = wstrB.ap()[:, (m % 2) * 1024:(m % 2) * 1024 + 1024]
                if wfree[m % 2] is not None:
                    p.wait("y", wfree[m % 2])
                hvw = p.dma("y", wb.rearrange("a (k b) -> a k b", b=128),
                            dscr_w[m, :, :, :].rearrange("k a b -> a k b"),
                            sem=sem_wB[m % 2])
                p.wait("t", hvw)
                bi = nbank0 + (m % 2)
                b = bk.acquire(p, bi)
                hv = None
                for k in range(8):
                    hv = p.top(lambda e, b=b, wb=wb, k=k, m=m:
                               e.matmul(b[:, 0:WB], wb[:, k * 128:(k + 1) * 128],
                                        rhs_fn(k), start=(k == 0), stop=(k == 7)))
                wfree[m % 2] = hv
                bk.produced(p, bi, hv)
                bk.consume(p, bi, "v")
                hv2 = out_fn(m, b)
                bk.consumed(p, bi, hv2)

        def conv4(raw_t, dst_ap_fn, cw_t):
            r3 = raw_t.ap()[:, :].rearrange("a (c w) -> a c w", w=264)
            for c in range(NCH):
                dst = dst_ap_fn(c)
                p.op("v", lambda e, c=c, dst=dst, r3=r3, cw_t=cw_t: e.tensor_scalar(
                    dst, r3[:, c, 3:259], cw_t.ap()[:, c * 4 + 3:c * 4 + 4], None, OP.mult))
                for j, tap in ((2, 2), (1, 1), (0, 0)):
                    p.op("v", lambda e, c=c, dst=dst, r3=r3, cw_t=cw_t, j=j:
                         e.scalar_tensor_tensor(
                             dst, r3[:, c, j:j + 256], cw_t.ap()[:, c * 4 + j:c * 4 + j + 1],
                             dst, OP.mult, OP.add))

        def headnorm(src_t, strip_off, is_rms):
            b2 = bk.acquire(p, 2)
            prev_mm = None
            for c in range(NCH):
                sl = src_t.ap()[:, c * WB:(c + 1) * WB]
                if prev_mm is not None:
                    p.wait("v", prev_mm)
                hv = p.op("v", lambda e, sl=sl: e.tensor_tensor(
                    smallB.ap()[:, 0:WB], sl, sl, OP.mult))
                p.wait("t", hv)
                prev_mm = hv = p.top(lambda e, c=c, b2=b2: e.matmul(
                    b2[:16, 0:WB], cS16.ap()[:, c * 16:(c + 1) * 16], smallB.ap()[:, 0:WB],
                    start=(c == 0), stop=(c == 7)))
            bk.produced(p, 2, hv)
            bk.consume(p, 2, "v")
            st_ = stripB.ap()[:16, strip_off:strip_off + WB]
            sc1 = smallB.ap()[:16, 256:257 + WB]
            if is_rms:
                hv = p.op("v", lambda e, b2=b2, st_=st_: e.tensor_scalar(
                    st_, b2[:16, 0:WB], 1.0 / HD, None, OP.mult))
                bk.consumed(p, 2, hv)
                emit_rsqrt(st_, st_, 1e-6, sc1[:, 0:WB], smallB.ap()[:16, 512:512 + WB],
                           smallB.ap()[:16, 768:768 + WB])
            else:
                hv = p.op("v", lambda e, b2=b2, st_=st_: e.tensor_copy(st_, b2[:16, 0:WB]))
                bk.consumed(p, 2, hv)
                emit_sqrt(st_, st_, sc1[:, 0:WB], smallB.ap()[:16, 512:512 + WB],
                          smallB.ap()[:16, 768:768 + WB])
                p.op("v", lambda e, st_=st_: e.tensor_scalar(st_, st_, 1e-6, None, OP.add))
                p.op("v", lambda e, st_=st_: e.reciprocal(st_, st_))
            hvr = p.sem["v"].cur()
            p.wait("t", hvr)
            for c in range(NCH):
                b3 = bk.acquire(p, 3)
                hv = p.top(lambda e, c=c, b3=b3, st_=st_: e.matmul(
                    b3[:, 0:WB], cS16T.ap()[:16, c * 128:(c + 1) * 128], st_,
                    start=True, stop=True))
                bk.produced(p, 3, hv)
                bk.consume(p, 3, "v")
                sl = src_t.ap()[:, c * WB:(c + 1) * WB]
                hv2 = p.op("v", lambda e, sl=sl, b3=b3: e.tensor_tensor(
                    sl, sl, b3[:, 0:WB], OP.mult))
                if is_rms:
                    hv2 = p.op("v", lambda e, sl=sl: e.tensor_scalar(
                        sl, sl, rmsw.ap()[:, 0:1], None, OP.mult))
                bk.consumed(p, 3, hv2)

        def delta_chunk(gi, ii, g):
            kf = bkh.ap()[:, g * WB + ii * 64: g * WB + ii * 64 + 64]
            qf = bfm.ap()[:, g * WB + ii * 64: g * WB + ii * 64 + 64]
            v3f = nbfm.ap()[:, g * WB + ii * 64: g * WB + ii * 64 + 64]
            bf = betab.ap()[:, g * WB + ii * 64: g * WB + ii * 64 + 64]
            nbf = bv0.ap()[:, :].rearrange("a (c w) -> a c w", w=264)[:, g,
                                                                     3 + ii * 64:3 + ii * 64 + 64]
            Wsl = dW.ap()[:, g * 64:(g + 1) * 64]
            Wasl = dWavg.ap()[:, g * 64:(g + 1) * 64]
            osl = lnB.ap()[:, g * WB + ii * 64: g * WB + ii * 64 + 64]
            tmt = cTMT.ap()[:64, gi * 64:(gi + 1) * 64]
            ab_ = cAB.ap()[:, gi * 64:(gi + 1) * 64]
            wcol = cWCOL.ap()[:64, gi:gi + 1]
            c1c = cC1.ap()[:, gi:gi + 1]
            c2c = cC2.ap()[:, gi:gi + 1]
            ID = cIDT.ap()

            def pairlow(bia, bib, lhsT_t, rhs_t):
                ba = bk.acquire(p, bia)
                hva = p.top(lambda e, ba=ba, lhsT_t=lhsT_t, rhs_t=rhs_t: e.matmul(
                    ba[0:64, 0:64], lhsT_t[0:64, :], rhs_t[0:64, :],
                    start=True, stop=True, tile_position=(0, 0)))
                bk.produced(p, bia, hva)
                bb2 = bk.acquire(p, bib)
                hvb = p.top(lambda e, bb2=bb2, lhsT_t=lhsT_t, rhs_t=rhs_t: e.matmul(
                    bb2[0:64, 0:64], lhsT_t[64:128, :], rhs_t[64:128, :],
                    start=True, stop=True, tile_position=(64, 0)))
                bk.produced(p, bib, hvb)
                return ba, bb2

            ba, bb2 = pairlow(0, 1, bf, kf)
            bk.consume(p, 0, "v")
            hv = p.op("v", lambda e, ba=ba: e.tensor_tensor(
                pwh("N1", 0), ba[0:64, 0:64], cNMASK.ap()[:64, 0:64], OP.mult))
            bk.consumed(p, 0, hv)
            bk.consume(p, 1, "v")
            hv = p.op("v", lambda e, bb2=bb2: e.tensor_tensor(
                pwh("N1", 1), bb2[0:64, 0:64], cNMASK.ap()[:64, 64:128], OP.mult))
            bk.consumed(p, 1, hv)
            ba, bb2 = pairlow(2, 3, kf, bf)
            bk.consume(p, 2, "v")
            hv = p.op("v", lambda e, ba=ba: e.tensor_tensor(
                pwh("NS1", 0), ba[0:64, 0:64], cNMASKT.ap()[:64, 0:64], OP.mult))
            bk.consumed(p, 2, hv)
            bk.consume(p, 3, "v")
            hv = p.op("v", lambda e, bb2=bb2: e.tensor_tensor(
                pwh("NS1", 1), bb2[0:64, 0:64], cNMASKT.ap()[:64, 64:128], OP.mult))
            bk.consumed(p, 3, hv)
            if DCUT is not None and DCUT <= 1: return
            b5 = bk.acquire(p, 5)
            hv = p.top(lambda e, b5=b5, kf=kf: e.transpose(b5[0:64, 0:128], kf, ID[:, :]))
            bk.produced(p, 5, hv)
            bk.consume(p, 5, "v")
            hv = p.op("v", lambda e, b5=b5: e.tensor_copy(dKts.ap()[:64, :], b5[0:64, 0:128]))
            bk.consumed(p, 5, hv)
            if DCUT is not None and DCUT <= 2: return
            seq = [("N2", "NS1", "N1"), ("NS2", "N1", "NS1"), ("N4", "NS2", "N2"),
                   ("NS4", "N2", "NS2"), ("N8", "NS4", "N4"), ("NS8", "N4", "NS4"),
                   ("N16", "NS8", "N8"), ("NS16", "N8", "NS8"), ("N32", "NS16", "N16")]
            for idx, (dst, la, lb) in enumerate(seq):
                bi = 2 + (idx % 2)
                b = bk.acquire(p, bi)
                wv = p.sem["v"].cur()
                p.wait("t", wv)
                for h in range(2):
                    hv = p.top(lambda e, b=b, la=la, lb=lb, h=h: e.matmul(
                        b[0:64, h * 64:h * 64 + 64], pw(la)[:, h * 64:h * 64 + 64],
                        pw(lb)[:, h * 64:h * 64 + 64],
                        start=(h == 0), stop=True, tile_position=(0, 0),
                        skip_group_check=(h == 1)))
                bk.produced(p, bi, hv)
                bk.consume(p, bi, "v")
                hv = p.op("v", lambda e, b=b, dst=dst: e.tensor_copy(pw(dst), b[0:64, 0:128]))
                bk.consumed(p, bi, hv)
            if DCUT is not None and DCUT <= 3: return
            p.op("v", lambda e: e.tensor_tensor(dP.ap()[:64, :], cI64S.ap()[:64, :],
                                                pw("NS1"), OP.add))
            for k in (1, 2, 3, 4, 5):
                nm = f"N{2 ** k}"
                p.op("v", lambda e, nm=nm: e.tensor_tensor(pw("PI"), cI64S.ap()[:64, :],
                                                           pw(nm), OP.add))
                b4 = bk.acquire(p, 4)
                wv = p.sem["v"].cur()
                p.wait("t", wv)
                for h in range(2):
                    hv = p.top(lambda e, b4=b4, h=h: e.matmul(
                        b4[0:64, h * 64:h * 64 + 64], pwh("PI", h),
                        dP.ap()[:64, h * 64:h * 64 + 64],
                        start=True, stop=True, tile_position=(0, 0),
                        skip_group_check=(h == 1)))
                bk.produced(p, 4, hv)
                bk.consume(p, 4, "v")
                hv = p.op("v", lambda e, b4=b4: e.tensor_copy(dP.ap()[:64, :], b4[0:64, 0:128]))
                bk.consumed(p, 4, hv)
            if DCUT is not None and DCUT <= 4: return
            b0 = bk.acquire(p, 0)
            hv = p.top(lambda e, b0=b0, v3f=v3f: e.transpose(b0[0:64, 0:128], v3f, ID[:, :]))
            bk.produced(p, 0, hv)
            wv = p.sem["v"].cur()
            p.wait("t", wv)
            b1 = bk.acquire(p, 1)
            hv = p.top(lambda e, b1=b1, nbf=nbf, Wsl=Wsl: e.matmul(
                b1[0:64, 0:64], nbf[0:64, :], Wsl[0:64, :],
                start=True, stop=True, tile_position=(0, 0)))
            bk.produced(p, 1, hv)
            b5g = bk.acquire(p, 5)
            hv = p.top(lambda e, b5g=b5g, nbf=nbf, Wsl=Wsl: e.matmul(
                b5g[0:64, 0:64], nbf[64:128, :], Wsl[64:128, :],
                start=True, stop=True, tile_position=(64, 0)))
            bk.produced(p, 5, hv)
            bk.consume(p, 0, "v")
            hv = p.op("v", lambda e, b0=b0: e.tensor_copy(dG.ap()[:64, :], b0[0:64, 0:128]))
            bk.consumed(p, 0, hv)
            bk.consume(p, 1, "v")
            hv = p.op("v", lambda e, b1=b1: e.tensor_tensor(
                dG.ap()[:64, 0:64], dG.ap()[:64, 0:64], b1[0:64, 0:64], OP.add))
            bk.consumed(p, 1, hv)
            bk.consume(p, 5, "v")
            hv = p.op("v", lambda e, b5g=b5g: e.tensor_tensor(
                dG.ap()[:64, 64:128], dG.ap()[:64, 64:128], b5g[0:64, 0:64], OP.add))
            bk.consumed(p, 5, hv)
            if DCUT is not None and DCUT <= 5: return
            b1 = bk.acquire(p, 1)
            wv = p.sem["v"].cur()
            p.wait("t", wv)
            for h in range(2):
                hv = p.top(lambda e, b1=b1, h=h: e.matmul(
                    b1[0:64, h * 64:h * 64 + 64], dP.ap()[:64, h * 64:h * 64 + 64],
                    dG.ap()[:64, h * 64:h * 64 + 64],
                    start=(h == 0), stop=True, tile_position=(0, 0),
                    skip_group_check=(h == 1)))
            bk.produced(p, 1, hv)
            bk.consume(p, 1, "v")
            p.op("v", lambda e, b1=b1: e.tensor_copy(dU.ap()[:64, :], b1[0:64, 0:128]))
            hv = p.op("v", lambda e, b1=b1, wcol=wcol: e.tensor_scalar(
                dUw.ap()[:64, :], b1[0:64, 0:128], wcol, None, OP.mult))
            bk.consumed(p, 1, hv)
            if DCUT is not None and DCUT <= 6: return
            b2 = bk.acquire(p, 2)
            p.top(lambda e, b2=b2, Wasl=Wasl, qf=qf: e.matmul(
                b2[0:64, 0:64], Wasl[0:64, :], qf[0:64, :],
                start=True, stop=True, tile_position=(0, 0)))
            hv = p.top(lambda e, b2=b2, Wasl=Wasl, qf=qf: e.matmul(
                b2[64:128, 0:64], Wasl[64:128, :], qf[64:128, :],
                start=True, stop=True, tile_position=(64, 64), skip_group_check=True))
            bk.produced(p, 2, hv)
            b3 = bk.acquire(p, 3)
            p.top(lambda e, b3=b3, Wsl=Wsl, qf=qf: e.matmul(
                b3[0:64, 0:64], Wsl[0:64, :], qf[0:64, :],
                start=True, stop=True, tile_position=(0, 0)))
            hv = p.top(lambda e, b3=b3, Wsl=Wsl, qf=qf: e.matmul(
                b3[64:128, 0:64], Wsl[64:128, :], qf[64:128, :],
                start=True, stop=True, tile_position=(64, 64), skip_group_check=True))
            bk.produced(p, 3, hv)
            b4 = bk.acquire(p, 4)
            hv = p.top(lambda e, b4=b4, kf=kf, qf=qf: e.matmul(
                b4[0:64, 0:64], kf[0:64, :], qf[0:64, :],
                start=True, stop=True, tile_position=(0, 0)))
            bk.produced(p, 4, hv)
            b0 = bk.acquire(p, 0)
            hv = p.top(lambda e, b0=b0, kf=kf, qf=qf: e.matmul(
                b0[0:64, 0:64], kf[64:128, :], qf[64:128, :],
                start=True, stop=True, tile_position=(64, 0)))
            bk.produced(p, 0, hv)
            bk.consume(p, 4, "v")
            hv = p.op("v", lambda e, b4=b4, tmt=tmt: e.tensor_tensor(
                dT1.ap()[:64, 0:64], b4[0:64, 0:64], tmt, OP.mult))
            bk.consumed(p, 4, hv)
            bk.consume(p, 0, "v")
            hv = p.op("v", lambda e, b0=b0, tmt=tmt: e.tensor_tensor(
                dT1.ap()[:64, 64:128], b0[0:64, 0:64], tmt, OP.mult))
            bk.consumed(p, 0, hv)
            if DCUT is not None and DCUT <= 7: return
            b5o = bk.acquire(p, 5)
            wv = p.sem["v"].cur()
            p.wait("t", wv)
            p.top(lambda e, b5o=b5o: e.matmul(
                b5o[0:64, 0:64], dU.ap()[:64, 0:64], dT1.ap()[:64, 0:64],
                start=True, stop=True, tile_position=(0, 0)))
            hv = p.top(lambda e, b5o=b5o: e.matmul(
                b5o[64:128, 0:64], dU.ap()[:64, 64:128], dT1.ap()[:64, 64:128],
                start=True, stop=True, tile_position=(0, 64), skip_group_check=True))
            bk.produced(p, 5, hv)
            bk.consume(p, 3, "v")
            p.op("v", lambda e, b3=b3: e.tensor_copy(dT2.ap()[:, :], b3[:, 0:64]))
            bk.consume(p, 2, "v")
            hv = p.op("v", lambda e, b2=b2: e.scalar_tensor_tensor(
                dCmb.ap()[:, :], dT2.ap()[:, :], -1.0, b2[:, 0:64], OP.mult, OP.add))
            bk.consumed(p, 2, hv)
            p.op("v", lambda e, ab_=ab_: e.tensor_tensor(dCmb.ap()[:, :], dCmb.ap()[:, :],
                                                         ab_, OP.mult))
            hv = p.op("v", lambda e, b3=b3: e.tensor_tensor(
                dCmb.ap()[:, :], dCmb.ap()[:, :], b3[:, 0:64], OP.add))
            bk.consumed(p, 3, hv)
            bk.consume(p, 5, "v")
            hv = p.op("v", lambda e, b5o=b5o, osl=osl: e.tensor_tensor(
                osl, dCmb.ap()[:, :], b5o[:, 0:64], OP.add))
            bk.consumed(p, 5, hv)
            if gi == 0 and g == 0:
                xbar(5)
            if gi == 0 and g == 0 and BAR == 6:
                p.wait("g", p.sem["v"].cur())
                p.wait("g", p.sem["s"].cur())
                semz = p.new_sem("bar6")
                hvz = p.dma("g", ddbg[0][:1, 0:1], dKts.ap()[:1, 0:1], slow=True, sem=semz)
                p.wait("v", hvz)
                p.wait("s", hvz)
                p.wait("t", hvz)
            if gi == 0 and g == 0 and BAR == 7:
                dump_dbg([(0, pw("N1")), (1, dP.ap()[:64, :]), (2, dG.ap()[:64, :]),
                          (3, dU.ap()[:64, :]), (4, osl), (5, dKts.ap()[:64, :])])
            if DBG == 4 and gi == DBGCH and g == 0:
                dump_dbg([(0, pw("N1")), (1, dP.ap()[:64, :]), (2, dG.ap()[:64, :]),
                          (3, dU.ap()[:64, :]), (4, osl), (5, dKts.ap()[:64, :])])
            if DBG == 6 and gi == DBGCH and g == 0:
                dump_dbg([(0, Wsl), (1, Wasl), (2, dU.ap()[:64, :]),
                          (3, dUw.ap()[:64, :]), (4, v3f), (5, bf)])
            if DCUT is not None and DCUT <= 8: return
            p.op("v", lambda e, Wsl=Wsl, c2c=c2c: e.tensor_scalar(
                dWmix.ap()[:, :], Wsl, c2c, None, OP.mult))
            p.op("v", lambda e, Wasl=Wasl, c1c=c1c: e.scalar_tensor_tensor(
                dWmix.ap()[:, :], Wasl, c1c, dWmix.ap()[:, :], OP.mult, OP.add))
            b1w = bk.acquire(p, 1)
            wv = p.sem["v"].cur()
            p.wait("t", wv)
            p.top(lambda e, b1w=b1w: e.matmul(
                b1w[0:64, 0:64], dKts.ap()[:64, 0:64], dU.ap()[:64, 0:64],
                start=True, stop=True, tile_position=(0, 0)))
            hv = p.top(lambda e, b1w=b1w: e.matmul(
                b1w[64:128, 0:64], dKts.ap()[:64, 64:128], dU.ap()[:64, 64:128],
                start=True, stop=True, tile_position=(0, 64), skip_group_check=True))
            bk.produced(p, 1, hv)
            b4w = bk.acquire(p, 4)
            p.top(lambda e, b4w=b4w: e.matmul(
                b4w[0:64, 0:64], dKts.ap()[:64, 0:64], dUw.ap()[:64, 0:64],
                start=True, stop=True, tile_position=(0, 0)))
            hv = p.top(lambda e, b4w=b4w: e.matmul(
                b4w[64:128, 0:64], dKts.ap()[:64, 64:128], dUw.ap()[:64, 64:128],
                start=True, stop=True, tile_position=(0, 64), skip_group_check=True))
            bk.produced(p, 4, hv)
            bk.consume(p, 1, "v")
            hv = p.op("v", lambda e, b1w=b1w, Wsl=Wsl: e.tensor_tensor(
                Wsl, Wsl, b1w[:, 0:64], OP.add))
            bk.consumed(p, 1, hv)
            bk.consume(p, 4, "v")
            hv = p.op("v", lambda e, b4w=b4w, Wasl=Wasl: e.tensor_tensor(
                Wasl, dWmix.ap()[:, :], b4w[:, 0:64], OP.add))
            bk.consumed(p, 4, hv)

        nstrip = T // WB if BSTRIPS is None else BSTRIPS
        for s in range(nstrip):
            t0 = s * WB
            p.wait("v", p.sem["t"].cur())
            p.wait("s", p.sem["t"].cur())
            ln_stats(t0, WB, mbB, sbB, smallB, stripB, bankbase=6)
            p.wait("v", p.sem["s"].cur())
            for c in range(NCH):
                ln = lnB.ap()[:, c * WB:(c + 1) * WB]
                p.op("v", lambda e, c=c, ln=ln, t0=t0: e.tensor_tensor(
                    ln, xc(c, t0, WB), mbB.ap()[:, :], OP.subtract))
                p.op("v", lambda e, ln=ln: e.tensor_tensor(ln, ln, sbB.ap()[:, :], OP.mult))
            hv_ln = p.sem["v"].cur()
            p.wait("t", hv_ln)
            if BCUT is not None and BCUT <= 1: continue
            r3k = bk0.ap()[:, :].rearrange("a (c w) -> a c w", w=264)
            r3q = bq0.ap()[:, :].rearrange("a (c w) -> a c w", w=264)
            r3v = bv0.ap()[:, :].rearrange("a (c w) -> a c w", w=264)
            lsv = leadsv.ap()[:, 0:24].rearrange("a (c w) -> a c w", w=3)
            if s > 0:
                for r3 in (r3k, r3q):
                    p.op("v", lambda e, r3=r3: e.tensor_copy(r3[:, :, 0:3], r3[:, :, 256:259]))
                p.op("v", lambda e, lsv=lsv, r3v=r3v: e.tensor_copy(r3v[:, :, 0:3], lsv[:, :, :]))
            hv_lead = p.sem["v"].cur()

            def mk_out(r3, bias_t):
                def out_fn(m, b, r3=r3, bias_t=bias_t):
                    return p.op("v", lambda e, m=m, b=b, r3=r3, bias_t=bias_t:
                                e.tensor_scalar(r3[:, m, 3:259], b[:, 0:WB],
                                                bias_t.ap()[:, m:m + 1], None, OP.add))
                return out_fn
            lnf = lambda k: lnB.ap()[:, k * WB:(k + 1) * WB]
            p.wait("t", hv_lead)
            stream_mm(scr["k"], lnf, mk_out(r3k, bias_k), nbank0=0)
            stream_mm(scr["q"], lnf, mk_out(r3q, bias_q), nbank0=0)
            stream_mm(scr["v"], lnf, mk_out(r3v, bias_v), nbank0=0)
            if s == 0:
                xbar(1)
            if BCUT is not None and BCUT <= 2: continue
            b4 = bk.acquire(p, 4)
            hv = None
            for k in range(8):
                hv = p.top(lambda e, b4=b4, k=k: e.matmul(
                    b4[:16, 0:WB], cWbT.ap()[:, k * 16:(k + 1) * 16], lnf(k),
                    start=(k == 0), stop=(k == 7)))
            bk.produced(p, 4, hv)
            bk.consume(p, 4, "s")
            bstrip = stripB.ap()[:16, 0:WB]
            hv = p.op("s", lambda e, b4=b4, bstrip=bstrip: e.activation(
                bstrip, b4[:16, 0:WB], AF.Sigmoid, bias=bias_beta.ap()[:16, 0:1]))
            bk.consumed(p, 4, hv)
            p.wait("t", hv)
            for c in range(NCH):
                b3 = bk.acquire(p, 3)
                hv = p.top(lambda e, c=c, b3=b3, bstrip=bstrip: e.matmul(
                    b3[:, 0:WB], cS16T.ap()[:16, c * 128:(c + 1) * 128], bstrip,
                    start=True, stop=True))
                bk.produced(p, 3, hv)
                bk.consume(p, 3, "v")
                hv2 = p.op("v", lambda e, c=c, b3=b3: e.tensor_copy(
                    betab.ap()[:, c * WB:(c + 1) * WB], b3[:, 0:WB]))
                bk.consumed(p, 3, hv2)
            if BCUT is not None and BCUT <= 3: continue
            conv4(bk0, lambda c: bkh.ap()[:, c * WB:(c + 1) * WB], convk)
            conv4(bq0, lambda c: bfm.ap()[:, c * WB:(c + 1) * WB], convq)
            conv4(bv0, lambda c: lnB.ap()[:, c * WB:(c + 1) * WB], convv)
            if BCUT is not None and BCUT <= 4: continue
            headnorm(bkh, 256, is_rms=False)
            headnorm(bfm, 512, is_rms=False)
            if BCUT is not None and BCUT <= 5: continue
            hvv = p.sem["v"].cur()
            p.wait("s", hvv)
            for c in range(NCH):
                sl = lnB.ap()[:, c * WB:(c + 1) * WB]
                hv = p.op("s", lambda e, sl=sl: e.activation(sl, sl, AF.Silu))
            p.wait("v", p.sem["s"].cur())
            for c in range(NCH):
                p.op("v", lambda e, c=c: e.tensor_tensor(
                    nbfm.ap()[:, c * WB:(c + 1) * WB], betab.ap()[:, c * WB:(c + 1) * WB],
                    lnB.ap()[:, c * WB:(c + 1) * WB], OP.mult))
            p.op("v", lambda e, lsv=lsv, r3v=r3v: e.tensor_copy(lsv[:, :, :], r3v[:, :, 256:259]))
            for c in range(NCH):
                bb_ = betab.ap()[:, c * WB:(c + 1) * WB]
                p.op("v", lambda e, c=c, bb_=bb_: e.tensor_tensor(
                    bb_, bb_, bkh.ap()[:, c * WB:(c + 1) * WB], OP.mult))
                p.op("v", lambda e, c=c, bb_=bb_, r3v=r3v: e.tensor_scalar(
                    r3v[:, c, 3:259], bb_, -1.0, None, OP.mult))
            hv_pre = p.sem["v"].cur()
            p.wait("t", hv_pre)
            if BCUT is not None and BCUT <= 6: continue
            if DBG == 5 and s == DBGCH // 4:
                dump_dbg([(0, bkh.ap()[:, 0:2048]), (1, bfm.ap()[:, 0:2048]),
                          (2, betab.ap()[:, 0:2048]), (3, nbfm.ap()[:, 0:2048]),
                          (5, bk0.ap()[:, 0:2048])])
            for ii in range(4):
                gi = s * 4 + ii
                for g in range(NCH):
                    delta_chunk(gi, ii, g)
            if BCUT is not None and BCUT <= 7: continue
            if s == 0:
                xbar(3)
            headnorm(lnB, 768, is_rms=True)
            if BCUT is not None and BCUT <= 8: continue
            hv_o = p.sem["v"].cur()
            p.wait("t", hv_o)
            def out_res(m, b):
                hv = p.op("v", lambda e, m=m, b=b: e.tensor_scalar(
                    smallB.ap()[:, 0:WB], b[:, 0:WB], bias_out.ap()[:, m:m + 1], None, OP.add))
                p.op("v", lambda e, m=m, t0=t0: e.tensor_tensor(
                    xc(m, t0, WB), xc(m, t0, WB), smallB.ap()[:, 0:WB], OP.add))
                return hv
            stream_mm(scr["out"], lnf if False else (lambda k: lnB.ap()[:, k * WB:(k + 1) * WB]),
                      out_res, nbank0=0)
            if s == 0:
                xbar(4)

    if stage >= 2:
        emit_ffn()
    if stage >= 3:
        emit_attn_b()
    if stage >= 4:
        emit_ffn()

    p.wait("t", p.sem["v"].cur())
    p.wait("t", p.sem["s"].cur())
    sem_out = [p.new_sem("outs0"), p.new_sem("outs1")]
    out_free = [None, None]
    for j in range(16):
        buf = xld.ap()[:, (j % 2) * 1024:(j % 2) * 1024 + 1024]
        if out_free[j % 2] is not None:
            p.wait("s", out_free[j % 2])
        for half in range(2):
            tiles = [xc(half * 4 + c, j * 128, 128) for c in range(4)]
            def evac(b, half=half, buf=buf):
                hv2 = None
                for q in range(4):
                    c = half * 4 + q
                    hv2 = p.op("s", lambda e, c=c, q=q, b=b, buf=buf:
                               e.copy(buf[:, c * 128:(c + 1) * 128], b[:, q * 128:(q + 1) * 128]))
                return hv2
            transpose_group(tiles, 6 + half, evac, evac_engine="s")
        out_free[j % 2] = p.dma("s", dy[j * 128:(j + 1) * 128, :], buf, sem=sem_out[j % 2])
    for so in sem_out:
        p.wait("s", so.cur())
        p.wait("y", so.cur())
        p.wait("g", so.cur())

    with nc.Block() as block:
        p.emit(block)
    stack.close()
    return nc


def kernel(L0_attn_a_x, L0_attn_a_Wqkv, L0_attn_a_bqkv, L0_attn_a_Wproj, L0_attn_a_bproj,
           L0_attn_a_scale, L0_attn_b_Wk, L0_attn_b_bk, L0_attn_b_Wq, L0_attn_b_bq,
           L0_attn_b_Wv, L0_attn_b_bv, L0_attn_b_Wbeta, L0_attn_b_bbeta,
           L0_attn_b_conv_k, L0_attn_b_conv_q, L0_attn_b_conv_v, L0_attn_b_rms_w,
           L0_attn_b_Wout, L0_attn_b_bout, L0_ffn_w1, L0_ffn_conv_w, L0_ffn_conv_b,
           L0_ffn_w2):
    nc = build_program(STAGE)
    consts = build_constants()
    shared = {
        "aWqkv": np.ascontiguousarray(L0_attn_a_Wqkv), "abqkv": np.ascontiguousarray(L0_attn_a_bqkv),
        "aWproj": np.ascontiguousarray(L0_attn_a_Wproj), "abproj": np.ascontiguousarray(L0_attn_a_bproj),
        "ascale": np.reshape(np.asarray(L0_attn_a_scale, dtype=np.float32), [1]),
        "bWk": np.ascontiguousarray(L0_attn_b_Wk), "bbk": np.ascontiguousarray(L0_attn_b_bk),
        "bWq": np.ascontiguousarray(L0_attn_b_Wq), "bbq": np.ascontiguousarray(L0_attn_b_bq),
        "bWv": np.ascontiguousarray(L0_attn_b_Wv), "bbv": np.ascontiguousarray(L0_attn_b_bv),
        "bWbeta": np.ascontiguousarray(L0_attn_b_Wbeta), "bbbeta": np.ascontiguousarray(L0_attn_b_bbeta),
        "bconvk": np.ascontiguousarray(L0_attn_b_conv_k), "bconvq": np.ascontiguousarray(L0_attn_b_conv_q),
        "bconvv": np.ascontiguousarray(L0_attn_b_conv_v), "brms": np.ascontiguousarray(L0_attn_b_rms_w),
        "bWout": np.ascontiguousarray(L0_attn_b_Wout), "bbout": np.ascontiguousarray(L0_attn_b_bout),
        "fw1": np.ascontiguousarray(L0_ffn_w1),
        "fconvw": np.ascontiguousarray(np.reshape(L0_ffn_conv_w, (DH, 3))),
        "fconvb": np.ascontiguousarray(L0_ffn_conv_b),
        "fw2": np.ascontiguousarray(L0_ffn_w2),
    }
    for k, v in consts.items():
        if k == "IDTB":
            shared["c_" + k] = np.ascontiguousarray(v.astype(ml_dtypes.bfloat16))
        else:
            shared["c_" + k] = np.ascontiguousarray(v.astype(np.float32))
    xin = np.ascontiguousarray(np.asarray(L0_attn_a_x, dtype=np.float32))
    in_maps = []
    for b in range(8):
        m = dict(shared)
        m["x_in"] = np.ascontiguousarray(xin[b])
        in_maps.append(m)
    res = run_bass_kernel_spmd(nc, in_maps, core_ids=list(range(8)))
    out = np.stack([res.results[b]["y_out"] for b in range(8)], axis=0)
    kernel.last_results = res
    return out


# version 03
# speedup vs baseline: 1.0409x; 1.0409x over previous
import contextlib
import numpy as np
import ml_dtypes
import concourse.bass as bass
import concourse.mybir as mybir
from concourse.bass_utils import run_bass_kernel_spmd

F32 = mybir.dt.float32
F16 = mybir.dt.float16
BF16 = mybir.dt.bfloat16
I32 = mybir.dt.int32
AF = mybir.ActivationFunctionType
OP = mybir.AluOpType
AX = mybir.AxisListType

T = 2048
D = 1024
NCH = 8
DH = 4096
HEADS = 16
HD = 64
CC = 64
NDC = 32
LN_EPS = 1e-5
WA = 256
WB = 256
WF = 128

STAGE = 4
DBG = False
DBGCH = 0
BSTRIPS = None
BCUT = None
DCUT = None
BAR = None


def _hilo(a64):
    hi = a64.astype(np.float32)
    lo = (a64 - hi.astype(np.float64)).astype(np.float32)
    return hi, lo


def build_constants():
    c = {}
    c["IDT"] = np.eye(128, dtype=np.float32)
    c["IDTB"] = np.eye(128, dtype=np.float32)
    c["ONES"] = np.ones((128, 128), dtype=np.float32)
    s16 = np.zeros((128, NCH * 16), dtype=np.float32)
    s16t = np.zeros((16, NCH * 128), dtype=np.float32)
    for ch in range(NCH):
        for p in range(128):
            h = 2 * ch + (1 if p >= 64 else 0)
            s16[p, ch * 16 + h] = 1.0
            s16t[h, ch * 128 + p] = 1.0
    c["S16"] = s16
    c["S16T"] = s16t
    tril = np.tril(np.ones((CC, CC)), -1)
    c["NMASK"] = np.tile(-tril, (1, 2)).astype(np.float32)
    c["NMASKT"] = np.tile(-tril.T, (1, 2)).astype(np.float32)
    c["I64S"] = np.tile(np.eye(CC), (1, 2)).astype(np.float32)
    a_all = np.zeros((NDC, CC))
    w_all = np.zeros((NDC, CC))
    c1_all = np.zeros(NDC)
    c2_all = np.zeros(NDC)
    tmT_all = np.zeros((NDC, CC, CC))
    Cf = float(CC)
    for i in range(NDC):
        ts = float(i * CC)
        idx = np.arange(1, CC + 1, dtype=np.float64) + ts
        cs = np.cumsum(idx)
        weighted_idx = idx + idx.sum() - cs
        w_all[i] = weighted_idx / ((1 + ts + Cf) * (ts + Cf) / 2)
        c1_all[i] = (ts / (ts + Cf)) * ((1 + ts) / (1 + ts + Cf)) if i > 0 else 0.0
        c2_all[i] = (Cf / (ts + Cf)) * ((ts * 2 + 1 + Cf) / (1 + ts + Cf))
        tri0 = (1 + ts) * ts / 2
        a_all[i] = tri0 / (tri0 + cs)
        r = cs[:, None]
        j_idx = (cs - idx)[None, :]
        tmT_all[i] = np.where(r >= j_idx, (r - j_idx) / (tri0 + r), 0.0).T
    ab = np.tile(a_all.reshape(1, NDC * CC), (128, 1))
    c["AB_hi"], c["AB_lo"] = _hilo(ab)
    wcol = np.concatenate([w_all.T, w_all.T], axis=0)
    c["WCOL_hi"], c["WCOL_lo"] = _hilo(wcol)
    c["C1_hi"], c["C1_lo"] = _hilo(np.tile(c1_all[None, :], (128, 1)))
    c["C2_hi"], c["C2_lo"] = _hilo(np.tile(c2_all[None, :], (128, 1)))
    tmt = np.zeros((64, NDC * CC))
    for i in range(NDC):
        tmt[:, i * CC:(i + 1) * CC] = tmT_all[i]
    c["TMT_hi"], c["TMT_lo"] = _hilo(tmt)
    return c



class Sem:
    CAP = 1_000_000_000

    def __init__(self, nc, stack, name):
        self.nc, self.stack, self.name = nc, stack, name
        self.idx = 0
        self.h = stack.enter_context(nc.semaphore(f"{name}0"))
        self.n = 0

    def add(self, k):
        if self.n + k > self.CAP:
            self.idx += 1
            self.h = self.stack.enter_context(self.nc.semaphore(f"{self.name}{self.idx}"))
            self.n = 0
        self.n += k
        return (self.h, self.n)

    def cur(self):
        return (self.h, self.n)


class Prog:
    ENGS = ("t", "v", "s", "g", "y")

    def __init__(self, nc, stack):
        self.nc = nc
        self.stack = stack
        self.ops = {e: [] for e in self.ENGS}
        self.sem = {e: Sem(nc, stack, f"c{e}") for e in self.ENGS}
        self.dmas = {q: Sem(nc, stack, f"d{q}") for q in ("y", "s", "g")}
        self.waited = {e: {} for e in self.ENGS}
        self._nsem = [0]

    def new_sem(self, name=None):
        self._nsem[0] += 1
        return Sem(self.nc, self.stack, name or f"x{self._nsem[0]}")

    def wait(self, e, hv):
        if hv is None:
            return
        h, v = hv
        seen = self.waited[e].get(id(h), -1)
        if v <= seen:
            return
        self.waited[e][id(h)] = v
        self.ops[e].append(lambda eng, h=h, v=v: eng.wait_ge(h, v))

    def op(self, e, fn, drain=True):
        hv = self.sem[e].add(1)
        def run(eng, fn=fn, hv=hv, drain=drain):
            fn(eng).then_inc(hv[0], 1)
            if drain:
                eng.drain()
        self.ops[e].append(run)
        return hv

    def top(self, fn):
        hv = self.sem["t"].add(1)
        self.ops["t"].append(lambda eng, fn=fn, hv=hv: fn(eng).then_inc(hv[0], 1))
        return hv

    def dma(self, q, dst, src, slow=False, sem=None):
        hv = (sem if sem is not None else self.dmas[q]).add(16)
        if slow:
            self.ops[q].append(lambda eng, d=dst, s=src, hv=hv: eng.dma_start(
                d, s, allow_slow_non_contiguous=True).then_inc(hv[0], 16))
        else:
            self.ops[q].append(lambda eng, d=dst, s=src, hv=hv: eng.dma_start(d, s).then_inc(hv[0], 16))
        return hv

    def emit(self, block):
        decs = {"t": block.tensor, "v": block.vector, "s": block.scalar,
                "g": block.gpsimd, "y": block.sync}
        for e in self.ENGS:
            ops = self.ops[e]
            @decs[e]
            def _(eng, ops=ops):
                for fn in ops:
                    fn(eng)


class Banks:
    def __init__(self, nc, stack):
        self.b = [stack.enter_context(nc.psum_tensor(f"bank{i}", [128, 512], F32))
                  for i in range(8)]
        self.cons = [[] for _ in range(8)]
        self.prod = [None] * 8

    def acquire(self, p, i):
        for hv in self.cons[i]:
            p.wait("t", hv)
        self.cons[i] = []
        return self.b[i]

    def produced(self, p, i, hv):
        self.prod[i] = hv

    def consume(self, p, i, e):
        if self.prod[i] is not None:
            p.wait(e, self.prod[i])

    def consumed(self, p, i, hv):
        self.cons[i].append(hv)



def build_program(stage=None):
    if stage is None:
        stage = STAGE
    stack = contextlib.ExitStack()
    nc = bass.Bass("TRN2", target_bir_lowering=False)

    dx = nc.dram_tensor("x_in", [T, D], F32, kind="ExternalInput").ap()
    dy = nc.dram_tensor("y_out", [T, D], F32, kind="ExternalOutput").ap()
    ddbg = [nc.dram_tensor(f"dbg{i}", [128, 2048], F32, kind="ExternalOutput").ap()
            for i in range(6)]
    dbg_done = [False]
    def dump_dbg(aps):
        if dbg_done[0]:
            return
        dbg_done[0] = True
        p.wait("g", p.sem["v"].cur())
        p.wait("g", p.sem["s"].cur())
        hv = None
        sem_dbg = p.new_sem("dbg")
        for i, ap in aps:
            n = int(ap.ap[-1][1])
            np_ = int(ap.ap[0][1])
            hv = p.dma("g", ddbg[i][:np_, 0:n], ap, sem=sem_dbg)
        p.wait("v", hv)
        p.wait("s", hv)
        p.wait("t", hv)

    din = {}
    for name, shape in [
        ("aWqkv", [3 * D, D]), ("abqkv", [3 * D]), ("aWproj", [D, D]), ("abproj", [D]),
        ("ascale", [1]),
        ("bWk", [D, D]), ("bbk", [D]), ("bWq", [D, D]), ("bbq", [D]),
        ("bWv", [D, D]), ("bbv", [D]), ("bWbeta", [HEADS, D]), ("bbbeta", [HEADS]),
        ("bconvk", [D, 4]), ("bconvq", [D, 4]), ("bconvv", [D, 4]),
        ("brms", [HD]), ("bWout", [D, D]), ("bbout", [D]),
        ("fw1", [DH, D]), ("fconvw", [DH, 3]), ("fconvb", [DH]), ("fw2", [D, DH]),
    ]:
        din[name] = nc.dram_tensor(name, shape, F32, kind="ExternalInput").ap()

    CONSTS = build_constants()
    dcon = {}
    for k, v in CONSTS.items():
        dt = BF16 if k == "IDTB" else F32
        if k == "IDTB":
            dcon[k] = nc.dram_tensor("c_" + k, list(v.shape), BF16, kind="ExternalInput").ap()
        else:
            dcon[k] = nc.dram_tensor("c_" + k, list(v.shape), F32, kind="ExternalInput").ap()

    scr = {}
    scr["qkv"] = nc.dram_tensor("s_qkv", [24, 8, 128, 128], F32, kind="Internal").ap()
    scr["proj"] = nc.dram_tensor("s_proj", [8, 8, 128, 128], F32, kind="Internal").ap()
    for w in ("k", "q", "v", "out"):
        scr[w] = nc.dram_tensor(f"s_{w}", [8, 8, 128, 128], F32, kind="Internal").ap()
    scr["t1"] = nc.dram_tensor("s_t1", [32, 128, 1024], BF16, kind="Internal").ap()
    scr["t2"] = nc.dram_tensor("s_t2", [32, 8, 128, 128], BF16, kind="Internal").ap()

    cur = [16512]
    offs = {}

    def P_(name, shape, dtype=F32, at=None, alias=None):
        save = None
        if alias is not None:
            save = cur[0]
            cur[0] = offs[alias]
        elif at is not None:
            cur[0] = at
        nbytes = int(np.prod(shape[1:])) * mybir.dt.size(dtype)
        off = (cur[0] + 31) // 32 * 32
        h = nc.alloc_sbuf_tensor_at(name, shape, dtype, offset=off)
        offs[name] = off
        cur[0] = off + nbytes
        if save is not None and save > cur[0]:
            cur[0] = save
        return h

    xfm = P_("xfm", [128, NCH * T])
    cIDT = P_("cIDT", [128, 128])
    cIDTB = P_("cIDTB", [128, 128], BF16)
    cONES = P_("cONES", [128, 128])
    cS16 = P_("cS16", [128, NCH * 16])
    cS16T = P_("cS16T", [16, NCH * 128])
    cNMASK = P_("cNMASK", [64, 2 * CC])
    cNMASKT = P_("cNMASKT", [64, 2 * CC])
    cI64S = P_("cI64S", [64, 2 * CC])
    cAB = P_("cAB", [128, NDC * CC])
    cTMT = P_("cTMT", [64, NDC * CC])
    cWCOL = P_("cWCOL", [128, NDC])
    cC1 = P_("cC1", [128, NDC])
    cC2 = P_("cC2", [128, NDC])
    bias_qkv = P_("b_qkv", [128, 24])
    bias_proj = P_("b_proj", [128, 8])
    bias_k = P_("b_k", [128, 8]); bias_q = P_("b_q", [128, 8])
    bias_v = P_("b_v", [128, 8]); bias_out = P_("b_out", [128, 8])
    bias_beta = P_("b_beta", [16, 1])
    convk = P_("convk", [128, NCH * 4]); convq = P_("convq", [128, NCH * 4])
    convv = P_("convv", [128, NCH * 4])
    fcw = P_("fcw", [128, 32 * 3]); fcb = P_("fcb", [128, 32])
    rmsw = P_("rmsw", [128, 1])
    scale_a = P_("scale_a", [128, 1])
    cw1t = P_("cw1t", [128, 1]); cw1r = P_("cw1r", [128, 1])
    cw2t = P_("cw2t", [128, 1]); cw2r = P_("cw2r", [128, 1])
    cWbT = P_("cWbT", [128, NCH * 16])
    carry = P_("carry", [128, NCH * 2])
    PERM_END = cur[0]
    AR = (PERM_END + 63) // 64 * 64
    assert AR < 110500, f"perm too big: {AR}"

    wnat = P_("wnat", [128, 2 * DH], at=AR)
    pstage = P_("pstage", [128, 8 * 128])
    pstageb = P_("pstageb", [128, 8 * 128], BF16)
    pval = P_("pval", [128, 512])
    pint = P_("pint", [128, 512], I32)
    pf = P_("pf", [128, 512])
    pd = P_("pd", [128, 512])
    xld = P_("xld", [128, 2 * 1024])
    lnA = P_("lnA", [128, NCH * WA], at=AR)
    qkvA = P_("qkvA", [128, 24 * WA])
    kvA = P_("kvA", [128, NCH * WA])
    kcA = P_("kcA", [128, NCH * WA])
    wstrA = P_("wstrA", [128, 2 * 1024])
    smallA = P_("smallA", [128, 1024])
    stripA = P_("stripA", [128, 1024])
    mbA = P_("mbA", [128, WA])
    sbA = P_("sbA", [128, WA])
    czA = P_("czA", [128, WA])
    aintA = P_("aintA", [128, 512], I32)
    t1T = P_("t1T", [128, 32 * 1024], BF16, at=AR)
    hbuf = P_("hbuf", [128, 32 * 131])
    hc8 = P_("hc8", [128, 8 * 132])
    hsave = P_("hsave", [128, 32])
    m1b = P_("m1b", [128, 8 * WF], BF16)
    m2b = P_("m2b", [128, 32 * 256], BF16)
    t2buf = P_("t2buf", [128, 8 * 128], BF16)
    fsc = P_("fsc", [128, 8 * 128])
    fstats = P_("fstats", [128, 1024], alias="hc8")
    fval = P_("fval", [128, 512]); fint = P_("fint", [128, 512], I32)
    ff = P_("ff", [128, 512]); fd = P_("fd", [128, 512])
    famx = P_("famx", [128, 8 * 129], alias="fval")
    fsm = P_("fsm", [128, 384])
    FF_END = cur[0]
    assert FF_END <= 229344, f"ffn map too big: {FF_END}"
    lnB = P_("lnB", [128, NCH * WB], at=AR)
    bk0 = P_("bk0", [128, NCH * 264])
    bq0 = P_("bq0", [128, NCH * 264])
    bv0 = P_("bv0", [128, NCH * 264])
    bkh = P_("bkh", [128, NCH * WB])
    bfm = P_("bfmq", [128, NCH * WB])
    betab = P_("betab", [128, NCH * WB])
    nbfm = P_("nbfm", [128, NCH * WB])
    wstrB = P_("wstrB", [128, 2 * 1024])
    smallB = P_("smallB", [128, 1024])
    stripB = P_("stripB", [128, 1024])
    mbB = P_("mbB", [128, WB]); sbB = P_("sbB", [128, WB])
    leadsv = P_("leadsv", [128, NCH * 3 * 3])
    dW = P_("dW", [128, NCH * CC])
    dWavg = P_("dWavg", [128, NCH * CC])
    dPows = P_("dPows", [64, 12 * 2 * CC])
    dP = P_("dP", [64, 2 * CC])
    dG = P_("dG", [64, 2 * CC])
    dU = P_("dU", [64, 2 * CC])
    dUw = P_("dUw", [64, 2 * CC])
    dWmix = P_("dWmix", [128, CC])
    dT1 = P_("dT1", [64, 2 * CC])
    dT2 = P_("dT2", [128, CC])
    dKts = P_("dKts", [64, 2 * CC])
    dCmb = P_("dCmb", [128, CC])
    AB_END = cur[0]
    assert AB_END <= 229344, f"attn_b map too big: {AB_END}"

    p = Prog(nc, stack)
    bk = Banks(nc, stack)

    def xbar(pid):
        if BAR != pid:
            return
        hv_t, hv_v, hv_s = p.sem["t"].cur(), p.sem["v"].cur(), p.sem["s"].cur()
        for e in ("t", "v", "s", "g", "y"):
            p.wait(e, hv_t)
            p.wait(e, hv_v)
            p.wait(e, hv_s)

    def xc(c, t0, w):
        return xfm.ap()[:, c * T + t0: c * T + t0 + w]

    POW = {"N1": 0, "NS1": 1, "N2": 2, "NS2": 3, "N4": 4, "NS4": 5, "N8": 6,
           "NS8": 7, "N16": 8, "NS16": 9, "N32": 10, "PI": 11}

    def pw(nm):
        i = POW[nm]
        return dPows.ap()[:64, i * 2 * CC:(i + 1) * 2 * CC]

    def pwh(nm, h):
        i = POW[nm]
        return dPows.ap()[:64, i * 2 * CC + h * CC:i * 2 * CC + (h + 1) * CC]

    def transpose_group(tiles, bank_i, evac_fn, evac_engine="s", ident=None):
        idn = ident if ident is not None else cIDT.ap()[:, :]
        b = bk.acquire(p, bank_i)
        hv = None
        for j, src in enumerate(tiles):
            hv = p.top(lambda e, s=src, d=b[:, j * 128:(j + 1) * 128], idn=idn:
                       e.transpose(d, s, idn))
        bk.produced(p, bank_i, hv)
        bk.consume(p, bank_i, evac_engine)
        hv2 = evac_fn(b)
        bk.consumed(p, bank_i, hv2)
        return hv2

    def emit_round(val_ap, int_ap, f_ap, d_ap, out_ap):
        p.wait("s", p.sem["v"].cur())
        p.op("s", lambda e: e.copy(int_ap, val_ap))
        hv = p.op("s", lambda e: e.copy(f_ap, int_ap))
        p.wait("v", hv)
        p.op("v", lambda e: e.tensor_tensor(d_ap, val_ap, f_ap, OP.subtract))
        p.op("v", lambda e: e.scalar_tensor_tensor(f_ap, d_ap, 0.5, f_ap, OP.is_ge, OP.add))
        p.op("v", lambda e: e.tensor_scalar(d_ap, d_ap, -0.5, None, OP.is_le))
        return p.op("v", lambda e: e.tensor_tensor(out_ap, f_ap, d_ap, OP.subtract))

    def emit_rsqrt(out_ap, in_ap, eps, t1_ap, t2_ap, ln_ap):
        p.op("v", lambda e: e.tensor_scalar(t1_ap, in_ap, float(eps), None, OP.add))
        p.wait("s", p.sem["v"].cur())
        p.op("s", lambda e: e.activation(ln_ap, t1_ap, AF.Ln))
        hv = p.op("s", lambda e: e.activation(out_ap, ln_ap, AF.Exp, scale=-0.5))
        p.wait("v", hv)
        p.op("v", lambda e: e.tensor_tensor(t2_ap, out_ap, out_ap, OP.mult))
        p.op("v", lambda e: e.tensor_tensor(t2_ap, t2_ap, t1_ap, OP.mult))
        p.op("v", lambda e: e.tensor_scalar(t2_ap, t2_ap, -0.5, 1.5, OP.mult, OP.add))
        return p.op("v", lambda e: e.tensor_tensor(out_ap, out_ap, t2_ap, OP.mult))

    def emit_sqrt(out_ap, in_ap, t1_ap, t2_ap, ln_ap):
        p.op("v", lambda e: e.tensor_copy(t2_ap, in_ap))
        p.wait("s", p.sem["v"].cur())
        p.op("s", lambda e: e.activation(ln_ap, t2_ap, AF.Ln))
        hv = p.op("s", lambda e: e.activation(out_ap, ln_ap, AF.Exp, scale=0.5))
        p.wait("v", hv)
        p.op("v", lambda e: e.reciprocal(t1_ap, out_ap))
        p.op("v", lambda e: e.tensor_tensor(t1_ap, t1_ap, t2_ap, OP.mult))
        p.op("v", lambda e: e.tensor_tensor(t2_ap, out_ap, t1_ap, OP.add))
        return p.op("v", lambda e: e.tensor_scalar(out_ap, t2_ap, 0.5, None, OP.mult))

    sem_const = p.new_sem("const")
    for name, dst in [("IDT", cIDT), ("IDTB", cIDTB), ("ONES", cONES), ("S16", cS16),
                      ("S16T", cS16T), ("NMASK", cNMASK), ("NMASKT", cNMASKT),
                      ("I64S", cI64S)]:
        sh = dst.shape
        p.dma("y", dst.ap()[:sh[0], :], dcon[name], sem=sem_const)
    prev_add = None
    for hi, lo, dst in [("AB_hi", "AB_lo", cAB), ("WCOL_hi", "WCOL_lo", cWCOL),
                        ("C1_hi", "C1_lo", cC1), ("C2_hi", "C2_lo", cC2),
                        ("TMT_hi", "TMT_lo", cTMT)]:
        w = dst.shape[1]
        np_ = dst.shape[0]
        if prev_add is not None:
            p.wait("y", prev_add)
        hv0 = p.dma("y", dst.ap()[:, :], dcon[hi], sem=sem_const)
        hv = p.dma("y", wnat.ap()[:np_, :w], dcon[lo], sem=sem_const)
        p.wait("v", hv)
        prev_add = p.op("v", lambda e, d=dst, w=w, np_=np_: e.tensor_tensor(
            d.ap()[:, :], d.ap()[:, :], wnat.ap()[:np_, :w], OP.add))
    p.wait("y", prev_add)
    for nm, dst in [("abqkv", bias_qkv), ("abproj", bias_proj), ("bbk", bias_k),
                    ("bbq", bias_q), ("bbv", bias_v), ("bbout", bias_out)]:
        p.dma("y", dst.ap()[:, :], din[nm].rearrange("(c p) -> p c", p=128), slow=True, sem=sem_const)
    p.dma("y", bias_beta.ap()[:16, :], din["bbbeta"].rearrange("(h o) -> h o", o=1), slow=True, sem=sem_const)
    for nm, dst in [("bconvk", convk), ("bconvq", convq), ("bconvv", convv)]:
        p.dma("y", dst.ap()[:, :].rearrange("p (c k) -> p c k", k=4),
              din[nm].rearrange("(c p) k -> p c k", p=128), slow=True, sem=sem_const)
    p.dma("y", fcw.ap()[:, :].rearrange("p (c k) -> p c k", k=3),
          din["fconvw"].rearrange("(c p) k -> p c k", p=128), slow=True, sem=sem_const)
    p.dma("y", fcb.ap()[:, :], din["fconvb"].rearrange("(c p) -> p c", p=128), slow=True, sem=sem_const)
    p.dma("y", rmsw.ap()[:64, :], din["brms"].rearrange("(d o) -> d o", o=1), slow=True, sem=sem_const)
    p.dma("y", rmsw.ap()[64:128, :], din["brms"].rearrange("(d o) -> d o", o=1), slow=True, sem=sem_const)
    p.dma("y", scale_a.ap()[:1, :], din["ascale"].rearrange("(a b) -> a b", b=1), slow=True, sem=sem_const)
    p.wait("t", sem_const.cur())
    b0 = bk.acquire(p, 0)
    hv = p.top(lambda e: e.matmul(b0[:, 0:1], cONES.ap()[:1, :], scale_a.ap()[:1, :],
                                  start=True, stop=True))
    bk.produced(p, 0, hv)
    bk.consume(p, 0, "v")
    hv = p.op("v", lambda e: e.tensor_copy(scale_a.ap()[:, :], b0[:, 0:1]))
    bk.consumed(p, 0, hv)

    sem_wld = [p.new_sem("wld0"), p.new_sem("wld1")]
    sem_pst = p.new_sem("pst")
    last_tr = [None, None]
    last_wr = [None, None]

    def stage_fp32_weight(dsrc, dscr, MCH):
        for m in range(MCH):
            buf = wnat.ap()[:, (m % 2) * D:(m % 2) * D + D]
            if last_tr[m % 2] is not None:
                p.wait("y", last_tr[m % 2])
            if last_wr[m % 2] is not None:
                p.wait("y", last_wr[m % 2])
            hv = p.dma("y", buf, dsrc[m * 128:(m + 1) * 128, :], sem=sem_wld[m % 2])
            last_wr[m % 2] = hv
            p.wait("t", hv)
            for half in range(2):
                tiles = [buf[:, (half * 4 + k) * 128:(half * 4 + k + 1) * 128] for k in range(4)]
                def evac(b, half=half):
                    return p.op("v", lambda e: e.tensor_copy(
                        pstage.ap()[:, half * 512:(half + 1) * 512], b[:, 0:512]))
                transpose_group(tiles, 2 + half, evac, evac_engine="v")
            last_tr[m % 2] = p.sem["t"].cur()
            hvv = p.sem["v"].cur()
            p.wait("s", hvv)
            hv = p.dma("s", dscr[m, :, :, :].rearrange("k a b -> a k b"),
                       pstage.ap()[:, :].rearrange("a (k b) -> a k b", b=128), sem=sem_pst)
            p.wait("v", hv)

    stage_fp32_weight(din["aWqkv"], scr["qkv"], 24)
    stage_fp32_weight(din["aWproj"], scr["proj"], 8)
    stage_fp32_weight(din["bWk"], scr["k"], 8)
    stage_fp32_weight(din["bWq"], scr["q"], 8)
    stage_fp32_weight(din["bWv"], scr["v"], 8)
    stage_fp32_weight(din["bWout"], scr["out"], 8)

    if last_tr[0] is not None:
        p.wait("y", last_tr[0])
    if last_wr[0] is not None:
        p.wait("y", last_wr[0])
    hv = p.dma("y", wnat.ap()[:16, 0:D], din["bWbeta"], sem=sem_wld[0])
    last_wr[0] = hv
    p.wait("t", hv)
    b4 = bk.acquire(p, 4)
    hv = None
    for k in range(8):
        hv = p.top(lambda e, k=k: e.transpose(b4[:, k * 16:(k + 1) * 16],
                                              wnat.ap()[:16, k * 128:(k + 1) * 128],
                                              cIDT.ap()[:16, :16]))
    bk.produced(p, 4, hv)
    last_tr[0] = hv
    last_tr[1] = hv
    bk.consume(p, 4, "v")
    hv = p.op("v", lambda e: e.tensor_copy(cWbT.ap()[:, :], b4[:, 0:NCH * 16]))
    bk.consumed(p, 4, hv)

    def ffn_stage2(dsrc, MCH, KW, cwt, cwr, dscr, kmajor):
        sem_f = [p.new_sem(), p.new_sem()]
        sem_fs = p.new_sem()
        for g in (last_tr[0], last_tr[1], last_wr[0], last_wr[1]):
            if g is not None:
                p.wait("y", g)
        p.op("v", lambda e: e.memset(pval.ap()[:, 0:1], 0.0))
        prev_red = [last_tr[0], last_tr[1]]
        for m in range(MCH):
            buf = wnat.ap()[:, (m % 2) * KW:(m % 2) * KW + KW]
            if prev_red[m % 2] is not None:
                p.wait("y", prev_red[m % 2])
            if last_wr[m % 2] is not None:
                p.wait("y", last_wr[m % 2])
            hv = p.dma("y", buf, dsrc[m * 128:(m + 1) * 128, :], sem=sem_f[m % 2])
            last_wr[m % 2] = hv
            p.wait("v", hv)
            p.op("v", lambda e, buf=buf: e.tensor_reduce(
                pval.ap()[:, 1:2], buf, AX.X, OP.add, apply_absolute_value=True))
            hvv = p.op("v", lambda e: e.tensor_tensor(pval.ap()[:, 0:1], pval.ap()[:, 0:1],
                                                      pval.ap()[:, 1:2], OP.add))
            if m + 1 < MCH:
                p.wait("y", hvv)
        p.wait("t", p.sem["v"].cur())
        b5 = bk.acquire(p, 5)
        hv = p.top(lambda e: e.matmul(b5[:1, 0:1], cONES.ap()[:, 0:1], pval.ap()[:, 0:1],
                                      start=True, stop=True))
        bk.produced(p, 5, hv)
        bk.consume(p, 5, "v")
        p.op("v", lambda e: e.tensor_scalar(pval.ap()[:1, 2:3], b5[:1, 0:1],
                                            1.0 / (MCH * 128 * KW), 1e-5, OP.mult, OP.max))
        hv = p.op("v", lambda e: e.reciprocal(pval.ap()[:1, 3:4], pval.ap()[:1, 2:3]))
        bk.consumed(p, 5, hv)
        p.wait("t", hv)
        b5 = bk.acquire(p, 5)
        hv = p.top(lambda e: e.matmul(b5[:, 0:2], cONES.ap()[:1, :], pval.ap()[:1, 2:4],
                                      start=True, stop=True))
        bk.produced(p, 5, hv)
        bk.consume(p, 5, "v")
        p.op("v", lambda e: e.tensor_copy(cwt.ap()[:, :], b5[:, 0:1]))
        hv = p.op("v", lambda e: e.tensor_copy(cwr.ap()[:, :], b5[:, 1:2]))
        bk.consumed(p, 5, hv)
        last = None
        hvv_last = p.sem["v"].cur()
        qgate = [hvv_last, hvv_last]
        for m in range(MCH):
            buf = wnat.ap()[:, (m % 2) * KW:(m % 2) * KW + KW]
            if qgate[m % 2] is not None:
                p.wait("y", qgate[m % 2])
            if last_wr[m % 2] is not None:
                p.wait("y", last_wr[m % 2])
            hv = p.dma("y", buf, dsrc[m * 128:(m + 1) * 128, :], sem=sem_f[m % 2])
            last_wr[m % 2] = hv
            p.wait("v", hv)
            p.wait("s", hv)
            for grp in range(KW // 512):
                g0 = grp * 512
                qgate[m % 2] = p.op("v", lambda e, buf=buf, g0=g0: e.tensor_scalar(
                    pval.ap()[:, :], buf[:, g0:g0 + 512], cwr.ap()[:, 0:1], None, OP.mult))
                emit_round(pval.ap()[:, :], pint.ap()[:, :], pf.ap()[:, :], pd.ap()[:, :],
                           pf.ap()[:, :])
                p.op("v", lambda e, g0=g0: e.tensor_scalar(
                    pstage.ap()[:, (g0 % 1024):(g0 % 1024) + 512],
                    pf.ap()[:, :], 1.0, -1.0, OP.min, OP.max))
                if (g0 % 1024) == 512 or KW == 512:
                    blk = g0 // 1024
                    hvq = p.sem["v"].cur()
                    p.wait("t", hvq)
                    for half in range(2):
                        tiles = [pstage.ap()[:, (half * 4 + k) * 128:(half * 4 + k + 1) * 128]
                                 for k in range(4)]
                        def evac(b, half=half):
                            return p.op("v", lambda e, half=half: e.tensor_copy(
                                pstageb.ap()[:, half * 512:(half + 1) * 512], b[:, 0:512]))
                        transpose_group(tiles, 2 + half, evac, evac_engine="v")
                    hvv = p.sem["v"].cur()
                    p.wait("g", hvv)
                    if last is not None:
                        p.wait("g", last)
                    if kmajor:
                        kbase = blk * 8
                        last = p.dma("g", dscr[kbase:kbase + 8, m, :, :].rearrange("k a b -> a k b"),
                                     pstageb.ap()[:, :].rearrange("a (k b) -> a k b", b=128),
                                     sem=sem_fs)
                    else:
                        last = p.dma("g", dscr[m, :, :], pstageb.ap()[:, :], sem=sem_fs)
                    p.wait("v", last)
        last_tr[0] = qgate[0]
        last_tr[1] = qgate[1]
        return

    if stage >= 2:
        ffn_stage2(din["fw1"], 32, D, cw1t, cw1r, scr["t1"], kmajor=False)
        ffn_stage2(din["fw2"], 8, DH, cw2t, cw2r, scr["t2"], kmajor=True)

    sem_xld = [p.new_sem(), p.new_sem()]
    xld_free = [None, None]
    for j in range(16):
        buf = xld.ap()[:, (j % 2) * 1024:(j % 2) * 1024 + 1024]
        if xld_free[j % 2] is not None:
            p.wait("y", xld_free[j % 2])
        hv = p.dma("y", buf, dx[j * 128:(j + 1) * 128, :], sem=sem_xld[j % 2])
        p.wait("t", hv)
        for half in range(2):
            tiles = [buf[:, (half * 4 + c) * 128:(half * 4 + c + 1) * 128] for c in range(4)]
            def evac(b, half=half, j=j):
                hv2 = None
                for q in range(4):
                    c = half * 4 + q
                    hv2 = p.op("s", lambda e, c=c, q=q, b=b:
                               e.copy(xfm.ap()[:, c * T + j * 128: c * T + (j + 1) * 128],
                                      b[:, q * 128:(q + 1) * 128]))
                return hv2
            transpose_group(tiles, 6 + half, evac, evac_engine="s")
        xld_free[j % 2] = p.sem["t"].cur()

    def ln_stats(t0, W, mb_t, sb_t, small_t, strip_t, bankbase=6, amax_dst=None):
        ntile = W // 128
        for j in range(ntile):
            tj = t0 + j * 128
            xt = strip_t.ap()[:, 0:1024]
            for half in range(2):
                tiles = [xc(half * 4 + c, tj, 128) for c in range(4)]
                def evac(b, half=half, xt=xt):
                    hv2 = None
                    for q in range(4):
                        c = half * 4 + q
                        hv2 = p.op("s", lambda e, c=c, q=q, b=b, xt=xt:
                                   e.copy(xt[:, c * 128:(c + 1) * 128],
                                          b[:, q * 128:(q + 1) * 128]))
                    return hv2
                transpose_group(tiles, bankbase + half, evac, evac_engine="s")
            hvs = p.sem["s"].cur()
            p.wait("v", hvs)
            sm = small_t.ap()
            p.op("v", lambda e, xt=xt, sm=sm: e.tensor_reduce(sm[:, 0:1], xt[:, 0:D],
                                                              AX.X, OP.add))
            p.op("v", lambda e, sm=sm: e.tensor_scalar(sm[:, 0:1], sm[:, 0:1], 1.0 / D,
                                                       None, OP.mult))
            if amax_dst is not None:
                p.op("v", lambda e, xt=xt, sm=sm: e.tensor_scalar(
                    xt[:, 0:D], xt[:, 0:D], sm[:, 0:1], None, OP.subtract))
                hvr = p.op("v", lambda e, xt=xt, j=j: e.tensor_reduce(
                    amax_dst[:, j:j + 1], xt[:, 0:D], AX.X, OP.max, apply_absolute_value=True))
                p.wait("s", hvr)
                hv = p.op("s", lambda e, xt=xt: e.activation(xt[:, 0:D], xt[:, 0:D], AF.Square))
                p.wait("v", hv)
                p.op("v", lambda e, xt=xt, sm=sm: e.tensor_reduce(sm[:, 1:2], xt[:, 0:D],
                                                                  AX.X, OP.add))
                p.op("v", lambda e, sm=sm: e.tensor_scalar(sm[:, 1:2], sm[:, 1:2], 1.0 / D,
                                                           None, OP.mult))
            else:
                p.wait("s", p.sem["v"].cur())
                hv = p.op("s", lambda e, xt=xt: e.activation(xt[:, 0:D], xt[:, 0:D], AF.Square))
                p.wait("v", hv)
                p.op("v", lambda e, xt=xt, sm=sm: e.tensor_reduce(sm[:, 1:2], xt[:, 0:D],
                                                                  AX.X, OP.add))
                p.op("v", lambda e, sm=sm: e.tensor_scalar(sm[:, 1:2], sm[:, 1:2], 1.0 / D,
                                                           None, OP.mult))
                p.op("v", lambda e, sm=sm: e.tensor_tensor(sm[:, 2:3], sm[:, 0:1],
                                                           sm[:, 0:1], OP.mult))
                p.op("v", lambda e, sm=sm: e.tensor_tensor(sm[:, 1:2], sm[:, 1:2],
                                                           sm[:, 2:3], OP.subtract))
            emit_rsqrt(sm[:, 1:2], sm[:, 1:2], LN_EPS, sm[:, 2:3], sm[:, 3:4], sm[:, 4:5])
            p.op("v", lambda e, sm=sm: e.tensor_scalar(sm[:, 128:256], cONES.ap()[:, :],
                                                       sm[:, 0:1], None, OP.mult))
            hv = p.op("v", lambda e, sm=sm: e.tensor_scalar(sm[:, 256:384], cONES.ap()[:, :],
                                                            sm[:, 1:2], None, OP.mult))
            p.wait("t", hv)
            bb = bk.acquire(p, bankbase)
            p.top(lambda e, sm=sm: e.transpose(bb[:, 0:128], sm[:, 128:256], cIDT.ap()[:, :]))
            hv = p.top(lambda e, sm=sm: e.transpose(bb[:, 128:256], sm[:, 256:384],
                                                    cIDT.ap()[:, :]))
            bk.produced(p, bankbase, hv)
            bk.consume(p, bankbase, "s")
            p.op("s", lambda e, j=j: e.copy(mb_t.ap()[:, j * 128:(j + 1) * 128], bb[:, 0:128]))
            hv = p.op("s", lambda e, j=j: e.copy(sb_t.ap()[:, j * 128:(j + 1) * 128],
                                                 bb[:, 128:256]))
            bk.consumed(p, bankbase, hv)

    sem_wsA = [p.new_sem(), p.new_sem()]

    def emit_attn_a():
        p.op("v", lambda e: e.memset(carry.ap()[:, :], 0.0))
        p.op("v", lambda e: e.memset(czA.ap()[:, :], 0.0))
        p.wait("t", p.sem["s"].cur())
        p.wait("t", p.sem["v"].cur())
        p.wait("y", p.sem["t"].cur())
        p.wait("y", p.sem["s"].cur())
        p.wait("g", p.sem["t"].cur())
        wfree = [None, None]
        for s in range(T // WA):
            t0 = s * WA
            ln_stats(t0, WA, mbA, sbA, smallA, stripA, bankbase=6)
            p.wait("v", p.sem["s"].cur())
            for c in range(NCH):
                ln = lnA.ap()[:, c * WA:(c + 1) * WA]
                p.op("v", lambda e, c=c, ln=ln, t0=t0: e.tensor_tensor(
                    ln, xc(c, t0, WA), mbA.ap()[:, :], OP.subtract))
                p.op("v", lambda e, ln=ln: e.tensor_tensor(ln, ln, sbA.ap()[:, :], OP.mult))
            hv_ln = p.sem["v"].cur()
            p.wait("t", hv_ln)
            if s == 0 and DBG == 1:
                dump_dbg([(0, lnA.ap()[:, 0:2048]), (1, mbA.ap()[:, 0:WA]),
                          (2, sbA.ap()[:, 0:WA]), (3, xfm.ap()[:, 0:2048])])
            for m in range(24):
                wb = wstrA.ap()[:, (m % 2) * 1024:(m % 2) * 1024 + 1024]
                if wfree[m % 2] is not None:
                    p.wait("y", wfree[m % 2])
                hvw = p.dma("y", wb.rearrange("a (k b) -> a k b", b=128),
                            scr["qkv"][m, :, :, :].rearrange("k a b -> a k b"),
                            sem=sem_wsA[m % 2])
                p.wait("t", hvw)
                bi = m % 2
                b = bk.acquire(p, bi)
                hv = None
                for k in range(8):
                    hv = p.top(lambda e, b=b, wb=wb, k=k:
                               e.matmul(b[:, 0:WA], wb[:, k * 128:(k + 1) * 128],
                                        lnA.ap()[:, k * WA:(k + 1) * WA],
                                        start=(k == 0), stop=(k == 7)))
                wfree[m % 2] = hv
                bk.produced(p, bi, hv)
                bk.consume(p, bi, "v")
                hv = p.op("v", lambda e, m=m, b=b:
                          e.tensor_scalar(qkvA.ap()[:, m * WA:(m + 1) * WA], b[:, 0:WA],
                                          bias_qkv.ap()[:, m:m + 1], None, OP.add))
                bk.consumed(p, bi, hv)
            for m in range(16):
                u = qkvA.ap()[:, m * WA:(m + 1) * WA]
                mn = smallA.ap()[:, 0:WA]
                mx = smallA.ap()[:, WA:2 * WA]
                p.op("v", lambda e, u=u, mn=mn: e.tensor_scalar(mn, u, 0.0, None, OP.min))
                hv = p.op("v", lambda e, u=u, mx=mx: e.tensor_scalar(mx, u, 0.0, None, OP.max))
                p.wait("s", hv)
                hv = p.op("s", lambda e, mn=mn: e.activation(mn, mn, AF.Exp,
                                                             scale=scale_a.ap()[:, 0:1]))
                p.wait("v", hv)
                hv = p.op("v", lambda e, u=u, mn=mn, mx=mx:
                          e.scalar_tensor_tensor(u, mx, scale_a.ap()[:, 0:1], mn,
                                                 OP.mult, OP.add))
                p.wait("s", hv)
            for c in range(NCH):
                kq = qkvA.ap()[:, (8 + c) * WA:(9 + c) * WA]
                vv = qkvA.ap()[:, (16 + c) * WA:(17 + c) * WA]
                p.op("v", lambda e, c=c, kq=kq, vv=vv:
                     e.tensor_tensor(kvA.ap()[:, c * WA:(c + 1) * WA], kq, vv, OP.mult))
            for c in range(NCH):
                kq = qkvA.ap()[:, (8 + c) * WA:(9 + c) * WA]
                kc = kcA.ap()[:, c * WA:(c + 1) * WA]
                p.op("v", lambda e, c=c: e.tensor_copy(czA.ap()[:, 0:1],
                                                       carry.ap()[:, 2 * c:2 * c + 1]))
                p.op("v", lambda e, kq=kq, kc=kc:
                     e.tensor_tensor_scan(kc, kq, czA.ap()[:, 0:WA], 0.0, OP.add, OP.add))
                p.op("v", lambda e, c=c, kc=kc:
                     e.tensor_copy(carry.ap()[:, 2 * c:2 * c + 1], kc[:, WA - 1:WA]))
                kv = kvA.ap()[:, c * WA:(c + 1) * WA]
                p.op("v", lambda e, c=c: e.tensor_copy(czA.ap()[:, 0:1],
                                                       carry.ap()[:, 2 * c + 1:2 * c + 2]))
                p.op("v", lambda e, kv=kv:
                     e.tensor_tensor_scan(kv, kv, czA.ap()[:, 0:WA], 0.0, OP.add, OP.add))
                p.op("v", lambda e, c=c, kv=kv:
                     e.tensor_copy(carry.ap()[:, 2 * c + 1:2 * c + 2], kv[:, WA - 1:WA]))
            b2 = bk.acquire(p, 2)
            prev_mm = None
            for c in range(NCH):
                qq = qkvA.ap()[:, c * WA:(c + 1) * WA]
                kc = kcA.ap()[:, c * WA:(c + 1) * WA]
                if prev_mm is not None:
                    p.wait("v", prev_mm)
                hv = p.op("v", lambda e, qq=qq, kc=kc:
                          e.tensor_tensor(smallA.ap()[:, 0:WA], qq, kc, OP.mult))
                p.wait("t", hv)
                prev_mm = hv = p.top(lambda e, c=c, b2=b2: e.matmul(
                    b2[:16, 0:WA], cS16.ap()[:, c * 16:(c + 1) * 16], smallA.ap()[:, 0:WA],
                    start=(c == 0), stop=(c == 7)))
            bk.produced(p, 2, hv)
            bk.consume(p, 2, "v")
            den = stripA.ap()[:16, 0:WA]
            p.op("v", lambda e, b2=b2: e.tensor_scalar(den, b2[:16, 0:WA], 1e-3, None, OP.max))
            hv = p.op("v", lambda e: e.reciprocal(den, den))
            bk.consumed(p, 2, hv)
            p.wait("t", hv)
            for c in range(NCH):
                b3 = bk.acquire(p, 3)
                hv = p.top(lambda e, c=c, b3=b3: e.matmul(
                    b3[:, 0:WA], cS16T.ap()[:16, c * 128:(c + 1) * 128], den,
                    start=True, stop=True))
                bk.produced(p, 3, hv)
                qq = qkvA.ap()[:, c * WA:(c + 1) * WA]
                kv = kvA.ap()[:, c * WA:(c + 1) * WA]
                yy = lnA.ap()[:, c * WA:(c + 1) * WA]
                p.op("v", lambda e, qq=qq, kv=kv, yy=yy: e.tensor_tensor(yy, qq, kv, OP.mult))
                bk.consume(p, 3, "v")
                hv = p.op("v", lambda e, yy=yy, b3=b3: e.tensor_tensor(yy, yy, b3[:, 0:WA],
                                                                       OP.mult))
                bk.consumed(p, 3, hv)
            hv_y = p.sem["v"].cur()
            p.wait("t", hv_y)
            if s == 0 and DBG == 2:
                dump_dbg([(0, lnA.ap()[:, 0:2048]), (1, qkvA.ap()[:, 0:2048]),
                          (2, qkvA.ap()[:, 8 * WA:8 * WA + 2048]),
                          (3, kcA.ap()[:, 0:2048]), (4, kvA.ap()[:, 0:2048]),
                          (5, stripA.ap()[:, 0:1024])])
            for m in range(8):
                wb = wstrA.ap()[:, (m % 2) * 1024:(m % 2) * 1024 + 1024]
                if wfree[m % 2] is not None:
                    p.wait("y", wfree[m % 2])
                hvw = p.dma("y", wb.rearrange("a (k b) -> a k b", b=128),
                            scr["proj"][m, :, :, :].rearrange("k a b -> a k b"),
                            sem=sem_wsA[m % 2])
                p.wait("t", hvw)
                bi = 4 + (m % 2)
                b = bk.acquire(p, bi)
                hv = None
                for k in range(8):
                    hv = p.top(lambda e, b=b, wb=wb, k=k:
                               e.matmul(b[:, 0:WA], wb[:, k * 128:(k + 1) * 128],
                                        lnA.ap()[:, k * WA:(k + 1) * WA],
                                        start=(k == 0), stop=(k == 7)))
                wfree[m % 2] = hv
                bk.produced(p, bi, hv)
                bk.consume(p, bi, "v")
                hv = p.op("v", lambda e, m=m, b=b: e.tensor_scalar(
                    smallA.ap()[:, 0:WA], b[:, 0:WA], bias_proj.ap()[:, m:m + 1], None, OP.add))
                bk.consumed(p, bi, hv)
                p.op("v", lambda e, m=m, t0=t0: e.tensor_tensor(
                    xc(m, t0, WA), xc(m, t0, WA), smallA.ap()[:, 0:WA], OP.add))

    if stage >= 1:
        emit_attn_a()

    def ln_stats_ffn(t0, sm):
        xt = fstats.ap()[:, 0:1024]
        for half in range(2):
            tiles = [xc(half * 4 + c, t0, 128) for c in range(4)]
            def evac(b, half=half):
                hv2 = None
                for q in range(4):
                    c = half * 4 + q
                    hv2 = p.op("s", lambda e, c=c, q=q, b=b:
                               e.copy(xt[:, c * 128:(c + 1) * 128], b[:, q * 128:(q + 1) * 128]))
                return hv2
            transpose_group(tiles, 6 + half, evac, evac_engine="s")
        p.wait("v", p.sem["s"].cur())
        p.op("v", lambda e: e.tensor_reduce(sm[:, 0:1], xt[:, 0:D], AX.X, OP.add))
        p.op("v", lambda e: e.tensor_scalar(sm[:, 0:1], sm[:, 0:1], 1.0 / D, None, OP.mult))
        p.op("v", lambda e: e.tensor_scalar(xt[:, 0:D], xt[:, 0:D], sm[:, 0:1], None,
                                            OP.subtract))
        hvr = p.op("v", lambda e: e.tensor_reduce(sm[:, 5:6], xt[:, 0:D], AX.X, OP.max,
                                            apply_absolute_value=True))
        p.wait("s", hvr)
        hv = p.op("s", lambda e: e.activation(xt[:, 0:D], xt[:, 0:D], AF.Square))
        p.wait("v", hv)
        p.op("v", lambda e: e.tensor_reduce(sm[:, 1:2], xt[:, 0:D], AX.X, OP.add))
        p.op("v", lambda e: e.tensor_scalar(sm[:, 1:2], sm[:, 1:2], 1.0 / D, None, OP.mult))
        emit_rsqrt(sm[:, 1:2], sm[:, 1:2], LN_EPS, sm[:, 2:3], sm[:, 3:4], sm[:, 4:5])

    bcast_gate = [None]

    def bcast_col(src_col_ap, dst_ap, sm, bank=6):
        if bcast_gate[0] is not None:
            p.wait("v", bcast_gate[0])
        hv = p.op("v", lambda e: e.tensor_scalar(sm[:, 128:256], cONES.ap()[:, :],
                                                 src_col_ap, None, OP.mult))
        p.wait("t", hv)
        bb = bk.acquire(p, bank)
        hv = p.top(lambda e, bb=bb: e.transpose(bb[:, 0:128], sm[:, 128:256], cIDT.ap()[:, :]))
        bcast_gate[0] = hv
        bk.produced(p, bank, hv)
        bk.consume(p, bank, "s")
        hv = p.op("s", lambda e, bb=bb, dst_ap=dst_ap: e.copy(dst_ap, bb[:, 0:128]))
        bk.consumed(p, bank, hv)
        return hv

    def emit_ffn():
        sem_t1 = p.new_sem()
        sem_t2 = p.new_sem()
        p.wait("y", p.sem["t"].cur())
        p.wait("y", p.sem["v"].cur())
        p.wait("y", p.sem["s"].cur())
        p.wait("g", p.sem["t"].cur())
        hv_t1 = p.dma("y", t1T.ap()[:, :].rearrange("a (m w) -> a m w", w=1024),
                      scr["t1"].rearrange("m a w -> a m w"), sem=sem_t1)
        p.wait("t", hv_t1)
        h3 = hbuf.ap()[:, :].rearrange("a (c w) -> a c w", w=131)
        m23 = m2b.ap()[:, :].rearrange("a (c w) -> a c w", w=256)
        p.op("v", lambda e: e.memset(hsave.ap()[:, :], 0.0))
        p.op("v", lambda e, h3=h3: e.memset(h3[:, :, 0:2], 0.0))
        p.op("v", lambda e, h3=h3: e.memset(h3[:, :, 130:131], 0.0))
        sm = fsm.ap()
        burst = {"tb0": -1, "bc": 0, "last_mm": None}

        def mm2_flush():
            if burst["bc"] == 0:
                return
            tb0 = burst["tb0"]
            sc = 1 if tb0 < 0 else 0
            ncol = burst["bc"] - sc
            out0 = tb0 + sc
            p.wait("t", p.sem["v"].cur())
            bs = [bk.acquire(p, i) for i in range(8)]
            hv = None
            t2gate = None
            for k in range(32):
                if t2gate is not None:
                    p.wait("g", t2gate)
                hvw = p.dma("g", t2buf.ap()[:, :].rearrange("a (m b) -> a m b", b=128),
                            scr["t2"][k, :, :, :].rearrange("m a b -> a m b"), sem=sem_t2)
                p.wait("t", hvw)
                for m in range(8):
                    b = bs[m]
                    hv = p.top(lambda e, b=b, m=m, k=k, ncol=ncol, sc=sc:
                               e.matmul(b[:, 0:ncol],
                                        t2buf.ap()[:, m * 128:(m + 1) * 128],
                                        m2b.ap()[:, k * 256 + sc:k * 256 + sc + ncol],
                                        start=(k == 0), stop=(k == 31)))
                t2gate = hv
            for i in range(8):
                bk.produced(p, i, hv)
            for m in range(8):
                b = bs[m]
                bk.consume(p, m, "v")
                p.op("v", lambda e, b=b, ncol=ncol, sc=sc:
                     e.tensor_tensor(fval.ap()[:, 0:ncol], b[:, 0:ncol],
                                     fsc.ap()[:, 512 + sc:512 + sc + ncol], OP.mult))
                hv2 = p.op("v", lambda e, m=m, out0=out0, ncol=ncol:
                           e.tensor_tensor(xc(m, out0, ncol), xc(m, out0, ncol),
                                           fval.ap()[:, 0:ncol], OP.add))
                bk.consumed(p, m, hv2)
            burst["last_mm"] = hv
            burst["bc"] = 0
            burst["tb0"] = out0 + ncol

        for s in range(T // WF):
            t0 = s * WF
            last_strip = (s == T // WF - 1)
            n = 129 if last_strip else 128
            if s > 0:
                p.op("v", lambda e, h3=h3: e.tensor_copy(
                    h3[:, :, 0:1], hsave.ap()[:, :].rearrange("a (c o) -> a c o", o=1)))
                p.op("v", lambda e, h3=h3: e.tensor_copy(h3[:, :, 1:2], h3[:, :, 129:130]))
            ln_stats_ffn(t0, sm)
            p.op("v", lambda e: e.tensor_tensor(sm[:, 6:7], sm[:, 5:6], sm[:, 1:2], OP.mult))
            p.op("v", lambda e: e.tensor_scalar(sm[:, 6:7], sm[:, 6:7], 1e-5, None, OP.max))
            p.op("v", lambda e: e.reciprocal(sm[:, 7:8], sm[:, 6:7]))
            p.op("v", lambda e: e.tensor_scalar(sm[:, 7:8], sm[:, 7:8], 127.0, None, OP.mult))
            p.op("v", lambda e: e.tensor_tensor(sm[:, 8:9], sm[:, 1:2], sm[:, 7:8], OP.mult))
            p.op("v", lambda e: e.tensor_scalar(sm[:, 9:10], sm[:, 6:7], 1.0 / 127.0,
                                                None, OP.mult))
            p.op("v", lambda e: e.tensor_tensor(sm[:, 9:10], sm[:, 9:10], cw1t.ap()[:, 0:1],
                                                OP.mult))
            bcast_col(sm[:, 0:1], fsc.ap()[:, 0:128], sm)
            bcast_col(sm[:, 8:9], fsc.ap()[:, 128:256], sm)
            bcast_col(sm[:, 9:10], fsc.ap()[:, 256:384], sm)
            p.wait("v", p.sem["s"].cur())
            if burst["last_mm"] is not None:
                p.wait("v", burst["last_mm"])
                p.wait("s", burst["last_mm"])
            if burst.get("last_mm1") is not None:
                p.wait("v", burst["last_mm1"])
                p.wait("s", burst["last_mm1"])
            for c in range(NCH):
                dst = fval.ap()[:, (c % 4) * 128:(c % 4) * 128 + 128]
                p.op("v", lambda e, c=c, dst=dst, t0=t0: e.tensor_tensor(
                    dst, xc(c, t0, WF), fsc.ap()[:, 0:128], OP.subtract))
                p.op("v", lambda e, dst=dst: e.tensor_tensor(
                    dst, dst, fsc.ap()[:, 128:256], OP.mult))
                if c % 4 == 3:
                    emit_round(fval.ap()[:, :], fint.ap()[:, :], ff.ap()[:, :],
                               fd.ap()[:, :],
                               m1b.ap()[:, (c - 3) * 128:(c + 1) * 128])
            hv_m1 = p.sem["v"].cur()
            p.wait("t", hv_m1)
            for m in range(32):
                bi = 4 + (m % 2)
                b = bk.acquire(p, bi)
                hv = None
                for k in range(8):
                    hv = p.top(lambda e, b=b, m=m, k=k: e.matmul(
                        b[:, 0:WF], t1T.ap()[:, m * 1024 + k * 128: m * 1024 + (k + 1) * 128],
                        m1b.ap()[:, k * WF:(k + 1) * WF], start=(k == 0), stop=(k == 7)))
                burst["last_mm1"] = hv
                bk.produced(p, bi, hv)
                bk.consume(p, bi, "v")
                hv = p.op("v", lambda e, b=b, m=m, h3=h3: e.tensor_tensor(
                    h3[:, m, 2:130], b[:, 0:WF], fsc.ap()[:, 256:384], OP.mult))
                bk.consumed(p, bi, hv)
            p.op("v", lambda e, h3=h3: e.tensor_copy(
                hsave.ap()[:, :].rearrange("a (c o) -> a c o", o=1), h3[:, :, 128:129]))
            hcv = hc8.ap()[:, :].rearrange("a (c w) -> a c w", w=132)
            for grp in range(4):
                for q in range(8):
                    m = grp * 8 + q
                    hvt1 = p.op("v", lambda e, m=m, q=q, n=n, h3=h3, hcv=hcv: e.tensor_scalar(
                        hcv[:, q, 0:n], h3[:, m, 1:1 + n], fcw.ap()[:, m * 3 + 1:m * 3 + 2],
                        fcb.ap()[:, m:m + 1], OP.mult, OP.add))
                    p.op("v", lambda e, m=m, q=q, n=n, h3=h3, hcv=hcv:
                         e.scalar_tensor_tensor(
                        hcv[:, q, 0:n], h3[:, m, 0:n], fcw.ap()[:, m * 3:m * 3 + 1],
                        hcv[:, q, 0:n], OP.mult, OP.add))
                    p.op("v", lambda e, m=m, q=q, n=n, h3=h3, hcv=hcv:
                         e.scalar_tensor_tensor(
                        hcv[:, q, 0:n], h3[:, m, 2:2 + n], fcw.ap()[:, m * 3 + 2:m * 3 + 3],
                        hcv[:, q, 0:n], OP.mult, OP.add))
                hvv = p.sem["v"].cur()
                p.wait("s", hvv)
                hv = p.op("s", lambda e, grp=grp, n=n, h3=h3, hcv=hcv: e.activation(
                    h3[:, grp * 8:(grp + 1) * 8, 1:1 + n], hcv[:, 0:8, 0:n], AF.Silu))
                p.wait("v", hv)
                p.wait("g", hv)
            amx = hcv
            hperm = hbuf.ap()[:, :].rearrange("a (c w) -> a w c", w=131)
            p.op("v", lambda e, hperm=hperm, amx=amx, n=n: e.tensor_reduce(
                amx[:, 0, 0:n], hperm[:, 1:1 + n, :], AX.X, OP.max,
                apply_absolute_value=True))
            hv = p.sem["v"].cur()
            p.wait("t", hv)
            bb = bk.acquire(p, 6)
            hv = p.top(lambda e, bb=bb, amx=amx: e.transpose(bb[:, 0:128], amx[:, 0, 0:128],
                                                             cIDT.ap()[:, :]))
            bk.produced(p, 6, hv)
            bk.consume(p, 6, "v")
            hv2 = p.op("v", lambda e, bb=bb: e.tensor_reduce(
                sm[:, 20:21], bb[:, 0:128], AX.X, OP.max, apply_absolute_value=True))
            bk.consumed(p, 6, hv2)
            if last_strip:
                bb = bk.acquire(p, 6)
                hv = p.top(lambda e, bb=bb, amx=amx: e.transpose(
                    bb[:1, 0:128], amx[:, 0:1, 128:129], cIDT.ap()[:, :]))
                bk.produced(p, 6, hv)
                bk.consume(p, 6, "v")
                hv2 = p.op("v", lambda e, bb=bb: e.tensor_reduce(
                    sm[:1, 24:25], bb[:1, 0:128], AX.X, OP.max, apply_absolute_value=True))
                bk.consumed(p, 6, hv2)
                p.op("v", lambda e: e.tensor_scalar(sm[:1, 24:25], sm[:1, 24:25], 1e-5,
                                                    None, OP.max))
                p.op("v", lambda e: e.reciprocal(sm[:1, 25:26], sm[:1, 24:25]))
                p.op("v", lambda e: e.tensor_scalar(sm[:1, 25:26], sm[:1, 25:26], 127.0,
                                                    None, OP.mult))
                p.op("v", lambda e: e.tensor_scalar(sm[:1, 26:27], sm[:1, 24:25],
                                                    1.0 / 127.0, None, OP.mult))
                p.op("v", lambda e: e.tensor_tensor(sm[:1, 26:27], sm[:1, 26:27],
                                                    cw2t.ap()[:1, 0:1], OP.mult))
            p.op("v", lambda e: e.tensor_scalar(sm[:, 20:21], sm[:, 20:21], 1e-5, None, OP.max))
            p.op("v", lambda e: e.reciprocal(sm[:, 21:22], sm[:, 20:21]))
            p.op("v", lambda e: e.tensor_scalar(sm[:, 21:22], sm[:, 21:22], 127.0, None,
                                                OP.mult))
            p.op("v", lambda e: e.tensor_scalar(sm[:, 22:23], sm[:, 20:21], 1.0 / 127.0,
                                                None, OP.mult))
            p.op("v", lambda e: e.tensor_tensor(sm[:, 22:23], sm[:, 22:23], cw2t.ap()[:, 0:1],
                                                OP.mult))
            bcast_col(sm[:, 21:22], fsc.ap()[:, 384:512], sm)
            bc = burst["bc"]
            bcast_col(sm[:, 22:23], fsc.ap()[:, 512 + bc:512 + bc + 128], sm)
            if last_strip:
                p.wait("t", p.sem["v"].cur())
                bb = bk.acquire(p, 6)
                p.top(lambda e, bb=bb: e.matmul(bb[:, 0:1], cONES.ap()[:1, :], sm[:1, 25:26],
                                                start=True, stop=True))
                hv = p.top(lambda e, bb=bb: e.matmul(bb[:, 1:2], cONES.ap()[:1, :],
                                                     sm[:1, 26:27], start=True, stop=True,
                                                     skip_group_check=True))
                bk.produced(p, 6, hv)
                bk.consume(p, 6, "s")
                p.op("s", lambda e, bb=bb: e.copy(fsc.ap()[:, 1000:1001], bb[:, 0:1]))
                hv = p.op("s", lambda e, bb=bb, bc=bc: e.copy(
                    fsc.ap()[:, 512 + bc + 128:512 + bc + 129], bb[:, 1:2]))
                bk.consumed(p, 6, hv)
            p.wait("v", p.sem["s"].cur())
            for grp in range(8):
                for q in range(4):
                    c = grp * 4 + q
                    p.op("v", lambda e, c=c, q=q, h3=h3: e.tensor_tensor(
                        fval.ap()[:, q * 128:(q + 1) * 128], h3[:, c, 1:129],
                        fsc.ap()[:, 384:512], OP.mult))
                emit_round(fval.ap()[:, :], fint.ap()[:, :], ff.ap()[:, :], fd.ap()[:, :],
                           m23[:, grp * 4:(grp + 1) * 4, bc:bc + 128])
            if last_strip:
                for grp in range(8):
                    for q in range(4):
                        c = grp * 4 + q
                        p.op("v", lambda e, c=c, q=q, grp=grp, h3=h3: e.tensor_tensor(
                            fval.ap()[:, grp * 4 + q:grp * 4 + q + 1], h3[:, c, 129:130],
                            fsc.ap()[:, 1000:1001], OP.mult))
                emit_round(fval.ap()[:, 0:32], fint.ap()[:, 0:32], ff.ap()[:, 0:32],
                           fd.ap()[:, 0:32],
                           m23[:, 0:32, bc + 128:bc + 129].rearrange("a c o -> a (c o)"))
            if s == 15 and DBG == 3:
                dump_dbg([(0, hbuf.ap()[:, 0:2048]), (1, fsc.ap()[:, 0:512])])
            burst["bc"] = bc + n
            if (s % 2 == 1 and s < 14) or s >= 14:
                mm2_flush()

    def emit_attn_b():
        sem_wB = [p.new_sem(), p.new_sem()]
        p.wait("y", p.sem["t"].cur())
        p.wait("y", p.sem["v"].cur())
        p.wait("y", p.sem["s"].cur())
        p.wait("g", p.sem["t"].cur())
        xbar(0)
        p.op("v", lambda e: e.memset(dW.ap()[:, :], 0.0))
        p.op("v", lambda e: e.memset(dWavg.ap()[:, :], 0.0))
        for raw in (bk0, bq0, bv0):
            p.op("v", lambda e, raw=raw: e.memset(
                raw.ap()[:, :].rearrange("a (c w) -> a c w", w=264)[:, :, 0:3], 0.0))
        wfree = [None, None]

        def stream_mm(dscr_w, rhs_fn, out_fn, nbank0=0):
            for m in range(8):
                wb = wstrB.ap()[:, (m % 2) * 1024:(m % 2) * 1024 + 1024]
                if wfree[m % 2] is not None:
                    p.wait("y", wfree[m % 2])
                hvw = p.dma("y", wb.rearrange("a (k b) -> a k b", b=128),
                            dscr_w[m, :, :, :].rearrange("k a b -> a k b"),
                            sem=sem_wB[m % 2])
                p.wait("t", hvw)
                bi = nbank0 + (m % 2)
                b = bk.acquire(p, bi)
                hv = None
                for k in range(8):
                    hv = p.top(lambda e, b=b, wb=wb, k=k, m=m:
                               e.matmul(b[:, 0:WB], wb[:, k * 128:(k + 1) * 128],
                                        rhs_fn(k), start=(k == 0), stop=(k == 7)))
                wfree[m % 2] = hv
                bk.produced(p, bi, hv)
                bk.consume(p, bi, "v")
                hv2 = out_fn(m, b)
                bk.consumed(p, bi, hv2)

        def conv4(raw_t, dst_ap_fn, cw_t):
            r3 = raw_t.ap()[:, :].rearrange("a (c w) -> a c w", w=264)
            for c in range(NCH):
                dst = dst_ap_fn(c)
                p.op("v", lambda e, c=c, dst=dst, r3=r3, cw_t=cw_t: e.tensor_scalar(
                    dst, r3[:, c, 3:259], cw_t.ap()[:, c * 4 + 3:c * 4 + 4], None, OP.mult))
                for j, tap in ((2, 2), (1, 1), (0, 0)):
                    p.op("v", lambda e, c=c, dst=dst, r3=r3, cw_t=cw_t, j=j:
                         e.scalar_tensor_tensor(
                             dst, r3[:, c, j:j + 256], cw_t.ap()[:, c * 4 + j:c * 4 + j + 1],
                             dst, OP.mult, OP.add))

        def headnorm(src_t, strip_off, is_rms):
            b2 = bk.acquire(p, 2)
            prev_mm = None
            for c in range(NCH):
                sl = src_t.ap()[:, c * WB:(c + 1) * WB]
                if prev_mm is not None:
                    p.wait("v", prev_mm)
                hv = p.op("v", lambda e, sl=sl: e.tensor_tensor(
                    smallB.ap()[:, 0:WB], sl, sl, OP.mult))
                p.wait("t", hv)
                prev_mm = hv = p.top(lambda e, c=c, b2=b2: e.matmul(
                    b2[:16, 0:WB], cS16.ap()[:, c * 16:(c + 1) * 16], smallB.ap()[:, 0:WB],
                    start=(c == 0), stop=(c == 7)))
            bk.produced(p, 2, hv)
            bk.consume(p, 2, "v")
            st_ = stripB.ap()[:16, strip_off:strip_off + WB]
            sc1 = smallB.ap()[:16, 256:257 + WB]
            if is_rms:
                hv = p.op("v", lambda e, b2=b2, st_=st_: e.tensor_scalar(
                    st_, b2[:16, 0:WB], 1.0 / HD, None, OP.mult))
                bk.consumed(p, 2, hv)
                emit_rsqrt(st_, st_, 1e-6, sc1[:, 0:WB], smallB.ap()[:16, 512:512 + WB],
                           smallB.ap()[:16, 768:768 + WB])
            else:
                hv = p.op("v", lambda e, b2=b2, st_=st_: e.tensor_copy(st_, b2[:16, 0:WB]))
                bk.consumed(p, 2, hv)
                emit_sqrt(st_, st_, sc1[:, 0:WB], smallB.ap()[:16, 512:512 + WB],
                          smallB.ap()[:16, 768:768 + WB])
                p.op("v", lambda e, st_=st_: e.tensor_scalar(st_, st_, 1e-6, None, OP.add))
                p.op("v", lambda e, st_=st_: e.reciprocal(st_, st_))
            hvr = p.sem["v"].cur()
            p.wait("t", hvr)
            for c in range(NCH):
                b3 = bk.acquire(p, 3)
                hv = p.top(lambda e, c=c, b3=b3, st_=st_: e.matmul(
                    b3[:, 0:WB], cS16T.ap()[:16, c * 128:(c + 1) * 128], st_,
                    start=True, stop=True))
                bk.produced(p, 3, hv)
                bk.consume(p, 3, "v")
                sl = src_t.ap()[:, c * WB:(c + 1) * WB]
                hv2 = p.op("v", lambda e, sl=sl, b3=b3: e.tensor_tensor(
                    sl, sl, b3[:, 0:WB], OP.mult))
                if is_rms:
                    hv2 = p.op("v", lambda e, sl=sl: e.tensor_scalar(
                        sl, sl, rmsw.ap()[:, 0:1], None, OP.mult))
                bk.consumed(p, 3, hv2)

        def delta_chunk(gi, ii, g):
            kf = bkh.ap()[:, g * WB + ii * 64: g * WB + ii * 64 + 64]
            qf = bfm.ap()[:, g * WB + ii * 64: g * WB + ii * 64 + 64]
            v3f = nbfm.ap()[:, g * WB + ii * 64: g * WB + ii * 64 + 64]
            bf = betab.ap()[:, g * WB + ii * 64: g * WB + ii * 64 + 64]
            nbf = bv0.ap()[:, :].rearrange("a (c w) -> a c w", w=264)[:, g,
                                                                     3 + ii * 64:3 + ii * 64 + 64]
            Wsl = dW.ap()[:, g * 64:(g + 1) * 64]
            Wasl = dWavg.ap()[:, g * 64:(g + 1) * 64]
            osl = lnB.ap()[:, g * WB + ii * 64: g * WB + ii * 64 + 64]
            tmt = cTMT.ap()[:64, gi * 64:(gi + 1) * 64]
            ab_ = cAB.ap()[:, gi * 64:(gi + 1) * 64]
            wcol = cWCOL.ap()[:64, gi:gi + 1]
            c1c = cC1.ap()[:, gi:gi + 1]
            c2c = cC2.ap()[:, gi:gi + 1]
            ID = cIDT.ap()

            def pairlow(bia, bib, lhsT_t, rhs_t):
                ba = bk.acquire(p, bia)
                hva = p.top(lambda e, ba=ba, lhsT_t=lhsT_t, rhs_t=rhs_t: e.matmul(
                    ba[0:64, 0:64], lhsT_t[0:64, :], rhs_t[0:64, :],
                    start=True, stop=True, tile_position=(0, 0)))
                bk.produced(p, bia, hva)
                bb2 = bk.acquire(p, bib)
                hvb = p.top(lambda e, bb2=bb2, lhsT_t=lhsT_t, rhs_t=rhs_t: e.matmul(
                    bb2[0:64, 0:64], lhsT_t[64:128, :], rhs_t[64:128, :],
                    start=True, stop=True, tile_position=(64, 0)))
                bk.produced(p, bib, hvb)
                return ba, bb2

            ba, bb2 = pairlow(0, 1, bf, kf)
            bk.consume(p, 0, "v")
            hv = p.op("v", lambda e, ba=ba: e.tensor_tensor(
                pwh("N1", 0), ba[0:64, 0:64], cNMASK.ap()[:64, 0:64], OP.mult), drain=False)
            bk.consumed(p, 0, hv)
            bk.consume(p, 1, "v")
            hv = p.op("v", lambda e, bb2=bb2: e.tensor_tensor(
                pwh("N1", 1), bb2[0:64, 0:64], cNMASK.ap()[:64, 64:128], OP.mult), drain=False)
            bk.consumed(p, 1, hv)
            ba, bb2 = pairlow(2, 3, kf, bf)
            bk.consume(p, 2, "v")
            hv = p.op("v", lambda e, ba=ba: e.tensor_tensor(
                pwh("NS1", 0), ba[0:64, 0:64], cNMASKT.ap()[:64, 0:64], OP.mult), drain=False)
            bk.consumed(p, 2, hv)
            bk.consume(p, 3, "v")
            hv = p.op("v", lambda e, bb2=bb2: e.tensor_tensor(
                pwh("NS1", 1), bb2[0:64, 0:64], cNMASKT.ap()[:64, 64:128], OP.mult), drain=False)
            bk.consumed(p, 3, hv)
            if DCUT is not None and DCUT <= 1: return
            b5 = bk.acquire(p, 5)
            hv = p.top(lambda e, b5=b5, kf=kf: e.transpose(b5[0:64, 0:128], kf, ID[:, :]))
            bk.produced(p, 5, hv)
            bk.consume(p, 5, "v")
            hv = p.op("v", lambda e, b5=b5: e.tensor_copy(dKts.ap()[:64, :], b5[0:64, 0:128]), drain=False)
            bk.consumed(p, 5, hv)
            if DCUT is not None and DCUT <= 2: return
            seq = [("N2", "NS1", "N1"), ("NS2", "N1", "NS1"), ("N4", "NS2", "N2"),
                   ("NS4", "N2", "NS2"), ("N8", "NS4", "N4"), ("NS8", "N4", "NS4"),
                   ("N16", "NS8", "N8"), ("NS16", "N8", "NS8"), ("N32", "NS16", "N16")]
            for idx, (dst, la, lb) in enumerate(seq):
                bi = 2 + (idx % 2)
                b = bk.acquire(p, bi)
                wv = p.sem["v"].cur()
                p.wait("t", wv)
                for h in range(2):
                    hv = p.top(lambda e, b=b, la=la, lb=lb, h=h: e.matmul(
                        b[0:64, h * 64:h * 64 + 64], pw(la)[:, h * 64:h * 64 + 64],
                        pw(lb)[:, h * 64:h * 64 + 64],
                        start=(h == 0), stop=True, tile_position=(0, 0),
                        skip_group_check=(h == 1)))
                bk.produced(p, bi, hv)
                bk.consume(p, bi, "v")
                hv = p.op("v", lambda e, b=b, dst=dst: e.tensor_copy(pw(dst), b[0:64, 0:128]), drain=False)
                bk.consumed(p, bi, hv)
            if DCUT is not None and DCUT <= 3: return
            p.op("v", lambda e: e.tensor_tensor(dP.ap()[:64, :], cI64S.ap()[:64, :],
                                                pw("NS1"), OP.add), drain=False)
            for k in (1, 2, 3, 4, 5):
                nm = f"N{2 ** k}"
                p.op("v", lambda e, nm=nm: e.tensor_tensor(pw("PI"), cI64S.ap()[:64, :],
                                                           pw(nm), OP.add), drain=False)
                b4 = bk.acquire(p, 4)
                wv = p.sem["v"].cur()
                p.wait("t", wv)
                for h in range(2):
                    hv = p.top(lambda e, b4=b4, h=h: e.matmul(
                        b4[0:64, h * 64:h * 64 + 64], pwh("PI", h),
                        dP.ap()[:64, h * 64:h * 64 + 64],
                        start=True, stop=True, tile_position=(0, 0),
                        skip_group_check=(h == 1)))
                bk.produced(p, 4, hv)
                bk.consume(p, 4, "v")
                hv = p.op("v", lambda e, b4=b4: e.tensor_copy(dP.ap()[:64, :], b4[0:64, 0:128]), drain=False)
                bk.consumed(p, 4, hv)
            if DCUT is not None and DCUT <= 4: return
            b0 = bk.acquire(p, 0)
            hv = p.top(lambda e, b0=b0, v3f=v3f: e.transpose(b0[0:64, 0:128], v3f, ID[:, :]))
            bk.produced(p, 0, hv)
            wv = p.sem["v"].cur()
            p.wait("t", wv)
            b1 = bk.acquire(p, 1)
            hv = p.top(lambda e, b1=b1, nbf=nbf, Wsl=Wsl: e.matmul(
                b1[0:64, 0:64], nbf[0:64, :], Wsl[0:64, :],
                start=True, stop=True, tile_position=(0, 0)))
            bk.produced(p, 1, hv)
            b5g = bk.acquire(p, 5)
            hv = p.top(lambda e, b5g=b5g, nbf=nbf, Wsl=Wsl: e.matmul(
                b5g[0:64, 0:64], nbf[64:128, :], Wsl[64:128, :],
                start=True, stop=True, tile_position=(64, 0)))
            bk.produced(p, 5, hv)
            bk.consume(p, 0, "v")
            hv = p.op("v", lambda e, b0=b0: e.tensor_copy(dG.ap()[:64, :], b0[0:64, 0:128]))
            bk.consumed(p, 0, hv)
            bk.consume(p, 1, "v")
            hv = p.op("v", lambda e, b1=b1: e.tensor_tensor(
                dG.ap()[:64, 0:64], dG.ap()[:64, 0:64], b1[0:64, 0:64], OP.add), drain=False)
            bk.consumed(p, 1, hv)
            bk.consume(p, 5, "v")
            hv = p.op("v", lambda e, b5g=b5g: e.tensor_tensor(
                dG.ap()[:64, 64:128], dG.ap()[:64, 64:128], b5g[0:64, 0:64], OP.add), drain=False)
            bk.consumed(p, 5, hv)
            if DCUT is not None and DCUT <= 5: return
            b1 = bk.acquire(p, 1)
            wv = p.sem["v"].cur()
            p.wait("t", wv)
            for h in range(2):
                hv = p.top(lambda e, b1=b1, h=h: e.matmul(
                    b1[0:64, h * 64:h * 64 + 64], dP.ap()[:64, h * 64:h * 64 + 64],
                    dG.ap()[:64, h * 64:h * 64 + 64],
                    start=(h == 0), stop=True, tile_position=(0, 0),
                    skip_group_check=(h == 1)))
            bk.produced(p, 1, hv)
            bk.consume(p, 1, "v")
            p.op("v", lambda e, b1=b1: e.tensor_copy(dU.ap()[:64, :], b1[0:64, 0:128]), drain=False)
            hv = p.op("v", lambda e, b1=b1, wcol=wcol: e.tensor_scalar(
                dUw.ap()[:64, :], b1[0:64, 0:128], wcol, None, OP.mult), drain=False)
            bk.consumed(p, 1, hv)
            if DCUT is not None and DCUT <= 6: return
            b2 = bk.acquire(p, 2)
            p.top(lambda e, b2=b2, Wasl=Wasl, qf=qf: e.matmul(
                b2[0:64, 0:64], Wasl[0:64, :], qf[0:64, :],
                start=True, stop=True, tile_position=(0, 0)))
            hv = p.top(lambda e, b2=b2, Wasl=Wasl, qf=qf: e.matmul(
                b2[64:128, 0:64], Wasl[64:128, :], qf[64:128, :],
                start=True, stop=True, tile_position=(64, 64), skip_group_check=True))
            bk.produced(p, 2, hv)
            b3 = bk.acquire(p, 3)
            p.top(lambda e, b3=b3, Wsl=Wsl, qf=qf: e.matmul(
                b3[0:64, 0:64], Wsl[0:64, :], qf[0:64, :],
                start=True, stop=True, tile_position=(0, 0)))
            hv = p.top(lambda e, b3=b3, Wsl=Wsl, qf=qf: e.matmul(
                b3[64:128, 0:64], Wsl[64:128, :], qf[64:128, :],
                start=True, stop=True, tile_position=(64, 64), skip_group_check=True))
            bk.produced(p, 3, hv)
            b4 = bk.acquire(p, 4)
            hv = p.top(lambda e, b4=b4, kf=kf, qf=qf: e.matmul(
                b4[0:64, 0:64], kf[0:64, :], qf[0:64, :],
                start=True, stop=True, tile_position=(0, 0)))
            bk.produced(p, 4, hv)
            b0 = bk.acquire(p, 0)
            hv = p.top(lambda e, b0=b0, kf=kf, qf=qf: e.matmul(
                b0[0:64, 0:64], kf[64:128, :], qf[64:128, :],
                start=True, stop=True, tile_position=(64, 0)))
            bk.produced(p, 0, hv)
            bk.consume(p, 4, "v")
            hv = p.op("v", lambda e, b4=b4, tmt=tmt: e.tensor_tensor(
                dT1.ap()[:64, 0:64], b4[0:64, 0:64], tmt, OP.mult), drain=False)
            bk.consumed(p, 4, hv)
            bk.consume(p, 0, "v")
            hv = p.op("v", lambda e, b0=b0, tmt=tmt: e.tensor_tensor(
                dT1.ap()[:64, 64:128], b0[0:64, 0:64], tmt, OP.mult), drain=False)
            bk.consumed(p, 0, hv)
            if DCUT is not None and DCUT <= 7: return
            b5o = bk.acquire(p, 5)
            wv = p.sem["v"].cur()
            p.wait("t", wv)
            p.top(lambda e, b5o=b5o: e.matmul(
                b5o[0:64, 0:64], dU.ap()[:64, 0:64], dT1.ap()[:64, 0:64],
                start=True, stop=True, tile_position=(0, 0)))
            hv = p.top(lambda e, b5o=b5o: e.matmul(
                b5o[64:128, 0:64], dU.ap()[:64, 64:128], dT1.ap()[:64, 64:128],
                start=True, stop=True, tile_position=(0, 64), skip_group_check=True))
            bk.produced(p, 5, hv)
            bk.consume(p, 3, "v")
            p.op("v", lambda e, b3=b3: e.tensor_copy(dT2.ap()[:, :], b3[:, 0:64]))
            bk.consume(p, 2, "v")
            hv = p.op("v", lambda e, b2=b2: e.scalar_tensor_tensor(
                dCmb.ap()[:, :], dT2.ap()[:, :], -1.0, b2[:, 0:64], OP.mult, OP.add))
            bk.consumed(p, 2, hv)
            p.op("v", lambda e, ab_=ab_: e.tensor_tensor(dCmb.ap()[:, :], dCmb.ap()[:, :],
                                                         ab_, OP.mult))
            hv = p.op("v", lambda e, b3=b3: e.tensor_tensor(
                dCmb.ap()[:, :], dCmb.ap()[:, :], b3[:, 0:64], OP.add))
            bk.consumed(p, 3, hv)
            bk.consume(p, 5, "v")
            hv = p.op("v", lambda e, b5o=b5o, osl=osl: e.tensor_tensor(
                osl, dCmb.ap()[:, :], b5o[:, 0:64], OP.add), drain=False)
            bk.consumed(p, 5, hv)
            if gi == 0 and g == 0:
                xbar(5)
            if gi == 0 and g == 0 and BAR == 6:
                p.wait("g", p.sem["v"].cur())
                p.wait("g", p.sem["s"].cur())
                semz = p.new_sem("bar6")
                hvz = p.dma("g", ddbg[0][:1, 0:1], dKts.ap()[:1, 0:1], slow=True, sem=semz)
                p.wait("v", hvz)
                p.wait("s", hvz)
                p.wait("t", hvz)
            if gi == 0 and g == 0 and BAR == 7:
                dump_dbg([(0, pw("N1")), (1, dP.ap()[:64, :]), (2, dG.ap()[:64, :]),
                          (3, dU.ap()[:64, :]), (4, osl), (5, dKts.ap()[:64, :])])
            if DBG == 4 and gi == DBGCH and g == 0:
                dump_dbg([(0, pw("N1")), (1, dP.ap()[:64, :]), (2, dG.ap()[:64, :]),
                          (3, dU.ap()[:64, :]), (4, osl), (5, dKts.ap()[:64, :])])
            if DBG == 6 and gi == DBGCH and g == 0:
                dump_dbg([(0, Wsl), (1, Wasl), (2, dU.ap()[:64, :]),
                          (3, dUw.ap()[:64, :]), (4, v3f), (5, bf)])
            if DCUT is not None and DCUT <= 8: return
            p.op("v", lambda e, Wsl=Wsl, c2c=c2c: e.tensor_scalar(
                dWmix.ap()[:, :], Wsl, c2c, None, OP.mult))
            p.op("v", lambda e, Wasl=Wasl, c1c=c1c: e.scalar_tensor_tensor(
                dWmix.ap()[:, :], Wasl, c1c, dWmix.ap()[:, :], OP.mult, OP.add))
            b1w = bk.acquire(p, 1)
            wv = p.sem["v"].cur()
            p.wait("t", wv)
            p.top(lambda e, b1w=b1w: e.matmul(
                b1w[0:64, 0:64], dKts.ap()[:64, 0:64], dU.ap()[:64, 0:64],
                start=True, stop=True, tile_position=(0, 0)))
            hv = p.top(lambda e, b1w=b1w: e.matmul(
                b1w[64:128, 0:64], dKts.ap()[:64, 64:128], dU.ap()[:64, 64:128],
                start=True, stop=True, tile_position=(0, 64), skip_group_check=True))
            bk.produced(p, 1, hv)
            b4w = bk.acquire(p, 4)
            p.top(lambda e, b4w=b4w: e.matmul(
                b4w[0:64, 0:64], dKts.ap()[:64, 0:64], dUw.ap()[:64, 0:64],
                start=True, stop=True, tile_position=(0, 0)))
            hv = p.top(lambda e, b4w=b4w: e.matmul(
                b4w[64:128, 0:64], dKts.ap()[:64, 64:128], dUw.ap()[:64, 64:128],
                start=True, stop=True, tile_position=(0, 64), skip_group_check=True))
            bk.produced(p, 4, hv)
            bk.consume(p, 1, "v")
            hv = p.op("v", lambda e, b1w=b1w, Wsl=Wsl: e.tensor_tensor(
                Wsl, Wsl, b1w[:, 0:64], OP.add), drain=False)
            bk.consumed(p, 1, hv)
            bk.consume(p, 4, "v")
            hv = p.op("v", lambda e, b4w=b4w, Wasl=Wasl: e.tensor_tensor(
                Wasl, dWmix.ap()[:, :], b4w[:, 0:64], OP.add), drain=False)
            bk.consumed(p, 4, hv)

        nstrip = T // WB if BSTRIPS is None else BSTRIPS
        for s in range(nstrip):
            t0 = s * WB
            p.wait("v", p.sem["t"].cur())
            p.wait("s", p.sem["t"].cur())
            ln_stats(t0, WB, mbB, sbB, smallB, stripB, bankbase=6)
            p.wait("v", p.sem["s"].cur())
            for c in range(NCH):
                ln = lnB.ap()[:, c * WB:(c + 1) * WB]
                p.op("v", lambda e, c=c, ln=ln, t0=t0: e.tensor_tensor(
                    ln, xc(c, t0, WB), mbB.ap()[:, :], OP.subtract))
                p.op("v", lambda e, ln=ln: e.tensor_tensor(ln, ln, sbB.ap()[:, :], OP.mult))
            hv_ln = p.sem["v"].cur()
            p.wait("t", hv_ln)
            if BCUT is not None and BCUT <= 1: continue
            r3k = bk0.ap()[:, :].rearrange("a (c w) -> a c w", w=264)
            r3q = bq0.ap()[:, :].rearrange("a (c w) -> a c w", w=264)
            r3v = bv0.ap()[:, :].rearrange("a (c w) -> a c w", w=264)
            lsv = leadsv.ap()[:, 0:24].rearrange("a (c w) -> a c w", w=3)
            if s > 0:
                for r3 in (r3k, r3q):
                    p.op("v", lambda e, r3=r3: e.tensor_copy(r3[:, :, 0:3], r3[:, :, 256:259]))
                p.op("v", lambda e, lsv=lsv, r3v=r3v: e.tensor_copy(r3v[:, :, 0:3], lsv[:, :, :]))
            hv_lead = p.sem["v"].cur()

            def mk_out(r3, bias_t):
                def out_fn(m, b, r3=r3, bias_t=bias_t):
                    return p.op("v", lambda e, m=m, b=b, r3=r3, bias_t=bias_t:
                                e.tensor_scalar(r3[:, m, 3:259], b[:, 0:WB],
                                                bias_t.ap()[:, m:m + 1], None, OP.add))
                return out_fn
            lnf = lambda k: lnB.ap()[:, k * WB:(k + 1) * WB]
            p.wait("t", hv_lead)
            stream_mm(scr["k"], lnf, mk_out(r3k, bias_k), nbank0=0)
            stream_mm(scr["q"], lnf, mk_out(r3q, bias_q), nbank0=0)
            stream_mm(scr["v"], lnf, mk_out(r3v, bias_v), nbank0=0)
            if s == 0:
                xbar(1)
            if BCUT is not None and BCUT <= 2: continue
            b4 = bk.acquire(p, 4)
            hv = None
            for k in range(8):
                hv = p.top(lambda e, b4=b4, k=k: e.matmul(
                    b4[:16, 0:WB], cWbT.ap()[:, k * 16:(k + 1) * 16], lnf(k),
                    start=(k == 0), stop=(k == 7)))
            bk.produced(p, 4, hv)
            bk.consume(p, 4, "s")
            bstrip = stripB.ap()[:16, 0:WB]
            hv = p.op("s", lambda e, b4=b4, bstrip=bstrip: e.activation(
                bstrip, b4[:16, 0:WB], AF.Sigmoid, bias=bias_beta.ap()[:16, 0:1]))
            bk.consumed(p, 4, hv)
            p.wait("t", hv)
            for c in range(NCH):
                b3 = bk.acquire(p, 3)
                hv = p.top(lambda e, c=c, b3=b3, bstrip=bstrip: e.matmul(
                    b3[:, 0:WB], cS16T.ap()[:16, c * 128:(c + 1) * 128], bstrip,
                    start=True, stop=True))
                bk.produced(p, 3, hv)
                bk.consume(p, 3, "v")
                hv2 = p.op("v", lambda e, c=c, b3=b3: e.tensor_copy(
                    betab.ap()[:, c * WB:(c + 1) * WB], b3[:, 0:WB]))
                bk.consumed(p, 3, hv2)
            if BCUT is not None and BCUT <= 3: continue
            conv4(bk0, lambda c: bkh.ap()[:, c * WB:(c + 1) * WB], convk)
            conv4(bq0, lambda c: bfm.ap()[:, c * WB:(c + 1) * WB], convq)
            conv4(bv0, lambda c: lnB.ap()[:, c * WB:(c + 1) * WB], convv)
            if BCUT is not None and BCUT <= 4: continue
            headnorm(bkh, 256, is_rms=False)
            headnorm(bfm, 512, is_rms=False)
            if BCUT is not None and BCUT <= 5: continue
            hvv = p.sem["v"].cur()
            p.wait("s", hvv)
            for c in range(NCH):
                sl = lnB.ap()[:, c * WB:(c + 1) * WB]
                hv = p.op("s", lambda e, sl=sl: e.activation(sl, sl, AF.Silu))
            p.wait("v", p.sem["s"].cur())
            for c in range(NCH):
                p.op("v", lambda e, c=c: e.tensor_tensor(
                    nbfm.ap()[:, c * WB:(c + 1) * WB], betab.ap()[:, c * WB:(c + 1) * WB],
                    lnB.ap()[:, c * WB:(c + 1) * WB], OP.mult))
            p.op("v", lambda e, lsv=lsv, r3v=r3v: e.tensor_copy(lsv[:, :, :], r3v[:, :, 256:259]))
            for c in range(NCH):
                bb_ = betab.ap()[:, c * WB:(c + 1) * WB]
                p.op("v", lambda e, c=c, bb_=bb_: e.tensor_tensor(
                    bb_, bb_, bkh.ap()[:, c * WB:(c + 1) * WB], OP.mult))
                p.op("v", lambda e, c=c, bb_=bb_, r3v=r3v: e.tensor_scalar(
                    r3v[:, c, 3:259], bb_, -1.0, None, OP.mult))
            hv_pre = p.sem["v"].cur()
            p.wait("t", hv_pre)
            if BCUT is not None and BCUT <= 6: continue
            if DBG == 5 and s == DBGCH // 4:
                dump_dbg([(0, bkh.ap()[:, 0:2048]), (1, bfm.ap()[:, 0:2048]),
                          (2, betab.ap()[:, 0:2048]), (3, nbfm.ap()[:, 0:2048]),
                          (5, bk0.ap()[:, 0:2048])])
            for ii in range(4):
                gi = s * 4 + ii
                for g in range(NCH):
                    delta_chunk(gi, ii, g)
            if BCUT is not None and BCUT <= 7: continue
            if s == 0:
                xbar(3)
            headnorm(lnB, 768, is_rms=True)
            if BCUT is not None and BCUT <= 8: continue
            hv_o = p.sem["v"].cur()
            p.wait("t", hv_o)
            def out_res(m, b):
                hv = p.op("v", lambda e, m=m, b=b: e.tensor_scalar(
                    smallB.ap()[:, 0:WB], b[:, 0:WB], bias_out.ap()[:, m:m + 1], None, OP.add))
                p.op("v", lambda e, m=m, t0=t0: e.tensor_tensor(
                    xc(m, t0, WB), xc(m, t0, WB), smallB.ap()[:, 0:WB], OP.add))
                return hv
            stream_mm(scr["out"], lnf if False else (lambda k: lnB.ap()[:, k * WB:(k + 1) * WB]),
                      out_res, nbank0=0)
            if s == 0:
                xbar(4)

    if stage >= 2:
        emit_ffn()
    if stage >= 3:
        emit_attn_b()
    if stage >= 4:
        emit_ffn()

    p.wait("t", p.sem["v"].cur())
    p.wait("t", p.sem["s"].cur())
    sem_out = [p.new_sem("outs0"), p.new_sem("outs1")]
    out_free = [None, None]
    for j in range(16):
        buf = xld.ap()[:, (j % 2) * 1024:(j % 2) * 1024 + 1024]
        if out_free[j % 2] is not None:
            p.wait("s", out_free[j % 2])
        for half in range(2):
            tiles = [xc(half * 4 + c, j * 128, 128) for c in range(4)]
            def evac(b, half=half, buf=buf):
                hv2 = None
                for q in range(4):
                    c = half * 4 + q
                    hv2 = p.op("s", lambda e, c=c, q=q, b=b, buf=buf:
                               e.copy(buf[:, c * 128:(c + 1) * 128], b[:, q * 128:(q + 1) * 128]))
                return hv2
            transpose_group(tiles, 6 + half, evac, evac_engine="s")
        out_free[j % 2] = p.dma("s", dy[j * 128:(j + 1) * 128, :], buf, sem=sem_out[j % 2])
    for so in sem_out:
        p.wait("s", so.cur())
        p.wait("y", so.cur())
        p.wait("g", so.cur())

    with nc.Block() as block:
        p.emit(block)
    stack.close()
    return nc


def kernel(L0_attn_a_x, L0_attn_a_Wqkv, L0_attn_a_bqkv, L0_attn_a_Wproj, L0_attn_a_bproj,
           L0_attn_a_scale, L0_attn_b_Wk, L0_attn_b_bk, L0_attn_b_Wq, L0_attn_b_bq,
           L0_attn_b_Wv, L0_attn_b_bv, L0_attn_b_Wbeta, L0_attn_b_bbeta,
           L0_attn_b_conv_k, L0_attn_b_conv_q, L0_attn_b_conv_v, L0_attn_b_rms_w,
           L0_attn_b_Wout, L0_attn_b_bout, L0_ffn_w1, L0_ffn_conv_w, L0_ffn_conv_b,
           L0_ffn_w2):
    nc = build_program(STAGE)
    consts = build_constants()
    shared = {
        "aWqkv": np.ascontiguousarray(L0_attn_a_Wqkv), "abqkv": np.ascontiguousarray(L0_attn_a_bqkv),
        "aWproj": np.ascontiguousarray(L0_attn_a_Wproj), "abproj": np.ascontiguousarray(L0_attn_a_bproj),
        "ascale": np.reshape(np.asarray(L0_attn_a_scale, dtype=np.float32), [1]),
        "bWk": np.ascontiguousarray(L0_attn_b_Wk), "bbk": np.ascontiguousarray(L0_attn_b_bk),
        "bWq": np.ascontiguousarray(L0_attn_b_Wq), "bbq": np.ascontiguousarray(L0_attn_b_bq),
        "bWv": np.ascontiguousarray(L0_attn_b_Wv), "bbv": np.ascontiguousarray(L0_attn_b_bv),
        "bWbeta": np.ascontiguousarray(L0_attn_b_Wbeta), "bbbeta": np.ascontiguousarray(L0_attn_b_bbeta),
        "bconvk": np.ascontiguousarray(L0_attn_b_conv_k), "bconvq": np.ascontiguousarray(L0_attn_b_conv_q),
        "bconvv": np.ascontiguousarray(L0_attn_b_conv_v), "brms": np.ascontiguousarray(L0_attn_b_rms_w),
        "bWout": np.ascontiguousarray(L0_attn_b_Wout), "bbout": np.ascontiguousarray(L0_attn_b_bout),
        "fw1": np.ascontiguousarray(L0_ffn_w1),
        "fconvw": np.ascontiguousarray(np.reshape(L0_ffn_conv_w, (DH, 3))),
        "fconvb": np.ascontiguousarray(L0_ffn_conv_b),
        "fw2": np.ascontiguousarray(L0_ffn_w2),
    }
    for k, v in consts.items():
        if k == "IDTB":
            shared["c_" + k] = np.ascontiguousarray(v.astype(ml_dtypes.bfloat16))
        else:
            shared["c_" + k] = np.ascontiguousarray(v.astype(np.float32))
    xin = np.ascontiguousarray(np.asarray(L0_attn_a_x, dtype=np.float32))
    in_maps = []
    for b in range(8):
        m = dict(shared)
        m["x_in"] = np.ascontiguousarray(xin[b])
        in_maps.append(m)
    res = run_bass_kernel_spmd(nc, in_maps, core_ids=list(range(8)))
    out = np.stack([res.results[b]["y_out"] for b in range(8)], axis=0)
    kernel.last_results = res
    return out


# version 04
# speedup vs baseline: 1.0669x; 1.0249x over previous
import contextlib
import numpy as np
import ml_dtypes
import concourse.bass as bass
import concourse.mybir as mybir
from concourse.bass_utils import run_bass_kernel_spmd

F32 = mybir.dt.float32
F16 = mybir.dt.float16
BF16 = mybir.dt.bfloat16
I32 = mybir.dt.int32
AF = mybir.ActivationFunctionType
OP = mybir.AluOpType
AX = mybir.AxisListType

T = 2048
D = 1024
NCH = 8
DH = 4096
HEADS = 16
HD = 64
CC = 64
NDC = 32
LN_EPS = 1e-5
WA = 256
WB = 256
WF = 128

STAGE = 4
DBG = False
DBGCH = 0
BSTRIPS = None
BCUT = None
DCUT = None
BAR = None


def _hilo(a64):
    hi = a64.astype(np.float32)
    lo = (a64 - hi.astype(np.float64)).astype(np.float32)
    return hi, lo


def build_constants():
    c = {}
    c["IDT"] = np.eye(128, dtype=np.float32)
    c["IDTB"] = np.eye(128, dtype=np.float32)
    c["ONES"] = np.ones((128, 128), dtype=np.float32)
    s16 = np.zeros((128, NCH * 16), dtype=np.float32)
    s16t = np.zeros((16, NCH * 128), dtype=np.float32)
    for ch in range(NCH):
        for p in range(128):
            h = 2 * ch + (1 if p >= 64 else 0)
            s16[p, ch * 16 + h] = 1.0
            s16t[h, ch * 128 + p] = 1.0
    c["S16"] = s16
    c["S16T"] = s16t
    tril = np.tril(np.ones((CC, CC)), -1)
    c["NMASK"] = np.tile(-tril, (1, 2)).astype(np.float32)
    c["NMASKT"] = np.tile(-tril.T, (1, 2)).astype(np.float32)
    c["I64S"] = np.tile(np.eye(CC), (1, 2)).astype(np.float32)
    a_all = np.zeros((NDC, CC))
    w_all = np.zeros((NDC, CC))
    c1_all = np.zeros(NDC)
    c2_all = np.zeros(NDC)
    tmT_all = np.zeros((NDC, CC, CC))
    Cf = float(CC)
    for i in range(NDC):
        ts = float(i * CC)
        idx = np.arange(1, CC + 1, dtype=np.float64) + ts
        cs = np.cumsum(idx)
        weighted_idx = idx + idx.sum() - cs
        w_all[i] = weighted_idx / ((1 + ts + Cf) * (ts + Cf) / 2)
        c1_all[i] = (ts / (ts + Cf)) * ((1 + ts) / (1 + ts + Cf)) if i > 0 else 0.0
        c2_all[i] = (Cf / (ts + Cf)) * ((ts * 2 + 1 + Cf) / (1 + ts + Cf))
        tri0 = (1 + ts) * ts / 2
        a_all[i] = tri0 / (tri0 + cs)
        r = cs[:, None]
        j_idx = (cs - idx)[None, :]
        tmT_all[i] = np.where(r >= j_idx, (r - j_idx) / (tri0 + r), 0.0).T
    ab = np.tile(a_all.reshape(1, NDC * CC), (128, 1))
    c["AB_hi"], c["AB_lo"] = _hilo(ab)
    wcol = np.concatenate([w_all.T, w_all.T], axis=0)
    c["WCOL_hi"], c["WCOL_lo"] = _hilo(wcol)
    c["C1_hi"], c["C1_lo"] = _hilo(np.tile(c1_all[None, :], (128, 1)))
    c["C2_hi"], c["C2_lo"] = _hilo(np.tile(c2_all[None, :], (128, 1)))
    tmt = np.zeros((64, NDC * CC))
    for i in range(NDC):
        tmt[:, i * CC:(i + 1) * CC] = tmT_all[i]
    c["TMT_hi"], c["TMT_lo"] = _hilo(tmt)
    return c



class Sem:
    CAP = 1_000_000_000

    def __init__(self, nc, stack, name):
        self.nc, self.stack, self.name = nc, stack, name
        self.idx = 0
        self.h = stack.enter_context(nc.semaphore(f"{name}0"))
        self.n = 0

    def add(self, k):
        if self.n + k > self.CAP:
            self.idx += 1
            self.h = self.stack.enter_context(self.nc.semaphore(f"{self.name}{self.idx}"))
            self.n = 0
        self.n += k
        return (self.h, self.n)

    def cur(self):
        return (self.h, self.n)


class Prog:
    ENGS = ("t", "v", "s", "g", "y")

    def __init__(self, nc, stack):
        self.nc = nc
        self.stack = stack
        self.ops = {e: [] for e in self.ENGS}
        self.sem = {e: Sem(nc, stack, f"c{e}") for e in self.ENGS}
        self.dmas = {q: Sem(nc, stack, f"d{q}") for q in ("y", "s", "g")}
        self.waited = {e: {} for e in self.ENGS}
        self._nsem = [0]

    def new_sem(self, name=None):
        self._nsem[0] += 1
        return Sem(self.nc, self.stack, name or f"x{self._nsem[0]}")

    def wait(self, e, hv):
        if hv is None:
            return
        h, v = hv
        seen = self.waited[e].get(id(h), -1)
        if v <= seen:
            return
        self.waited[e][id(h)] = v
        self.ops[e].append(lambda eng, h=h, v=v: eng.wait_ge(h, v))

    def op(self, e, fn, drain=True):
        hv = self.sem[e].add(1)
        def run(eng, fn=fn, hv=hv, drain=drain):
            fn(eng).then_inc(hv[0], 1)
            if drain:
                eng.drain()
        self.ops[e].append(run)
        return hv

    def drain(self, e):
        self.ops[e].append(lambda eng: eng.drain())

    def top(self, fn):
        hv = self.sem["t"].add(1)
        self.ops["t"].append(lambda eng, fn=fn, hv=hv: fn(eng).then_inc(hv[0], 1))
        return hv

    def dma(self, q, dst, src, slow=False, sem=None):
        hv = (sem if sem is not None else self.dmas[q]).add(16)
        if slow:
            self.ops[q].append(lambda eng, d=dst, s=src, hv=hv: eng.dma_start(
                d, s, allow_slow_non_contiguous=True).then_inc(hv[0], 16))
        else:
            self.ops[q].append(lambda eng, d=dst, s=src, hv=hv: eng.dma_start(d, s).then_inc(hv[0], 16))
        return hv

    def emit(self, block):
        decs = {"t": block.tensor, "v": block.vector, "s": block.scalar,
                "g": block.gpsimd, "y": block.sync}
        for e in self.ENGS:
            ops = self.ops[e]
            @decs[e]
            def _(eng, ops=ops):
                for fn in ops:
                    fn(eng)


class Banks:
    def __init__(self, nc, stack):
        self.b = [stack.enter_context(nc.psum_tensor(f"bank{i}", [128, 512], F32))
                  for i in range(8)]
        self.cons = [[] for _ in range(8)]
        self.prod = [None] * 8

    def acquire(self, p, i):
        for hv in self.cons[i]:
            p.wait("t", hv)
        self.cons[i] = []
        return self.b[i]

    def produced(self, p, i, hv):
        self.prod[i] = hv

    def consume(self, p, i, e):
        if self.prod[i] is not None:
            p.wait(e, self.prod[i])

    def consumed(self, p, i, hv):
        self.cons[i].append(hv)



def build_program(stage=None):
    if stage is None:
        stage = STAGE
    stack = contextlib.ExitStack()
    nc = bass.Bass("TRN2", target_bir_lowering=False)

    dx = nc.dram_tensor("x_in", [T, D], F32, kind="ExternalInput").ap()
    dy = nc.dram_tensor("y_out", [T, D], F32, kind="ExternalOutput").ap()
    ddbg = [nc.dram_tensor(f"dbg{i}", [128, 2048], F32, kind="ExternalOutput").ap()
            for i in range(6)]
    dbg_done = [False]
    def dump_dbg(aps):
        if dbg_done[0]:
            return
        dbg_done[0] = True
        p.wait("g", p.sem["v"].cur())
        p.wait("g", p.sem["s"].cur())
        hv = None
        sem_dbg = p.new_sem("dbg")
        for i, ap in aps:
            n = int(ap.ap[-1][1])
            np_ = int(ap.ap[0][1])
            hv = p.dma("g", ddbg[i][:np_, 0:n], ap, sem=sem_dbg)
        p.wait("v", hv)
        p.wait("s", hv)
        p.wait("t", hv)

    din = {}
    for name, shape in [
        ("aWqkv", [3 * D, D]), ("abqkv", [3 * D]), ("aWproj", [D, D]), ("abproj", [D]),
        ("ascale", [1]),
        ("bWk", [D, D]), ("bbk", [D]), ("bWq", [D, D]), ("bbq", [D]),
        ("bWv", [D, D]), ("bbv", [D]), ("bWbeta", [HEADS, D]), ("bbbeta", [HEADS]),
        ("bconvk", [D, 4]), ("bconvq", [D, 4]), ("bconvv", [D, 4]),
        ("brms", [HD]), ("bWout", [D, D]), ("bbout", [D]),
        ("fw1", [DH, D]), ("fconvw", [DH, 3]), ("fconvb", [DH]), ("fw2", [D, DH]),
    ]:
        din[name] = nc.dram_tensor(name, shape, F32, kind="ExternalInput").ap()

    CONSTS = build_constants()
    dcon = {}
    for k, v in CONSTS.items():
        dt = BF16 if k == "IDTB" else F32
        if k == "IDTB":
            dcon[k] = nc.dram_tensor("c_" + k, list(v.shape), BF16, kind="ExternalInput").ap()
        else:
            dcon[k] = nc.dram_tensor("c_" + k, list(v.shape), F32, kind="ExternalInput").ap()

    scr = {}
    scr["qkv"] = nc.dram_tensor("s_qkv", [24, 8, 128, 128], F32, kind="Internal").ap()
    scr["proj"] = nc.dram_tensor("s_proj", [8, 8, 128, 128], F32, kind="Internal").ap()
    for w in ("k", "q", "v", "out"):
        scr[w] = nc.dram_tensor(f"s_{w}", [8, 8, 128, 128], F32, kind="Internal").ap()
    scr["t1"] = nc.dram_tensor("s_t1", [32, 128, 1024], BF16, kind="Internal").ap()
    scr["t2"] = nc.dram_tensor("s_t2", [32, 8, 128, 128], BF16, kind="Internal").ap()

    cur = [16512]
    offs = {}

    def P_(name, shape, dtype=F32, at=None, alias=None):
        save = None
        if alias is not None:
            save = cur[0]
            cur[0] = offs[alias]
        elif at is not None:
            cur[0] = at
        nbytes = int(np.prod(shape[1:])) * mybir.dt.size(dtype)
        off = (cur[0] + 31) // 32 * 32
        h = nc.alloc_sbuf_tensor_at(name, shape, dtype, offset=off)
        offs[name] = off
        cur[0] = off + nbytes
        if save is not None and save > cur[0]:
            cur[0] = save
        return h

    xfm = P_("xfm", [128, NCH * T])
    cIDT = P_("cIDT", [128, 128])
    cIDTB = P_("cIDTB", [128, 128], BF16)
    cONES = P_("cONES", [128, 128])
    cS16 = P_("cS16", [128, NCH * 16])
    cS16T = P_("cS16T", [16, NCH * 128])
    cNMASK = P_("cNMASK", [64, 2 * CC])
    cNMASKT = P_("cNMASKT", [64, 2 * CC])
    cI64S = P_("cI64S", [64, 2 * CC])
    cAB = P_("cAB", [128, NDC * CC])
    cTMT = P_("cTMT", [64, NDC * CC])
    cWCOL = P_("cWCOL", [128, NDC])
    cC1 = P_("cC1", [128, NDC])
    cC2 = P_("cC2", [128, NDC])
    bias_qkv = P_("b_qkv", [128, 24])
    bias_proj = P_("b_proj", [128, 8])
    bias_k = P_("b_k", [128, 8]); bias_q = P_("b_q", [128, 8])
    bias_v = P_("b_v", [128, 8]); bias_out = P_("b_out", [128, 8])
    bias_beta = P_("b_beta", [16, 1])
    convk = P_("convk", [128, NCH * 4]); convq = P_("convq", [128, NCH * 4])
    convv = P_("convv", [128, NCH * 4])
    fcw = P_("fcw", [128, 32 * 3]); fcb = P_("fcb", [128, 32])
    rmsw = P_("rmsw", [128, 1])
    scale_a = P_("scale_a", [128, 1])
    cw1t = P_("cw1t", [128, 1]); cw1r = P_("cw1r", [128, 1])
    cw2t = P_("cw2t", [128, 1]); cw2r = P_("cw2r", [128, 1])
    cWbT = P_("cWbT", [128, NCH * 16])
    carry = P_("carry", [128, NCH * 2])
    PERM_END = cur[0]
    AR = (PERM_END + 63) // 64 * 64
    assert AR < 110500, f"perm too big: {AR}"

    wnat = P_("wnat", [128, 2 * DH], at=AR)
    pstage = P_("pstage", [128, 8 * 128])
    pstageb = P_("pstageb", [128, 8 * 128], BF16)
    pval = P_("pval", [128, 512])
    pint = P_("pint", [128, 512], I32)
    pf = P_("pf", [128, 512])
    pd = P_("pd", [128, 512])
    xld = P_("xld", [128, 2 * 1024])
    lnA = P_("lnA", [128, NCH * WA], at=AR)
    qkvA = P_("qkvA", [128, 24 * WA])
    kvA = P_("kvA", [128, NCH * WA])
    kcA = P_("kcA", [128, NCH * WA])
    wstrA = P_("wstrA", [128, 2 * 1024])
    smallA = P_("smallA", [128, 1024])
    stripA = P_("stripA", [128, 1024])
    mbA = P_("mbA", [128, WA])
    sbA = P_("sbA", [128, WA])
    czA = P_("czA", [128, WA])
    aintA = P_("aintA", [128, 512], I32)
    t1T = P_("t1T", [128, 32 * 1024], BF16, at=AR)
    hbuf = P_("hbuf", [128, 32 * 131])
    hc8 = P_("hc8", [128, 8 * 132])
    hsave = P_("hsave", [128, 32])
    m1b = P_("m1b", [128, 8 * WF], BF16)
    m2b = P_("m2b", [128, 32 * 256], BF16)
    t2buf = P_("t2buf", [128, 8 * 128], BF16)
    fsc = P_("fsc", [128, 8 * 128])
    fstats = P_("fstats", [128, 1024], alias="hc8")
    fval = P_("fval", [128, 512]); fint = P_("fint", [128, 512], I32)
    ff = P_("ff", [128, 512]); fd = P_("fd", [128, 512])
    famx = P_("famx", [128, 8 * 129], alias="fval")
    fsm = P_("fsm", [128, 384])
    FF_END = cur[0]
    assert FF_END <= 229344, f"ffn map too big: {FF_END}"
    lnB = P_("lnB", [128, NCH * WB], at=AR)
    bk0 = P_("bk0", [128, NCH * 264])
    bq0 = P_("bq0", [128, NCH * 264])
    bv0 = P_("bv0", [128, NCH * 264])
    bkh = P_("bkh", [128, NCH * WB])
    bfm = P_("bfmq", [128, NCH * WB])
    betab = P_("betab", [128, NCH * WB])
    nbfm = P_("nbfm", [128, NCH * WB])
    wstrB = P_("wstrB", [128, 2 * 1024])
    smallB = P_("smallB", [128, 1024])
    stripB = P_("stripB", [128, 1024])
    mbB = P_("mbB", [128, WB]); sbB = P_("sbB", [128, WB])
    leadsv = P_("leadsv", [128, NCH * 3 * 3])
    dW = P_("dW", [128, NCH * CC])
    dWavg = P_("dWavg", [128, NCH * CC])
    dPows = P_("dPows", [64, 12 * 2 * CC])
    dP = P_("dP", [64, 2 * CC])
    dG = P_("dG", [64, 2 * CC])
    dU = P_("dU", [64, 2 * CC])
    dUw = P_("dUw", [64, 2 * CC])
    dWmix = P_("dWmix", [128, CC])
    dT1 = P_("dT1", [64, 2 * CC])
    dT2 = P_("dT2", [128, CC])
    dKts = P_("dKts", [64, 2 * CC])
    dCmb = P_("dCmb", [128, CC])
    AB_END = cur[0]
    assert AB_END <= 229344, f"attn_b map too big: {AB_END}"

    p = Prog(nc, stack)
    bk = Banks(nc, stack)

    def xbar(pid):
        if BAR != pid:
            return
        hv_t, hv_v, hv_s = p.sem["t"].cur(), p.sem["v"].cur(), p.sem["s"].cur()
        for e in ("t", "v", "s", "g", "y"):
            p.wait(e, hv_t)
            p.wait(e, hv_v)
            p.wait(e, hv_s)

    def xc(c, t0, w):
        return xfm.ap()[:, c * T + t0: c * T + t0 + w]

    POW = {"N1": 0, "NS1": 1, "N2": 2, "NS2": 3, "N4": 4, "NS4": 5, "N8": 6,
           "NS8": 7, "N16": 8, "NS16": 9, "N32": 10, "PI": 11}

    def pw(nm):
        i = POW[nm]
        return dPows.ap()[:64, i * 2 * CC:(i + 1) * 2 * CC]

    def pwh(nm, h):
        i = POW[nm]
        return dPows.ap()[:64, i * 2 * CC + h * CC:i * 2 * CC + (h + 1) * CC]

    def transpose_group(tiles, bank_i, evac_fn, evac_engine="s", ident=None):
        idn = ident if ident is not None else cIDT.ap()[:, :]
        b = bk.acquire(p, bank_i)
        hv = None
        for j, src in enumerate(tiles):
            hv = p.top(lambda e, s=src, d=b[:, j * 128:(j + 1) * 128], idn=idn:
                       e.transpose(d, s, idn))
        bk.produced(p, bank_i, hv)
        bk.consume(p, bank_i, evac_engine)
        hv2 = evac_fn(b)
        bk.consumed(p, bank_i, hv2)
        return hv2

    def emit_round(val_ap, int_ap, f_ap, d_ap, out_ap):
        p.wait("s", p.sem["v"].cur())
        p.op("s", lambda e: e.copy(int_ap, val_ap))
        hv = p.op("s", lambda e: e.copy(f_ap, int_ap))
        p.wait("v", hv)
        p.op("v", lambda e: e.tensor_tensor(d_ap, val_ap, f_ap, OP.subtract))
        p.op("v", lambda e: e.scalar_tensor_tensor(f_ap, d_ap, 0.5, f_ap, OP.is_ge, OP.add))
        p.op("v", lambda e: e.tensor_scalar(d_ap, d_ap, -0.5, None, OP.is_le))
        return p.op("v", lambda e: e.tensor_tensor(out_ap, f_ap, d_ap, OP.subtract))

    def emit_rsqrt(out_ap, in_ap, eps, t1_ap, t2_ap, ln_ap):
        p.op("v", lambda e: e.tensor_scalar(t1_ap, in_ap, float(eps), None, OP.add))
        p.wait("s", p.sem["v"].cur())
        p.op("s", lambda e: e.activation(ln_ap, t1_ap, AF.Ln))
        hv = p.op("s", lambda e: e.activation(out_ap, ln_ap, AF.Exp, scale=-0.5))
        p.wait("v", hv)
        p.op("v", lambda e: e.tensor_tensor(t2_ap, out_ap, out_ap, OP.mult))
        p.op("v", lambda e: e.tensor_tensor(t2_ap, t2_ap, t1_ap, OP.mult))
        p.op("v", lambda e: e.tensor_scalar(t2_ap, t2_ap, -0.5, 1.5, OP.mult, OP.add))
        return p.op("v", lambda e: e.tensor_tensor(out_ap, out_ap, t2_ap, OP.mult))

    def emit_sqrt(out_ap, in_ap, t1_ap, t2_ap, ln_ap):
        p.op("v", lambda e: e.tensor_copy(t2_ap, in_ap))
        p.wait("s", p.sem["v"].cur())
        p.op("s", lambda e: e.activation(ln_ap, t2_ap, AF.Ln))
        hv = p.op("s", lambda e: e.activation(out_ap, ln_ap, AF.Exp, scale=0.5))
        p.wait("v", hv)
        p.op("v", lambda e: e.reciprocal(t1_ap, out_ap))
        p.op("v", lambda e: e.tensor_tensor(t1_ap, t1_ap, t2_ap, OP.mult))
        p.op("v", lambda e: e.tensor_tensor(t2_ap, out_ap, t1_ap, OP.add))
        return p.op("v", lambda e: e.tensor_scalar(out_ap, t2_ap, 0.5, None, OP.mult))

    sem_const = p.new_sem("const")
    for name, dst in [("IDT", cIDT), ("IDTB", cIDTB), ("ONES", cONES), ("S16", cS16),
                      ("S16T", cS16T), ("NMASK", cNMASK), ("NMASKT", cNMASKT),
                      ("I64S", cI64S)]:
        sh = dst.shape
        p.dma("y", dst.ap()[:sh[0], :], dcon[name], sem=sem_const)
    prev_add = None
    for hi, lo, dst in [("AB_hi", "AB_lo", cAB), ("WCOL_hi", "WCOL_lo", cWCOL),
                        ("C1_hi", "C1_lo", cC1), ("C2_hi", "C2_lo", cC2),
                        ("TMT_hi", "TMT_lo", cTMT)]:
        w = dst.shape[1]
        np_ = dst.shape[0]
        if prev_add is not None:
            p.wait("y", prev_add)
        hv0 = p.dma("y", dst.ap()[:, :], dcon[hi], sem=sem_const)
        hv = p.dma("y", wnat.ap()[:np_, :w], dcon[lo], sem=sem_const)
        p.wait("v", hv)
        prev_add = p.op("v", lambda e, d=dst, w=w, np_=np_: e.tensor_tensor(
            d.ap()[:, :], d.ap()[:, :], wnat.ap()[:np_, :w], OP.add))
    p.wait("y", prev_add)
    for nm, dst in [("abqkv", bias_qkv), ("abproj", bias_proj), ("bbk", bias_k),
                    ("bbq", bias_q), ("bbv", bias_v), ("bbout", bias_out)]:
        p.dma("y", dst.ap()[:, :], din[nm].rearrange("(c p) -> p c", p=128), slow=True, sem=sem_const)
    p.dma("y", bias_beta.ap()[:16, :], din["bbbeta"].rearrange("(h o) -> h o", o=1), slow=True, sem=sem_const)
    for nm, dst in [("bconvk", convk), ("bconvq", convq), ("bconvv", convv)]:
        p.dma("y", dst.ap()[:, :].rearrange("p (c k) -> p c k", k=4),
              din[nm].rearrange("(c p) k -> p c k", p=128), slow=True, sem=sem_const)
    p.dma("y", fcw.ap()[:, :].rearrange("p (c k) -> p c k", k=3),
          din["fconvw"].rearrange("(c p) k -> p c k", p=128), slow=True, sem=sem_const)
    p.dma("y", fcb.ap()[:, :], din["fconvb"].rearrange("(c p) -> p c", p=128), slow=True, sem=sem_const)
    p.dma("y", rmsw.ap()[:64, :], din["brms"].rearrange("(d o) -> d o", o=1), slow=True, sem=sem_const)
    p.dma("y", rmsw.ap()[64:128, :], din["brms"].rearrange("(d o) -> d o", o=1), slow=True, sem=sem_const)
    p.dma("y", scale_a.ap()[:1, :], din["ascale"].rearrange("(a b) -> a b", b=1), slow=True, sem=sem_const)
    p.wait("t", sem_const.cur())
    b0 = bk.acquire(p, 0)
    hv = p.top(lambda e: e.matmul(b0[:, 0:1], cONES.ap()[:1, :], scale_a.ap()[:1, :],
                                  start=True, stop=True))
    bk.produced(p, 0, hv)
    bk.consume(p, 0, "v")
    hv = p.op("v", lambda e: e.tensor_copy(scale_a.ap()[:, :], b0[:, 0:1]))
    bk.consumed(p, 0, hv)

    sem_wld = [p.new_sem("wld0"), p.new_sem("wld1")]
    sem_pst = p.new_sem("pst")
    last_tr = [None, None]
    last_wr = [None, None]

    def stage_fp32_weight(dsrc, dscr, MCH):
        for m in range(MCH):
            buf = wnat.ap()[:, (m % 2) * D:(m % 2) * D + D]
            if last_tr[m % 2] is not None:
                p.wait("y", last_tr[m % 2])
            if last_wr[m % 2] is not None:
                p.wait("y", last_wr[m % 2])
            hv = p.dma("y", buf, dsrc[m * 128:(m + 1) * 128, :], sem=sem_wld[m % 2])
            last_wr[m % 2] = hv
            p.wait("t", hv)
            for half in range(2):
                tiles = [buf[:, (half * 4 + k) * 128:(half * 4 + k + 1) * 128] for k in range(4)]
                def evac(b, half=half):
                    return p.op("v", lambda e: e.tensor_copy(
                        pstage.ap()[:, half * 512:(half + 1) * 512], b[:, 0:512]))
                transpose_group(tiles, 2 + half, evac, evac_engine="v")
            last_tr[m % 2] = p.sem["t"].cur()
            hvv = p.sem["v"].cur()
            p.wait("s", hvv)
            hv = p.dma("s", dscr[m, :, :, :].rearrange("k a b -> a k b"),
                       pstage.ap()[:, :].rearrange("a (k b) -> a k b", b=128), sem=sem_pst)
            p.wait("v", hv)

    stage_fp32_weight(din["aWqkv"], scr["qkv"], 24)
    stage_fp32_weight(din["aWproj"], scr["proj"], 8)
    stage_fp32_weight(din["bWk"], scr["k"], 8)
    stage_fp32_weight(din["bWq"], scr["q"], 8)
    stage_fp32_weight(din["bWv"], scr["v"], 8)
    stage_fp32_weight(din["bWout"], scr["out"], 8)

    if last_tr[0] is not None:
        p.wait("y", last_tr[0])
    if last_wr[0] is not None:
        p.wait("y", last_wr[0])
    hv = p.dma("y", wnat.ap()[:16, 0:D], din["bWbeta"], sem=sem_wld[0])
    last_wr[0] = hv
    p.wait("t", hv)
    b4 = bk.acquire(p, 4)
    hv = None
    for k in range(8):
        hv = p.top(lambda e, k=k: e.transpose(b4[:, k * 16:(k + 1) * 16],
                                              wnat.ap()[:16, k * 128:(k + 1) * 128],
                                              cIDT.ap()[:16, :16]))
    bk.produced(p, 4, hv)
    last_tr[0] = hv
    last_tr[1] = hv
    bk.consume(p, 4, "v")
    hv = p.op("v", lambda e: e.tensor_copy(cWbT.ap()[:, :], b4[:, 0:NCH * 16]))
    bk.consumed(p, 4, hv)

    def ffn_stage2(dsrc, MCH, KW, cwt, cwr, dscr, kmajor):
        sem_f = [p.new_sem(), p.new_sem()]
        sem_fs = p.new_sem()
        for g in (last_tr[0], last_tr[1], last_wr[0], last_wr[1]):
            if g is not None:
                p.wait("y", g)
        p.op("v", lambda e: e.memset(pval.ap()[:, 0:1], 0.0))
        prev_red = [last_tr[0], last_tr[1]]
        for m in range(MCH):
            buf = wnat.ap()[:, (m % 2) * KW:(m % 2) * KW + KW]
            if prev_red[m % 2] is not None:
                p.wait("y", prev_red[m % 2])
            if last_wr[m % 2] is not None:
                p.wait("y", last_wr[m % 2])
            hv = p.dma("y", buf, dsrc[m * 128:(m + 1) * 128, :], sem=sem_f[m % 2])
            last_wr[m % 2] = hv
            p.wait("v", hv)
            p.op("v", lambda e, buf=buf: e.tensor_reduce(
                pval.ap()[:, 1:2], buf, AX.X, OP.add, apply_absolute_value=True))
            hvv = p.op("v", lambda e: e.tensor_tensor(pval.ap()[:, 0:1], pval.ap()[:, 0:1],
                                                      pval.ap()[:, 1:2], OP.add))
            if m + 1 < MCH:
                p.wait("y", hvv)
        p.wait("t", p.sem["v"].cur())
        b5 = bk.acquire(p, 5)
        hv = p.top(lambda e: e.matmul(b5[:1, 0:1], cONES.ap()[:, 0:1], pval.ap()[:, 0:1],
                                      start=True, stop=True))
        bk.produced(p, 5, hv)
        bk.consume(p, 5, "v")
        p.op("v", lambda e: e.tensor_scalar(pval.ap()[:1, 2:3], b5[:1, 0:1],
                                            1.0 / (MCH * 128 * KW), 1e-5, OP.mult, OP.max))
        hv = p.op("v", lambda e: e.reciprocal(pval.ap()[:1, 3:4], pval.ap()[:1, 2:3]))
        bk.consumed(p, 5, hv)
        p.wait("t", hv)
        b5 = bk.acquire(p, 5)
        hv = p.top(lambda e: e.matmul(b5[:, 0:2], cONES.ap()[:1, :], pval.ap()[:1, 2:4],
                                      start=True, stop=True))
        bk.produced(p, 5, hv)
        bk.consume(p, 5, "v")
        p.op("v", lambda e: e.tensor_copy(cwt.ap()[:, :], b5[:, 0:1]))
        hv = p.op("v", lambda e: e.tensor_copy(cwr.ap()[:, :], b5[:, 1:2]))
        bk.consumed(p, 5, hv)
        last = None
        hvv_last = p.sem["v"].cur()
        qgate = [hvv_last, hvv_last]
        for m in range(MCH):
            buf = wnat.ap()[:, (m % 2) * KW:(m % 2) * KW + KW]
            if qgate[m % 2] is not None:
                p.wait("y", qgate[m % 2])
            if last_wr[m % 2] is not None:
                p.wait("y", last_wr[m % 2])
            hv = p.dma("y", buf, dsrc[m * 128:(m + 1) * 128, :], sem=sem_f[m % 2])
            last_wr[m % 2] = hv
            p.wait("v", hv)
            p.wait("s", hv)
            for grp in range(KW // 512):
                g0 = grp * 512
                qgate[m % 2] = p.op("v", lambda e, buf=buf, g0=g0: e.tensor_scalar(
                    pval.ap()[:, :], buf[:, g0:g0 + 512], cwr.ap()[:, 0:1], None, OP.mult))
                emit_round(pval.ap()[:, :], pint.ap()[:, :], pf.ap()[:, :], pd.ap()[:, :],
                           pf.ap()[:, :])
                p.op("v", lambda e, g0=g0: e.tensor_scalar(
                    pstage.ap()[:, (g0 % 1024):(g0 % 1024) + 512],
                    pf.ap()[:, :], 1.0, -1.0, OP.min, OP.max))
                if (g0 % 1024) == 512 or KW == 512:
                    blk = g0 // 1024
                    hvq = p.sem["v"].cur()
                    p.wait("t", hvq)
                    for half in range(2):
                        tiles = [pstage.ap()[:, (half * 4 + k) * 128:(half * 4 + k + 1) * 128]
                                 for k in range(4)]
                        def evac(b, half=half):
                            return p.op("v", lambda e, half=half: e.tensor_copy(
                                pstageb.ap()[:, half * 512:(half + 1) * 512], b[:, 0:512]))
                        transpose_group(tiles, 2 + half, evac, evac_engine="v")
                    hvv = p.sem["v"].cur()
                    p.wait("g", hvv)
                    if last is not None:
                        p.wait("g", last)
                    if kmajor:
                        kbase = blk * 8
                        last = p.dma("g", dscr[kbase:kbase + 8, m, :, :].rearrange("k a b -> a k b"),
                                     pstageb.ap()[:, :].rearrange("a (k b) -> a k b", b=128),
                                     sem=sem_fs)
                    else:
                        last = p.dma("g", dscr[m, :, :], pstageb.ap()[:, :], sem=sem_fs)
                    p.wait("v", last)
        last_tr[0] = qgate[0]
        last_tr[1] = qgate[1]
        return

    if stage >= 2:
        ffn_stage2(din["fw1"], 32, D, cw1t, cw1r, scr["t1"], kmajor=False)
        ffn_stage2(din["fw2"], 8, DH, cw2t, cw2r, scr["t2"], kmajor=True)

    sem_xld = [p.new_sem(), p.new_sem()]
    xld_free = [None, None]
    for j in range(16):
        buf = xld.ap()[:, (j % 2) * 1024:(j % 2) * 1024 + 1024]
        if xld_free[j % 2] is not None:
            p.wait("y", xld_free[j % 2])
        hv = p.dma("y", buf, dx[j * 128:(j + 1) * 128, :], sem=sem_xld[j % 2])
        p.wait("t", hv)
        for half in range(2):
            tiles = [buf[:, (half * 4 + c) * 128:(half * 4 + c + 1) * 128] for c in range(4)]
            def evac(b, half=half, j=j):
                hv2 = None
                for q in range(4):
                    c = half * 4 + q
                    hv2 = p.op("s", lambda e, c=c, q=q, b=b:
                               e.copy(xfm.ap()[:, c * T + j * 128: c * T + (j + 1) * 128],
                                      b[:, q * 128:(q + 1) * 128]))
                return hv2
            transpose_group(tiles, 6 + half, evac, evac_engine="s")
        xld_free[j % 2] = p.sem["t"].cur()

    def ln_stats(t0, W, mb_t, sb_t, small_t, strip_t, bankbase=6, amax_dst=None):
        ntile = W // 128
        for j in range(ntile):
            tj = t0 + j * 128
            xt = strip_t.ap()[:, 0:1024]
            for half in range(2):
                tiles = [xc(half * 4 + c, tj, 128) for c in range(4)]
                def evac(b, half=half, xt=xt):
                    hv2 = None
                    for q in range(4):
                        c = half * 4 + q
                        hv2 = p.op("s", lambda e, c=c, q=q, b=b, xt=xt:
                                   e.copy(xt[:, c * 128:(c + 1) * 128],
                                          b[:, q * 128:(q + 1) * 128]))
                    return hv2
                transpose_group(tiles, bankbase + half, evac, evac_engine="s")
            hvs = p.sem["s"].cur()
            p.wait("v", hvs)
            sm = small_t.ap()
            p.op("v", lambda e, xt=xt, sm=sm: e.tensor_reduce(sm[:, 0:1], xt[:, 0:D],
                                                              AX.X, OP.add))
            p.op("v", lambda e, sm=sm: e.tensor_scalar(sm[:, 0:1], sm[:, 0:1], 1.0 / D,
                                                       None, OP.mult))
            if amax_dst is not None:
                p.op("v", lambda e, xt=xt, sm=sm: e.tensor_scalar(
                    xt[:, 0:D], xt[:, 0:D], sm[:, 0:1], None, OP.subtract))
                hvr = p.op("v", lambda e, xt=xt, j=j: e.tensor_reduce(
                    amax_dst[:, j:j + 1], xt[:, 0:D], AX.X, OP.max, apply_absolute_value=True))
                p.wait("s", hvr)
                hv = p.op("s", lambda e, xt=xt: e.activation(xt[:, 0:D], xt[:, 0:D], AF.Square))
                p.wait("v", hv)
                p.op("v", lambda e, xt=xt, sm=sm: e.tensor_reduce(sm[:, 1:2], xt[:, 0:D],
                                                                  AX.X, OP.add))
                p.op("v", lambda e, sm=sm: e.tensor_scalar(sm[:, 1:2], sm[:, 1:2], 1.0 / D,
                                                           None, OP.mult))
            else:
                p.wait("s", p.sem["v"].cur())
                hv = p.op("s", lambda e, xt=xt: e.activation(xt[:, 0:D], xt[:, 0:D], AF.Square))
                p.wait("v", hv)
                p.op("v", lambda e, xt=xt, sm=sm: e.tensor_reduce(sm[:, 1:2], xt[:, 0:D],
                                                                  AX.X, OP.add))
                p.op("v", lambda e, sm=sm: e.tensor_scalar(sm[:, 1:2], sm[:, 1:2], 1.0 / D,
                                                           None, OP.mult))
                p.op("v", lambda e, sm=sm: e.tensor_tensor(sm[:, 2:3], sm[:, 0:1],
                                                           sm[:, 0:1], OP.mult))
                p.op("v", lambda e, sm=sm: e.tensor_tensor(sm[:, 1:2], sm[:, 1:2],
                                                           sm[:, 2:3], OP.subtract))
            emit_rsqrt(sm[:, 1:2], sm[:, 1:2], LN_EPS, sm[:, 2:3], sm[:, 3:4], sm[:, 4:5])
            p.op("v", lambda e, sm=sm: e.tensor_scalar(sm[:, 128:256], cONES.ap()[:, :],
                                                       sm[:, 0:1], None, OP.mult))
            hv = p.op("v", lambda e, sm=sm: e.tensor_scalar(sm[:, 256:384], cONES.ap()[:, :],
                                                            sm[:, 1:2], None, OP.mult))
            p.wait("t", hv)
            bb = bk.acquire(p, bankbase)
            p.top(lambda e, sm=sm: e.transpose(bb[:, 0:128], sm[:, 128:256], cIDT.ap()[:, :]))
            hv = p.top(lambda e, sm=sm: e.transpose(bb[:, 128:256], sm[:, 256:384],
                                                    cIDT.ap()[:, :]))
            bk.produced(p, bankbase, hv)
            bk.consume(p, bankbase, "s")
            p.op("s", lambda e, j=j: e.copy(mb_t.ap()[:, j * 128:(j + 1) * 128], bb[:, 0:128]))
            hv = p.op("s", lambda e, j=j: e.copy(sb_t.ap()[:, j * 128:(j + 1) * 128],
                                                 bb[:, 128:256]))
            bk.consumed(p, bankbase, hv)

    sem_wsA = [p.new_sem(), p.new_sem()]

    def emit_attn_a():
        p.op("v", lambda e: e.memset(carry.ap()[:, :], 0.0))
        p.op("v", lambda e: e.memset(czA.ap()[:, :], 0.0))
        p.wait("t", p.sem["s"].cur())
        p.wait("t", p.sem["v"].cur())
        p.wait("y", p.sem["t"].cur())
        p.wait("y", p.sem["s"].cur())
        p.wait("g", p.sem["t"].cur())
        wfree = [None, None]
        for s in range(T // WA):
            t0 = s * WA
            ln_stats(t0, WA, mbA, sbA, smallA, stripA, bankbase=6)
            p.wait("v", p.sem["s"].cur())
            for c in range(NCH):
                ln = lnA.ap()[:, c * WA:(c + 1) * WA]
                p.op("v", lambda e, c=c, ln=ln, t0=t0: e.tensor_tensor(
                    ln, xc(c, t0, WA), mbA.ap()[:, :], OP.subtract), drain=False)
            p.drain("v")
            for c in range(NCH):
                ln = lnA.ap()[:, c * WA:(c + 1) * WA]
                p.op("v", lambda e, ln=ln: e.tensor_tensor(ln, ln, sbA.ap()[:, :], OP.mult),
                     drain=False)
            hv_ln = p.sem["v"].cur()
            p.wait("t", hv_ln)
            if s == 0 and DBG == 1:
                dump_dbg([(0, lnA.ap()[:, 0:2048]), (1, mbA.ap()[:, 0:WA]),
                          (2, sbA.ap()[:, 0:WA]), (3, xfm.ap()[:, 0:2048])])
            for m in range(24):
                wb = wstrA.ap()[:, (m % 2) * 1024:(m % 2) * 1024 + 1024]
                if wfree[m % 2] is not None:
                    p.wait("y", wfree[m % 2])
                hvw = p.dma("y", wb.rearrange("a (k b) -> a k b", b=128),
                            scr["qkv"][m, :, :, :].rearrange("k a b -> a k b"),
                            sem=sem_wsA[m % 2])
                p.wait("t", hvw)
                bi = m % 2
                b = bk.acquire(p, bi)
                hv = None
                for k in range(8):
                    hv = p.top(lambda e, b=b, wb=wb, k=k:
                               e.matmul(b[:, 0:WA], wb[:, k * 128:(k + 1) * 128],
                                        lnA.ap()[:, k * WA:(k + 1) * WA],
                                        start=(k == 0), stop=(k == 7)))
                wfree[m % 2] = hv
                bk.produced(p, bi, hv)
                bk.consume(p, bi, "v")
                hv = p.op("v", lambda e, m=m, b=b:
                          e.tensor_scalar(qkvA.ap()[:, m * WA:(m + 1) * WA], b[:, 0:WA],
                                          bias_qkv.ap()[:, m:m + 1], None, OP.add),
                          drain=(m == 23))
                bk.consumed(p, bi, hv)
            for m in range(16):
                u = qkvA.ap()[:, m * WA:(m + 1) * WA]
                mn = smallA.ap()[:, 0:WA]
                mx = smallA.ap()[:, WA:2 * WA]
                p.op("v", lambda e, u=u, mn=mn: e.tensor_scalar(mn, u, 0.0, None, OP.min))
                hv = p.op("v", lambda e, u=u, mx=mx: e.tensor_scalar(mx, u, 0.0, None, OP.max))
                p.wait("s", hv)
                hv = p.op("s", lambda e, mn=mn: e.activation(mn, mn, AF.Exp,
                                                             scale=scale_a.ap()[:, 0:1]))
                p.wait("v", hv)
                hv = p.op("v", lambda e, u=u, mn=mn, mx=mx:
                          e.scalar_tensor_tensor(u, mx, scale_a.ap()[:, 0:1], mn,
                                                 OP.mult, OP.add))
                p.wait("s", hv)
            for c in range(NCH):
                kq = qkvA.ap()[:, (8 + c) * WA:(9 + c) * WA]
                vv = qkvA.ap()[:, (16 + c) * WA:(17 + c) * WA]
                p.op("v", lambda e, c=c, kq=kq, vv=vv:
                     e.tensor_tensor(kvA.ap()[:, c * WA:(c + 1) * WA], kq, vv, OP.mult))
            for c in range(NCH):
                kq = qkvA.ap()[:, (8 + c) * WA:(9 + c) * WA]
                kc = kcA.ap()[:, c * WA:(c + 1) * WA]
                p.op("v", lambda e, c=c: e.tensor_copy(czA.ap()[:, 0:1],
                                                       carry.ap()[:, 2 * c:2 * c + 1]))
                p.op("v", lambda e, kq=kq, kc=kc:
                     e.tensor_tensor_scan(kc, kq, czA.ap()[:, 0:WA], 0.0, OP.add, OP.add))
                p.op("v", lambda e, c=c, kc=kc:
                     e.tensor_copy(carry.ap()[:, 2 * c:2 * c + 1], kc[:, WA - 1:WA]))
                kv = kvA.ap()[:, c * WA:(c + 1) * WA]
                p.op("v", lambda e, c=c: e.tensor_copy(czA.ap()[:, 0:1],
                                                       carry.ap()[:, 2 * c + 1:2 * c + 2]))
                p.op("v", lambda e, kv=kv:
                     e.tensor_tensor_scan(kv, kv, czA.ap()[:, 0:WA], 0.0, OP.add, OP.add))
                p.op("v", lambda e, c=c, kv=kv:
                     e.tensor_copy(carry.ap()[:, 2 * c + 1:2 * c + 2], kv[:, WA - 1:WA]))
            b2 = bk.acquire(p, 2)
            prev_mm = None
            for c in range(NCH):
                qq = qkvA.ap()[:, c * WA:(c + 1) * WA]
                kc = kcA.ap()[:, c * WA:(c + 1) * WA]
                if prev_mm is not None:
                    p.wait("v", prev_mm)
                hv = p.op("v", lambda e, qq=qq, kc=kc:
                          e.tensor_tensor(smallA.ap()[:, 0:WA], qq, kc, OP.mult))
                p.wait("t", hv)
                prev_mm = hv = p.top(lambda e, c=c, b2=b2: e.matmul(
                    b2[:16, 0:WA], cS16.ap()[:, c * 16:(c + 1) * 16], smallA.ap()[:, 0:WA],
                    start=(c == 0), stop=(c == 7)))
            bk.produced(p, 2, hv)
            bk.consume(p, 2, "v")
            den = stripA.ap()[:16, 0:WA]
            p.op("v", lambda e, b2=b2: e.tensor_scalar(den, b2[:16, 0:WA], 1e-3, None, OP.max))
            hv = p.op("v", lambda e: e.reciprocal(den, den))
            bk.consumed(p, 2, hv)
            p.wait("t", hv)
            for c in range(NCH):
                b3 = bk.acquire(p, 3)
                hv = p.top(lambda e, c=c, b3=b3: e.matmul(
                    b3[:, 0:WA], cS16T.ap()[:16, c * 128:(c + 1) * 128], den,
                    start=True, stop=True))
                bk.produced(p, 3, hv)
                qq = qkvA.ap()[:, c * WA:(c + 1) * WA]
                kv = kvA.ap()[:, c * WA:(c + 1) * WA]
                yy = lnA.ap()[:, c * WA:(c + 1) * WA]
                p.op("v", lambda e, qq=qq, kv=kv, yy=yy: e.tensor_tensor(yy, qq, kv, OP.mult))
                bk.consume(p, 3, "v")
                hv = p.op("v", lambda e, yy=yy, b3=b3: e.tensor_tensor(yy, yy, b3[:, 0:WA],
                                                                       OP.mult))
                bk.consumed(p, 3, hv)
            hv_y = p.sem["v"].cur()
            p.wait("t", hv_y)
            if s == 0 and DBG == 2:
                dump_dbg([(0, lnA.ap()[:, 0:2048]), (1, qkvA.ap()[:, 0:2048]),
                          (2, qkvA.ap()[:, 8 * WA:8 * WA + 2048]),
                          (3, kcA.ap()[:, 0:2048]), (4, kvA.ap()[:, 0:2048]),
                          (5, stripA.ap()[:, 0:1024])])
            for m in range(8):
                wb = wstrA.ap()[:, (m % 2) * 1024:(m % 2) * 1024 + 1024]
                if wfree[m % 2] is not None:
                    p.wait("y", wfree[m % 2])
                hvw = p.dma("y", wb.rearrange("a (k b) -> a k b", b=128),
                            scr["proj"][m, :, :, :].rearrange("k a b -> a k b"),
                            sem=sem_wsA[m % 2])
                p.wait("t", hvw)
                bi = 4 + (m % 2)
                b = bk.acquire(p, bi)
                hv = None
                for k in range(8):
                    hv = p.top(lambda e, b=b, wb=wb, k=k:
                               e.matmul(b[:, 0:WA], wb[:, k * 128:(k + 1) * 128],
                                        lnA.ap()[:, k * WA:(k + 1) * WA],
                                        start=(k == 0), stop=(k == 7)))
                wfree[m % 2] = hv
                bk.produced(p, bi, hv)
                bk.consume(p, bi, "v")
                hv = p.op("v", lambda e, m=m, b=b: e.tensor_scalar(
                    smallA.ap()[:, 0:WA], b[:, 0:WA], bias_proj.ap()[:, m:m + 1], None, OP.add))
                bk.consumed(p, bi, hv)
                p.op("v", lambda e, m=m, t0=t0: e.tensor_tensor(
                    xc(m, t0, WA), xc(m, t0, WA), smallA.ap()[:, 0:WA], OP.add))

    if stage >= 1:
        emit_attn_a()

    def ln_stats_ffn(t0, sm):
        xt = fstats.ap()[:, 0:1024]
        for half in range(2):
            tiles = [xc(half * 4 + c, t0, 128) for c in range(4)]
            def evac(b, half=half):
                hv2 = None
                for q in range(4):
                    c = half * 4 + q
                    hv2 = p.op("s", lambda e, c=c, q=q, b=b:
                               e.copy(xt[:, c * 128:(c + 1) * 128], b[:, q * 128:(q + 1) * 128]))
                return hv2
            transpose_group(tiles, 6 + half, evac, evac_engine="s")
        p.wait("v", p.sem["s"].cur())
        p.op("v", lambda e: e.tensor_reduce(sm[:, 0:1], xt[:, 0:D], AX.X, OP.add))
        p.op("v", lambda e: e.tensor_scalar(sm[:, 0:1], sm[:, 0:1], 1.0 / D, None, OP.mult))
        p.op("v", lambda e: e.tensor_scalar(xt[:, 0:D], xt[:, 0:D], sm[:, 0:1], None,
                                            OP.subtract))
        hvr = p.op("v", lambda e: e.tensor_reduce(sm[:, 5:6], xt[:, 0:D], AX.X, OP.max,
                                            apply_absolute_value=True))
        p.wait("s", hvr)
        hv = p.op("s", lambda e: e.activation(xt[:, 0:D], xt[:, 0:D], AF.Square))
        p.wait("v", hv)
        p.op("v", lambda e: e.tensor_reduce(sm[:, 1:2], xt[:, 0:D], AX.X, OP.add))
        p.op("v", lambda e: e.tensor_scalar(sm[:, 1:2], sm[:, 1:2], 1.0 / D, None, OP.mult))
        emit_rsqrt(sm[:, 1:2], sm[:, 1:2], LN_EPS, sm[:, 2:3], sm[:, 3:4], sm[:, 4:5])

    bcast_gate = [None]

    def bcast_col(src_col_ap, dst_ap, sm, bank=6):
        if bcast_gate[0] is not None:
            p.wait("v", bcast_gate[0])
        hv = p.op("v", lambda e: e.tensor_scalar(sm[:, 128:256], cONES.ap()[:, :],
                                                 src_col_ap, None, OP.mult))
        p.wait("t", hv)
        bb = bk.acquire(p, bank)
        hv = p.top(lambda e, bb=bb: e.transpose(bb[:, 0:128], sm[:, 128:256], cIDT.ap()[:, :]))
        bcast_gate[0] = hv
        bk.produced(p, bank, hv)
        bk.consume(p, bank, "s")
        hv = p.op("s", lambda e, bb=bb, dst_ap=dst_ap: e.copy(dst_ap, bb[:, 0:128]))
        bk.consumed(p, bank, hv)
        return hv

    def emit_ffn():
        sem_t1 = p.new_sem()
        sem_t2 = p.new_sem()
        p.wait("y", p.sem["t"].cur())
        p.wait("y", p.sem["v"].cur())
        p.wait("y", p.sem["s"].cur())
        p.wait("g", p.sem["t"].cur())
        hv_t1 = p.dma("y", t1T.ap()[:, :].rearrange("a (m w) -> a m w", w=1024),
                      scr["t1"].rearrange("m a w -> a m w"), sem=sem_t1)
        p.wait("t", hv_t1)
        h3 = hbuf.ap()[:, :].rearrange("a (c w) -> a c w", w=131)
        m23 = m2b.ap()[:, :].rearrange("a (c w) -> a c w", w=256)
        p.op("v", lambda e: e.memset(hsave.ap()[:, :], 0.0))
        p.op("v", lambda e, h3=h3: e.memset(h3[:, :, 0:2], 0.0))
        p.op("v", lambda e, h3=h3: e.memset(h3[:, :, 130:131], 0.0))
        sm = fsm.ap()
        burst = {"tb0": -1, "bc": 0, "last_mm": None}

        def mm2_flush():
            if burst["bc"] == 0:
                return
            tb0 = burst["tb0"]
            sc = 1 if tb0 < 0 else 0
            ncol = burst["bc"] - sc
            out0 = tb0 + sc
            p.wait("t", p.sem["v"].cur())
            bs = [bk.acquire(p, i) for i in range(8)]
            hv = None
            t2gate = None
            for k in range(32):
                if t2gate is not None:
                    p.wait("g", t2gate)
                hvw = p.dma("g", t2buf.ap()[:, :].rearrange("a (m b) -> a m b", b=128),
                            scr["t2"][k, :, :, :].rearrange("m a b -> a m b"), sem=sem_t2)
                p.wait("t", hvw)
                for m in range(8):
                    b = bs[m]
                    hv = p.top(lambda e, b=b, m=m, k=k, ncol=ncol, sc=sc:
                               e.matmul(b[:, 0:ncol],
                                        t2buf.ap()[:, m * 128:(m + 1) * 128],
                                        m2b.ap()[:, k * 256 + sc:k * 256 + sc + ncol],
                                        start=(k == 0), stop=(k == 31)))
                t2gate = hv
            for i in range(8):
                bk.produced(p, i, hv)
            for m in range(8):
                b = bs[m]
                bk.consume(p, m, "v")
                p.op("v", lambda e, b=b, ncol=ncol, sc=sc:
                     e.tensor_tensor(fval.ap()[:, 0:ncol], b[:, 0:ncol],
                                     fsc.ap()[:, 512 + sc:512 + sc + ncol], OP.mult))
                hv2 = p.op("v", lambda e, m=m, out0=out0, ncol=ncol:
                           e.tensor_tensor(xc(m, out0, ncol), xc(m, out0, ncol),
                                           fval.ap()[:, 0:ncol], OP.add))
                bk.consumed(p, m, hv2)
            burst["last_mm"] = hv
            burst["bc"] = 0
            burst["tb0"] = out0 + ncol

        for s in range(T // WF):
            t0 = s * WF
            last_strip = (s == T // WF - 1)
            n = 129 if last_strip else 128
            if s > 0:
                p.op("v", lambda e, h3=h3: e.tensor_copy(
                    h3[:, :, 0:1], hsave.ap()[:, :].rearrange("a (c o) -> a c o", o=1)))
                p.op("v", lambda e, h3=h3: e.tensor_copy(h3[:, :, 1:2], h3[:, :, 129:130]))
            ln_stats_ffn(t0, sm)
            p.op("v", lambda e: e.tensor_tensor(sm[:, 6:7], sm[:, 5:6], sm[:, 1:2], OP.mult))
            p.op("v", lambda e: e.tensor_scalar(sm[:, 6:7], sm[:, 6:7], 1e-5, None, OP.max))
            p.op("v", lambda e: e.reciprocal(sm[:, 7:8], sm[:, 6:7]))
            p.op("v", lambda e: e.tensor_scalar(sm[:, 7:8], sm[:, 7:8], 127.0, None, OP.mult))
            p.op("v", lambda e: e.tensor_tensor(sm[:, 8:9], sm[:, 1:2], sm[:, 7:8], OP.mult))
            p.op("v", lambda e: e.tensor_scalar(sm[:, 9:10], sm[:, 6:7], 1.0 / 127.0,
                                                None, OP.mult))
            p.op("v", lambda e: e.tensor_tensor(sm[:, 9:10], sm[:, 9:10], cw1t.ap()[:, 0:1],
                                                OP.mult))
            bcast_col(sm[:, 0:1], fsc.ap()[:, 0:128], sm)
            bcast_col(sm[:, 8:9], fsc.ap()[:, 128:256], sm)
            bcast_col(sm[:, 9:10], fsc.ap()[:, 256:384], sm)
            p.wait("v", p.sem["s"].cur())
            if burst["last_mm"] is not None:
                p.wait("v", burst["last_mm"])
                p.wait("s", burst["last_mm"])
            if burst.get("last_mm1") is not None:
                p.wait("v", burst["last_mm1"])
                p.wait("s", burst["last_mm1"])
            for c in range(NCH):
                dst = fval.ap()[:, (c % 4) * 128:(c % 4) * 128 + 128]
                p.op("v", lambda e, c=c, dst=dst, t0=t0: e.tensor_tensor(
                    dst, xc(c, t0, WF), fsc.ap()[:, 0:128], OP.subtract))
                p.op("v", lambda e, dst=dst: e.tensor_tensor(
                    dst, dst, fsc.ap()[:, 128:256], OP.mult))
                if c % 4 == 3:
                    emit_round(fval.ap()[:, :], fint.ap()[:, :], ff.ap()[:, :],
                               fd.ap()[:, :],
                               m1b.ap()[:, (c - 3) * 128:(c + 1) * 128])
            hv_m1 = p.sem["v"].cur()
            p.wait("t", hv_m1)
            for m in range(32):
                bi = 4 + (m % 2)
                b = bk.acquire(p, bi)
                hv = None
                for k in range(8):
                    hv = p.top(lambda e, b=b, m=m, k=k: e.matmul(
                        b[:, 0:WF], t1T.ap()[:, m * 1024 + k * 128: m * 1024 + (k + 1) * 128],
                        m1b.ap()[:, k * WF:(k + 1) * WF], start=(k == 0), stop=(k == 7)))
                burst["last_mm1"] = hv
                bk.produced(p, bi, hv)
                bk.consume(p, bi, "v")
                hv = p.op("v", lambda e, b=b, m=m, h3=h3: e.tensor_tensor(
                    h3[:, m, 2:130], b[:, 0:WF], fsc.ap()[:, 256:384], OP.mult),
                    drain=(m == 31))
                bk.consumed(p, bi, hv)
            p.op("v", lambda e, h3=h3: e.tensor_copy(
                hsave.ap()[:, :].rearrange("a (c o) -> a c o", o=1), h3[:, :, 128:129]))
            hcv = hc8.ap()[:, :].rearrange("a (c w) -> a c w", w=132)
            for grp in range(4):
                for q in range(8):
                    m = grp * 8 + q
                    p.op("v", lambda e, m=m, q=q, n=n, h3=h3, hcv=hcv: e.tensor_scalar(
                        hcv[:, q, 0:n], h3[:, m, 1:1 + n], fcw.ap()[:, m * 3 + 1:m * 3 + 2],
                        fcb.ap()[:, m:m + 1], OP.mult, OP.add), drain=False)
                p.drain("v")
                for q in range(8):
                    m = grp * 8 + q
                    p.op("v", lambda e, m=m, q=q, n=n, h3=h3, hcv=hcv:
                         e.scalar_tensor_tensor(
                        hcv[:, q, 0:n], h3[:, m, 0:n], fcw.ap()[:, m * 3:m * 3 + 1],
                        hcv[:, q, 0:n], OP.mult, OP.add), drain=False)
                p.drain("v")
                for q in range(8):
                    m = grp * 8 + q
                    p.op("v", lambda e, m=m, q=q, n=n, h3=h3, hcv=hcv:
                         e.scalar_tensor_tensor(
                        hcv[:, q, 0:n], h3[:, m, 2:2 + n], fcw.ap()[:, m * 3 + 2:m * 3 + 3],
                        hcv[:, q, 0:n], OP.mult, OP.add), drain=False)
                hvv = p.sem["v"].cur()
                p.wait("s", hvv)
                hv = p.op("s", lambda e, grp=grp, n=n, h3=h3, hcv=hcv: e.activation(
                    h3[:, grp * 8:(grp + 1) * 8, 1:1 + n], hcv[:, 0:8, 0:n], AF.Silu))
                p.wait("v", hv)
                p.wait("g", hv)
            amx = hcv
            hperm = hbuf.ap()[:, :].rearrange("a (c w) -> a w c", w=131)
            p.op("v", lambda e, hperm=hperm, amx=amx, n=n: e.tensor_reduce(
                amx[:, 0, 0:n], hperm[:, 1:1 + n, :], AX.X, OP.max,
                apply_absolute_value=True))
            hv = p.sem["v"].cur()
            p.wait("t", hv)
            bb = bk.acquire(p, 6)
            hv = p.top(lambda e, bb=bb, amx=amx: e.transpose(bb[:, 0:128], amx[:, 0, 0:128],
                                                             cIDT.ap()[:, :]))
            bk.produced(p, 6, hv)
            bk.consume(p, 6, "v")
            hv2 = p.op("v", lambda e, bb=bb: e.tensor_reduce(
                sm[:, 20:21], bb[:, 0:128], AX.X, OP.max, apply_absolute_value=True))
            bk.consumed(p, 6, hv2)
            if last_strip:
                bb = bk.acquire(p, 6)
                hv = p.top(lambda e, bb=bb, amx=amx: e.transpose(
                    bb[:1, 0:128], amx[:, 0:1, 128:129], cIDT.ap()[:, :]))
                bk.produced(p, 6, hv)
                bk.consume(p, 6, "v")
                hv2 = p.op("v", lambda e, bb=bb: e.tensor_reduce(
                    sm[:1, 24:25], bb[:1, 0:128], AX.X, OP.max, apply_absolute_value=True))
                bk.consumed(p, 6, hv2)
                p.op("v", lambda e: e.tensor_scalar(sm[:1, 24:25], sm[:1, 24:25], 1e-5,
                                                    None, OP.max))
                p.op("v", lambda e: e.reciprocal(sm[:1, 25:26], sm[:1, 24:25]))
                p.op("v", lambda e: e.tensor_scalar(sm[:1, 25:26], sm[:1, 25:26], 127.0,
                                                    None, OP.mult))
                p.op("v", lambda e: e.tensor_scalar(sm[:1, 26:27], sm[:1, 24:25],
                                                    1.0 / 127.0, None, OP.mult))
                p.op("v", lambda e: e.tensor_tensor(sm[:1, 26:27], sm[:1, 26:27],
                                                    cw2t.ap()[:1, 0:1], OP.mult))
            p.op("v", lambda e: e.tensor_scalar(sm[:, 20:21], sm[:, 20:21], 1e-5, None, OP.max))
            p.op("v", lambda e: e.reciprocal(sm[:, 21:22], sm[:, 20:21]))
            p.op("v", lambda e: e.tensor_scalar(sm[:, 21:22], sm[:, 21:22], 127.0, None,
                                                OP.mult))
            p.op("v", lambda e: e.tensor_scalar(sm[:, 22:23], sm[:, 20:21], 1.0 / 127.0,
                                                None, OP.mult))
            p.op("v", lambda e: e.tensor_tensor(sm[:, 22:23], sm[:, 22:23], cw2t.ap()[:, 0:1],
                                                OP.mult))
            bcast_col(sm[:, 21:22], fsc.ap()[:, 384:512], sm)
            bc = burst["bc"]
            bcast_col(sm[:, 22:23], fsc.ap()[:, 512 + bc:512 + bc + 128], sm)
            if last_strip:
                p.wait("t", p.sem["v"].cur())
                bb = bk.acquire(p, 6)
                p.top(lambda e, bb=bb: e.matmul(bb[:, 0:1], cONES.ap()[:1, :], sm[:1, 25:26],
                                                start=True, stop=True))
                hv = p.top(lambda e, bb=bb: e.matmul(bb[:, 1:2], cONES.ap()[:1, :],
                                                     sm[:1, 26:27], start=True, stop=True,
                                                     skip_group_check=True))
                bk.produced(p, 6, hv)
                bk.consume(p, 6, "s")
                p.op("s", lambda e, bb=bb: e.copy(fsc.ap()[:, 1000:1001], bb[:, 0:1]))
                hv = p.op("s", lambda e, bb=bb, bc=bc: e.copy(
                    fsc.ap()[:, 512 + bc + 128:512 + bc + 129], bb[:, 1:2]))
                bk.consumed(p, 6, hv)
            p.wait("v", p.sem["s"].cur())
            for grp in range(8):
                for q in range(4):
                    c = grp * 4 + q
                    p.op("v", lambda e, c=c, q=q, h3=h3: e.tensor_tensor(
                        fval.ap()[:, q * 128:(q + 1) * 128], h3[:, c, 1:129],
                        fsc.ap()[:, 384:512], OP.mult))
                emit_round(fval.ap()[:, :], fint.ap()[:, :], ff.ap()[:, :], fd.ap()[:, :],
                           m23[:, grp * 4:(grp + 1) * 4, bc:bc + 128])
            if last_strip:
                for grp in range(8):
                    for q in range(4):
                        c = grp * 4 + q
                        p.op("v", lambda e, c=c, q=q, grp=grp, h3=h3: e.tensor_tensor(
                            fval.ap()[:, grp * 4 + q:grp * 4 + q + 1], h3[:, c, 129:130],
                            fsc.ap()[:, 1000:1001], OP.mult))
                emit_round(fval.ap()[:, 0:32], fint.ap()[:, 0:32], ff.ap()[:, 0:32],
                           fd.ap()[:, 0:32],
                           m23[:, 0:32, bc + 128:bc + 129].rearrange("a c o -> a (c o)"))
            if s == 15 and DBG == 3:
                dump_dbg([(0, hbuf.ap()[:, 0:2048]), (1, fsc.ap()[:, 0:512])])
            burst["bc"] = bc + n
            if (s % 2 == 1 and s < 14) or s >= 14:
                mm2_flush()

    def emit_attn_b():
        sem_wB = [p.new_sem(), p.new_sem()]
        p.wait("y", p.sem["t"].cur())
        p.wait("y", p.sem["v"].cur())
        p.wait("y", p.sem["s"].cur())
        p.wait("g", p.sem["t"].cur())
        xbar(0)
        p.op("v", lambda e: e.memset(dW.ap()[:, :], 0.0))
        p.op("v", lambda e: e.memset(dWavg.ap()[:, :], 0.0))
        for raw in (bk0, bq0, bv0):
            p.op("v", lambda e, raw=raw: e.memset(
                raw.ap()[:, :].rearrange("a (c w) -> a c w", w=264)[:, :, 0:3], 0.0))
        wfree = [None, None]

        def stream_mm(dscr_w, rhs_fn, out_fn, nbank0=0):
            for m in range(8):
                wb = wstrB.ap()[:, (m % 2) * 1024:(m % 2) * 1024 + 1024]
                if wfree[m % 2] is not None:
                    p.wait("y", wfree[m % 2])
                hvw = p.dma("y", wb.rearrange("a (k b) -> a k b", b=128),
                            dscr_w[m, :, :, :].rearrange("k a b -> a k b"),
                            sem=sem_wB[m % 2])
                p.wait("t", hvw)
                bi = nbank0 + (m % 2)
                b = bk.acquire(p, bi)
                hv = None
                for k in range(8):
                    hv = p.top(lambda e, b=b, wb=wb, k=k, m=m:
                               e.matmul(b[:, 0:WB], wb[:, k * 128:(k + 1) * 128],
                                        rhs_fn(k), start=(k == 0), stop=(k == 7)))
                wfree[m % 2] = hv
                bk.produced(p, bi, hv)
                bk.consume(p, bi, "v")
                hv2 = out_fn(m, b)
                bk.consumed(p, bi, hv2)

        def conv4(raw_t, dst_ap_fn, cw_t):
            r3 = raw_t.ap()[:, :].rearrange("a (c w) -> a c w", w=264)
            for c in range(NCH):
                dst = dst_ap_fn(c)
                p.op("v", lambda e, c=c, dst=dst, r3=r3, cw_t=cw_t: e.tensor_scalar(
                    dst, r3[:, c, 3:259], cw_t.ap()[:, c * 4 + 3:c * 4 + 4], None, OP.mult),
                    drain=False)
            for j, tap in ((2, 2), (1, 1), (0, 0)):
                p.drain("v")
                for c in range(NCH):
                    dst = dst_ap_fn(c)
                    p.op("v", lambda e, c=c, dst=dst, r3=r3, cw_t=cw_t, j=j:
                         e.scalar_tensor_tensor(
                             dst, r3[:, c, j:j + 256], cw_t.ap()[:, c * 4 + j:c * 4 + j + 1],
                             dst, OP.mult, OP.add), drain=False)
            p.drain("v")

        def headnorm(src_t, strip_off, is_rms):
            b2 = bk.acquire(p, 2)
            prev_mm = None
            for c in range(NCH):
                sl = src_t.ap()[:, c * WB:(c + 1) * WB]
                if prev_mm is not None:
                    p.wait("v", prev_mm)
                hv = p.op("v", lambda e, sl=sl: e.tensor_tensor(
                    smallB.ap()[:, 0:WB], sl, sl, OP.mult))
                p.wait("t", hv)
                prev_mm = hv = p.top(lambda e, c=c, b2=b2: e.matmul(
                    b2[:16, 0:WB], cS16.ap()[:, c * 16:(c + 1) * 16], smallB.ap()[:, 0:WB],
                    start=(c == 0), stop=(c == 7)))
            bk.produced(p, 2, hv)
            bk.consume(p, 2, "v")
            st_ = stripB.ap()[:16, strip_off:strip_off + WB]
            sc1 = smallB.ap()[:16, 256:257 + WB]
            if is_rms:
                hv = p.op("v", lambda e, b2=b2, st_=st_: e.tensor_scalar(
                    st_, b2[:16, 0:WB], 1.0 / HD, None, OP.mult))
                bk.consumed(p, 2, hv)
                emit_rsqrt(st_, st_, 1e-6, sc1[:, 0:WB], smallB.ap()[:16, 512:512 + WB],
                           smallB.ap()[:16, 768:768 + WB])
            else:
                hv = p.op("v", lambda e, b2=b2, st_=st_: e.tensor_copy(st_, b2[:16, 0:WB]))
                bk.consumed(p, 2, hv)
                emit_sqrt(st_, st_, sc1[:, 0:WB], smallB.ap()[:16, 512:512 + WB],
                          smallB.ap()[:16, 768:768 + WB])
                p.op("v", lambda e, st_=st_: e.tensor_scalar(st_, st_, 1e-6, None, OP.add))
                p.op("v", lambda e, st_=st_: e.reciprocal(st_, st_))
            hvr = p.sem["v"].cur()
            p.wait("t", hvr)
            for c in range(NCH):
                b3 = bk.acquire(p, 3)
                hv = p.top(lambda e, c=c, b3=b3, st_=st_: e.matmul(
                    b3[:, 0:WB], cS16T.ap()[:16, c * 128:(c + 1) * 128], st_,
                    start=True, stop=True))
                bk.produced(p, 3, hv)
                bk.consume(p, 3, "v")
                sl = src_t.ap()[:, c * WB:(c + 1) * WB]
                hv2 = p.op("v", lambda e, sl=sl, b3=b3: e.tensor_tensor(
                    sl, sl, b3[:, 0:WB], OP.mult))
                if is_rms:
                    hv2 = p.op("v", lambda e, sl=sl: e.tensor_scalar(
                        sl, sl, rmsw.ap()[:, 0:1], None, OP.mult))
                bk.consumed(p, 3, hv2)

        def delta_chunk(gi, ii, g):
            kf = bkh.ap()[:, g * WB + ii * 64: g * WB + ii * 64 + 64]
            qf = bfm.ap()[:, g * WB + ii * 64: g * WB + ii * 64 + 64]
            v3f = nbfm.ap()[:, g * WB + ii * 64: g * WB + ii * 64 + 64]
            bf = betab.ap()[:, g * WB + ii * 64: g * WB + ii * 64 + 64]
            nbf = bv0.ap()[:, :].rearrange("a (c w) -> a c w", w=264)[:, g,
                                                                     3 + ii * 64:3 + ii * 64 + 64]
            Wsl = dW.ap()[:, g * 64:(g + 1) * 64]
            Wasl = dWavg.ap()[:, g * 64:(g + 1) * 64]
            osl = lnB.ap()[:, g * WB + ii * 64: g * WB + ii * 64 + 64]
            tmt = cTMT.ap()[:64, gi * 64:(gi + 1) * 64]
            ab_ = cAB.ap()[:, gi * 64:(gi + 1) * 64]
            wcol = cWCOL.ap()[:64, gi:gi + 1]
            c1c = cC1.ap()[:, gi:gi + 1]
            c2c = cC2.ap()[:, gi:gi + 1]
            ID = cIDT.ap()

            def pairlow(bia, bib, lhsT_t, rhs_t):
                ba = bk.acquire(p, bia)
                hva = p.top(lambda e, ba=ba, lhsT_t=lhsT_t, rhs_t=rhs_t: e.matmul(
                    ba[0:64, 0:64], lhsT_t[0:64, :], rhs_t[0:64, :],
                    start=True, stop=True, tile_position=(0, 0)))
                bk.produced(p, bia, hva)
                bb2 = bk.acquire(p, bib)
                hvb = p.top(lambda e, bb2=bb2, lhsT_t=lhsT_t, rhs_t=rhs_t: e.matmul(
                    bb2[0:64, 0:64], lhsT_t[64:128, :], rhs_t[64:128, :],
                    start=True, stop=True, tile_position=(64, 0)))
                bk.produced(p, bib, hvb)
                return ba, bb2

            ba, bb2 = pairlow(0, 1, bf, kf)
            bk.consume(p, 0, "v")
            hv = p.op("v", lambda e, ba=ba: e.tensor_tensor(
                pwh("N1", 0), ba[0:64, 0:64], cNMASK.ap()[:64, 0:64], OP.mult), drain=False)
            bk.consumed(p, 0, hv)
            bk.consume(p, 1, "v")
            hv = p.op("v", lambda e, bb2=bb2: e.tensor_tensor(
                pwh("N1", 1), bb2[0:64, 0:64], cNMASK.ap()[:64, 64:128], OP.mult), drain=False)
            bk.consumed(p, 1, hv)
            ba, bb2 = pairlow(2, 3, kf, bf)
            bk.consume(p, 2, "v")
            hv = p.op("v", lambda e, ba=ba: e.tensor_tensor(
                pwh("NS1", 0), ba[0:64, 0:64], cNMASKT.ap()[:64, 0:64], OP.mult), drain=False)
            bk.consumed(p, 2, hv)
            bk.consume(p, 3, "v")
            hv = p.op("v", lambda e, bb2=bb2: e.tensor_tensor(
                pwh("NS1", 1), bb2[0:64, 0:64], cNMASKT.ap()[:64, 64:128], OP.mult), drain=False)
            bk.consumed(p, 3, hv)
            if DCUT is not None and DCUT <= 1: return
            b5 = bk.acquire(p, 5)
            hv = p.top(lambda e, b5=b5, kf=kf: e.transpose(b5[0:64, 0:128], kf, ID[:, :]))
            bk.produced(p, 5, hv)
            bk.consume(p, 5, "v")
            hv = p.op("v", lambda e, b5=b5: e.tensor_copy(dKts.ap()[:64, :], b5[0:64, 0:128]), drain=False)
            bk.consumed(p, 5, hv)
            if DCUT is not None and DCUT <= 2: return
            seq = [("N2", "NS1", "N1"), ("NS2", "N1", "NS1"), ("N4", "NS2", "N2"),
                   ("NS4", "N2", "NS2"), ("N8", "NS4", "N4"), ("NS8", "N4", "NS4"),
                   ("N16", "NS8", "N8"), ("NS16", "N8", "NS8"), ("N32", "NS16", "N16")]
            for idx, (dst, la, lb) in enumerate(seq):
                bi = 2 + (idx % 2)
                b = bk.acquire(p, bi)
                wv = p.sem["v"].cur()
                p.wait("t", wv)
                for h in range(2):
                    hv = p.top(lambda e, b=b, la=la, lb=lb, h=h: e.matmul(
                        b[0:64, h * 64:h * 64 + 64], pw(la)[:, h * 64:h * 64 + 64],
                        pw(lb)[:, h * 64:h * 64 + 64],
                        start=(h == 0), stop=True, tile_position=(0, 0),
                        skip_group_check=(h == 1)))
                bk.produced(p, bi, hv)
                bk.consume(p, bi, "v")
                hv = p.op("v", lambda e, b=b, dst=dst: e.tensor_copy(pw(dst), b[0:64, 0:128]), drain=False)
                bk.consumed(p, bi, hv)
            if DCUT is not None and DCUT <= 3: return
            p.op("v", lambda e: e.tensor_tensor(dP.ap()[:64, :], cI64S.ap()[:64, :],
                                                pw("NS1"), OP.add), drain=False)
            for k in (1, 2, 3, 4, 5):
                nm = f"N{2 ** k}"
                p.op("v", lambda e, nm=nm: e.tensor_tensor(pw("PI"), cI64S.ap()[:64, :],
                                                           pw(nm), OP.add), drain=False)
                b4 = bk.acquire(p, 4)
                wv = p.sem["v"].cur()
                p.wait("t", wv)
                for h in range(2):
                    hv = p.top(lambda e, b4=b4, h=h: e.matmul(
                        b4[0:64, h * 64:h * 64 + 64], pwh("PI", h),
                        dP.ap()[:64, h * 64:h * 64 + 64],
                        start=True, stop=True, tile_position=(0, 0),
                        skip_group_check=(h == 1)))
                bk.produced(p, 4, hv)
                bk.consume(p, 4, "v")
                hv = p.op("v", lambda e, b4=b4: e.tensor_copy(dP.ap()[:64, :], b4[0:64, 0:128]), drain=False)
                bk.consumed(p, 4, hv)
            if DCUT is not None and DCUT <= 4: return
            b0 = bk.acquire(p, 0)
            hv = p.top(lambda e, b0=b0, v3f=v3f: e.transpose(b0[0:64, 0:128], v3f, ID[:, :]))
            bk.produced(p, 0, hv)
            wv = p.sem["v"].cur()
            p.wait("t", wv)
            b1 = bk.acquire(p, 1)
            hv = p.top(lambda e, b1=b1, nbf=nbf, Wsl=Wsl: e.matmul(
                b1[0:64, 0:64], nbf[0:64, :], Wsl[0:64, :],
                start=True, stop=True, tile_position=(0, 0)))
            bk.produced(p, 1, hv)
            b5g = bk.acquire(p, 5)
            hv = p.top(lambda e, b5g=b5g, nbf=nbf, Wsl=Wsl: e.matmul(
                b5g[0:64, 0:64], nbf[64:128, :], Wsl[64:128, :],
                start=True, stop=True, tile_position=(64, 0)))
            bk.produced(p, 5, hv)
            bk.consume(p, 0, "v")
            hv = p.op("v", lambda e, b0=b0: e.tensor_copy(dG.ap()[:64, :], b0[0:64, 0:128]))
            bk.consumed(p, 0, hv)
            bk.consume(p, 1, "v")
            hv = p.op("v", lambda e, b1=b1: e.tensor_tensor(
                dG.ap()[:64, 0:64], dG.ap()[:64, 0:64], b1[0:64, 0:64], OP.add), drain=False)
            bk.consumed(p, 1, hv)
            bk.consume(p, 5, "v")
            hv = p.op("v", lambda e, b5g=b5g: e.tensor_tensor(
                dG.ap()[:64, 64:128], dG.ap()[:64, 64:128], b5g[0:64, 0:64], OP.add), drain=False)
            bk.consumed(p, 5, hv)
            if DCUT is not None and DCUT <= 5: return
            b1 = bk.acquire(p, 1)
            wv = p.sem["v"].cur()
            p.wait("t", wv)
            for h in range(2):
                hv = p.top(lambda e, b1=b1, h=h: e.matmul(
                    b1[0:64, h * 64:h * 64 + 64], dP.ap()[:64, h * 64:h * 64 + 64],
                    dG.ap()[:64, h * 64:h * 64 + 64],
                    start=(h == 0), stop=True, tile_position=(0, 0),
                    skip_group_check=(h == 1)))
            bk.produced(p, 1, hv)
            bk.consume(p, 1, "v")
            p.op("v", lambda e, b1=b1: e.tensor_copy(dU.ap()[:64, :], b1[0:64, 0:128]), drain=False)
            hv = p.op("v", lambda e, b1=b1, wcol=wcol: e.tensor_scalar(
                dUw.ap()[:64, :], b1[0:64, 0:128], wcol, None, OP.mult), drain=False)
            bk.consumed(p, 1, hv)
            if DCUT is not None and DCUT <= 6: return
            b2 = bk.acquire(p, 2)
            p.top(lambda e, b2=b2, Wasl=Wasl, qf=qf: e.matmul(
                b2[0:64, 0:64], Wasl[0:64, :], qf[0:64, :],
                start=True, stop=True, tile_position=(0, 0)))
            hv = p.top(lambda e, b2=b2, Wasl=Wasl, qf=qf: e.matmul(
                b2[64:128, 0:64], Wasl[64:128, :], qf[64:128, :],
                start=True, stop=True, tile_position=(64, 64), skip_group_check=True))
            bk.produced(p, 2, hv)
            b3 = bk.acquire(p, 3)
            p.top(lambda e, b3=b3, Wsl=Wsl, qf=qf: e.matmul(
                b3[0:64, 0:64], Wsl[0:64, :], qf[0:64, :],
                start=True, stop=True, tile_position=(0, 0)))
            hv = p.top(lambda e, b3=b3, Wsl=Wsl, qf=qf: e.matmul(
                b3[64:128, 0:64], Wsl[64:128, :], qf[64:128, :],
                start=True, stop=True, tile_position=(64, 64), skip_group_check=True))
            bk.produced(p, 3, hv)
            b4 = bk.acquire(p, 4)
            hv = p.top(lambda e, b4=b4, kf=kf, qf=qf: e.matmul(
                b4[0:64, 0:64], kf[0:64, :], qf[0:64, :],
                start=True, stop=True, tile_position=(0, 0)))
            bk.produced(p, 4, hv)
            b0 = bk.acquire(p, 0)
            hv = p.top(lambda e, b0=b0, kf=kf, qf=qf: e.matmul(
                b0[0:64, 0:64], kf[64:128, :], qf[64:128, :],
                start=True, stop=True, tile_position=(64, 0)))
            bk.produced(p, 0, hv)
            bk.consume(p, 4, "v")
            hv = p.op("v", lambda e, b4=b4, tmt=tmt: e.tensor_tensor(
                dT1.ap()[:64, 0:64], b4[0:64, 0:64], tmt, OP.mult), drain=False)
            bk.consumed(p, 4, hv)
            bk.consume(p, 0, "v")
            hv = p.op("v", lambda e, b0=b0, tmt=tmt: e.tensor_tensor(
                dT1.ap()[:64, 64:128], b0[0:64, 0:64], tmt, OP.mult), drain=False)
            bk.consumed(p, 0, hv)
            if DCUT is not None and DCUT <= 7: return
            b5o = bk.acquire(p, 5)
            wv = p.sem["v"].cur()
            p.wait("t", wv)
            p.top(lambda e, b5o=b5o: e.matmul(
                b5o[0:64, 0:64], dU.ap()[:64, 0:64], dT1.ap()[:64, 0:64],
                start=True, stop=True, tile_position=(0, 0)))
            hv = p.top(lambda e, b5o=b5o: e.matmul(
                b5o[64:128, 0:64], dU.ap()[:64, 64:128], dT1.ap()[:64, 64:128],
                start=True, stop=True, tile_position=(0, 64), skip_group_check=True))
            bk.produced(p, 5, hv)
            bk.consume(p, 3, "v")
            p.op("v", lambda e, b3=b3: e.tensor_copy(dT2.ap()[:, :], b3[:, 0:64]))
            bk.consume(p, 2, "v")
            hv = p.op("v", lambda e, b2=b2: e.scalar_tensor_tensor(
                dCmb.ap()[:, :], dT2.ap()[:, :], -1.0, b2[:, 0:64], OP.mult, OP.add))
            bk.consumed(p, 2, hv)
            p.op("v", lambda e, ab_=ab_: e.tensor_tensor(dCmb.ap()[:, :], dCmb.ap()[:, :],
                                                         ab_, OP.mult))
            hv = p.op("v", lambda e, b3=b3: e.tensor_tensor(
                dCmb.ap()[:, :], dCmb.ap()[:, :], b3[:, 0:64], OP.add))
            bk.consumed(p, 3, hv)
            bk.consume(p, 5, "v")
            hv = p.op("v", lambda e, b5o=b5o, osl=osl: e.tensor_tensor(
                osl, dCmb.ap()[:, :], b5o[:, 0:64], OP.add), drain=False)
            bk.consumed(p, 5, hv)
            if gi == 0 and g == 0:
                xbar(5)
            if gi == 0 and g == 0 and BAR == 6:
                p.wait("g", p.sem["v"].cur())
                p.wait("g", p.sem["s"].cur())
                semz = p.new_sem("bar6")
                hvz = p.dma("g", ddbg[0][:1, 0:1], dKts.ap()[:1, 0:1], slow=True, sem=semz)
                p.wait("v", hvz)
                p.wait("s", hvz)
                p.wait("t", hvz)
            if gi == 0 and g == 0 and BAR == 7:
                dump_dbg([(0, pw("N1")), (1, dP.ap()[:64, :]), (2, dG.ap()[:64, :]),
                          (3, dU.ap()[:64, :]), (4, osl), (5, dKts.ap()[:64, :])])
            if DBG == 4 and gi == DBGCH and g == 0:
                dump_dbg([(0, pw("N1")), (1, dP.ap()[:64, :]), (2, dG.ap()[:64, :]),
                          (3, dU.ap()[:64, :]), (4, osl), (5, dKts.ap()[:64, :])])
            if DBG == 6 and gi == DBGCH and g == 0:
                dump_dbg([(0, Wsl), (1, Wasl), (2, dU.ap()[:64, :]),
                          (3, dUw.ap()[:64, :]), (4, v3f), (5, bf)])
            if DCUT is not None and DCUT <= 8: return
            p.op("v", lambda e, Wsl=Wsl, c2c=c2c: e.tensor_scalar(
                dWmix.ap()[:, :], Wsl, c2c, None, OP.mult))
            p.op("v", lambda e, Wasl=Wasl, c1c=c1c: e.scalar_tensor_tensor(
                dWmix.ap()[:, :], Wasl, c1c, dWmix.ap()[:, :], OP.mult, OP.add))
            b1w = bk.acquire(p, 1)
            wv = p.sem["v"].cur()
            p.wait("t", wv)
            p.top(lambda e, b1w=b1w: e.matmul(
                b1w[0:64, 0:64], dKts.ap()[:64, 0:64], dU.ap()[:64, 0:64],
                start=True, stop=True, tile_position=(0, 0)))
            hv = p.top(lambda e, b1w=b1w: e.matmul(
                b1w[64:128, 0:64], dKts.ap()[:64, 64:128], dU.ap()[:64, 64:128],
                start=True, stop=True, tile_position=(0, 64), skip_group_check=True))
            bk.produced(p, 1, hv)
            b4w = bk.acquire(p, 4)
            p.top(lambda e, b4w=b4w: e.matmul(
                b4w[0:64, 0:64], dKts.ap()[:64, 0:64], dUw.ap()[:64, 0:64],
                start=True, stop=True, tile_position=(0, 0)))
            hv = p.top(lambda e, b4w=b4w: e.matmul(
                b4w[64:128, 0:64], dKts.ap()[:64, 64:128], dUw.ap()[:64, 64:128],
                start=True, stop=True, tile_position=(0, 64), skip_group_check=True))
            bk.produced(p, 4, hv)
            bk.consume(p, 1, "v")
            hv = p.op("v", lambda e, b1w=b1w, Wsl=Wsl: e.tensor_tensor(
                Wsl, Wsl, b1w[:, 0:64], OP.add), drain=False)
            bk.consumed(p, 1, hv)
            bk.consume(p, 4, "v")
            hv = p.op("v", lambda e, b4w=b4w, Wasl=Wasl: e.tensor_tensor(
                Wasl, dWmix.ap()[:, :], b4w[:, 0:64], OP.add), drain=False)
            bk.consumed(p, 4, hv)

        nstrip = T // WB if BSTRIPS is None else BSTRIPS
        for s in range(nstrip):
            t0 = s * WB
            p.wait("v", p.sem["t"].cur())
            p.wait("s", p.sem["t"].cur())
            ln_stats(t0, WB, mbB, sbB, smallB, stripB, bankbase=6)
            p.wait("v", p.sem["s"].cur())
            for c in range(NCH):
                ln = lnB.ap()[:, c * WB:(c + 1) * WB]
                p.op("v", lambda e, c=c, ln=ln, t0=t0: e.tensor_tensor(
                    ln, xc(c, t0, WB), mbB.ap()[:, :], OP.subtract), drain=False)
            p.drain("v")
            for c in range(NCH):
                ln = lnB.ap()[:, c * WB:(c + 1) * WB]
                p.op("v", lambda e, ln=ln: e.tensor_tensor(ln, ln, sbB.ap()[:, :], OP.mult),
                     drain=False)
            hv_ln = p.sem["v"].cur()
            p.wait("t", hv_ln)
            if BCUT is not None and BCUT <= 1: continue
            r3k = bk0.ap()[:, :].rearrange("a (c w) -> a c w", w=264)
            r3q = bq0.ap()[:, :].rearrange("a (c w) -> a c w", w=264)
            r3v = bv0.ap()[:, :].rearrange("a (c w) -> a c w", w=264)
            lsv = leadsv.ap()[:, 0:24].rearrange("a (c w) -> a c w", w=3)
            if s > 0:
                for r3 in (r3k, r3q):
                    p.op("v", lambda e, r3=r3: e.tensor_copy(r3[:, :, 0:3], r3[:, :, 256:259]))
                p.op("v", lambda e, lsv=lsv, r3v=r3v: e.tensor_copy(r3v[:, :, 0:3], lsv[:, :, :]))
            hv_lead = p.sem["v"].cur()

            def mk_out(r3, bias_t):
                def out_fn(m, b, r3=r3, bias_t=bias_t):
                    return p.op("v", lambda e, m=m, b=b, r3=r3, bias_t=bias_t:
                                e.tensor_scalar(r3[:, m, 3:259], b[:, 0:WB],
                                                bias_t.ap()[:, m:m + 1], None, OP.add))
                return out_fn
            lnf = lambda k: lnB.ap()[:, k * WB:(k + 1) * WB]
            p.wait("t", hv_lead)
            stream_mm(scr["k"], lnf, mk_out(r3k, bias_k), nbank0=0)
            stream_mm(scr["q"], lnf, mk_out(r3q, bias_q), nbank0=0)
            stream_mm(scr["v"], lnf, mk_out(r3v, bias_v), nbank0=0)
            if s == 0:
                xbar(1)
            if BCUT is not None and BCUT <= 2: continue
            b4 = bk.acquire(p, 4)
            hv = None
            for k in range(8):
                hv = p.top(lambda e, b4=b4, k=k: e.matmul(
                    b4[:16, 0:WB], cWbT.ap()[:, k * 16:(k + 1) * 16], lnf(k),
                    start=(k == 0), stop=(k == 7)))
            bk.produced(p, 4, hv)
            bk.consume(p, 4, "s")
            bstrip = stripB.ap()[:16, 0:WB]
            hv = p.op("s", lambda e, b4=b4, bstrip=bstrip: e.activation(
                bstrip, b4[:16, 0:WB], AF.Sigmoid, bias=bias_beta.ap()[:16, 0:1]))
            bk.consumed(p, 4, hv)
            p.wait("t", hv)
            for c in range(NCH):
                b3 = bk.acquire(p, 3)
                hv = p.top(lambda e, c=c, b3=b3, bstrip=bstrip: e.matmul(
                    b3[:, 0:WB], cS16T.ap()[:16, c * 128:(c + 1) * 128], bstrip,
                    start=True, stop=True))
                bk.produced(p, 3, hv)
                bk.consume(p, 3, "v")
                hv2 = p.op("v", lambda e, c=c, b3=b3: e.tensor_copy(
                    betab.ap()[:, c * WB:(c + 1) * WB], b3[:, 0:WB]))
                bk.consumed(p, 3, hv2)
            if BCUT is not None and BCUT <= 3: continue
            conv4(bk0, lambda c: bkh.ap()[:, c * WB:(c + 1) * WB], convk)
            conv4(bq0, lambda c: bfm.ap()[:, c * WB:(c + 1) * WB], convq)
            conv4(bv0, lambda c: lnB.ap()[:, c * WB:(c + 1) * WB], convv)
            if BCUT is not None and BCUT <= 4: continue
            headnorm(bkh, 256, is_rms=False)
            headnorm(bfm, 512, is_rms=False)
            if BCUT is not None and BCUT <= 5: continue
            hvv = p.sem["v"].cur()
            p.wait("s", hvv)
            for c in range(NCH):
                sl = lnB.ap()[:, c * WB:(c + 1) * WB]
                hv = p.op("s", lambda e, sl=sl: e.activation(sl, sl, AF.Silu))
            p.wait("v", p.sem["s"].cur())
            for c in range(NCH):
                p.op("v", lambda e, c=c: e.tensor_tensor(
                    nbfm.ap()[:, c * WB:(c + 1) * WB], betab.ap()[:, c * WB:(c + 1) * WB],
                    lnB.ap()[:, c * WB:(c + 1) * WB], OP.mult))
            p.op("v", lambda e, lsv=lsv, r3v=r3v: e.tensor_copy(lsv[:, :, :], r3v[:, :, 256:259]))
            for c in range(NCH):
                bb_ = betab.ap()[:, c * WB:(c + 1) * WB]
                p.op("v", lambda e, c=c, bb_=bb_: e.tensor_tensor(
                    bb_, bb_, bkh.ap()[:, c * WB:(c + 1) * WB], OP.mult))
                p.op("v", lambda e, c=c, bb_=bb_, r3v=r3v: e.tensor_scalar(
                    r3v[:, c, 3:259], bb_, -1.0, None, OP.mult))
            hv_pre = p.sem["v"].cur()
            p.wait("t", hv_pre)
            if BCUT is not None and BCUT <= 6: continue
            if DBG == 5 and s == DBGCH // 4:
                dump_dbg([(0, bkh.ap()[:, 0:2048]), (1, bfm.ap()[:, 0:2048]),
                          (2, betab.ap()[:, 0:2048]), (3, nbfm.ap()[:, 0:2048]),
                          (5, bk0.ap()[:, 0:2048])])
            for ii in range(4):
                gi = s * 4 + ii
                for g in range(NCH):
                    delta_chunk(gi, ii, g)
            if BCUT is not None and BCUT <= 7: continue
            if s == 0:
                xbar(3)
            headnorm(lnB, 768, is_rms=True)
            if BCUT is not None and BCUT <= 8: continue
            hv_o = p.sem["v"].cur()
            p.wait("t", hv_o)
            def out_res(m, b):
                hv = p.op("v", lambda e, m=m, b=b: e.tensor_scalar(
                    smallB.ap()[:, 0:WB], b[:, 0:WB], bias_out.ap()[:, m:m + 1], None, OP.add))
                p.op("v", lambda e, m=m, t0=t0: e.tensor_tensor(
                    xc(m, t0, WB), xc(m, t0, WB), smallB.ap()[:, 0:WB], OP.add))
                return hv
            stream_mm(scr["out"], lnf if False else (lambda k: lnB.ap()[:, k * WB:(k + 1) * WB]),
                      out_res, nbank0=0)
            if s == 0:
                xbar(4)

    if stage >= 2:
        emit_ffn()
    if stage >= 3:
        emit_attn_b()
    if stage >= 4:
        emit_ffn()

    p.wait("t", p.sem["v"].cur())
    p.wait("t", p.sem["s"].cur())
    sem_out = [p.new_sem("outs0"), p.new_sem("outs1")]
    out_free = [None, None]
    for j in range(16):
        buf = xld.ap()[:, (j % 2) * 1024:(j % 2) * 1024 + 1024]
        if out_free[j % 2] is not None:
            p.wait("s", out_free[j % 2])
        for half in range(2):
            tiles = [xc(half * 4 + c, j * 128, 128) for c in range(4)]
            def evac(b, half=half, buf=buf):
                hv2 = None
                for q in range(4):
                    c = half * 4 + q
                    hv2 = p.op("s", lambda e, c=c, q=q, b=b, buf=buf:
                               e.copy(buf[:, c * 128:(c + 1) * 128], b[:, q * 128:(q + 1) * 128]))
                return hv2
            transpose_group(tiles, 6 + half, evac, evac_engine="s")
        out_free[j % 2] = p.dma("s", dy[j * 128:(j + 1) * 128, :], buf, sem=sem_out[j % 2])
    for so in sem_out:
        p.wait("s", so.cur())
        p.wait("y", so.cur())
        p.wait("g", so.cur())

    with nc.Block() as block:
        p.emit(block)
    stack.close()
    return nc


def kernel(L0_attn_a_x, L0_attn_a_Wqkv, L0_attn_a_bqkv, L0_attn_a_Wproj, L0_attn_a_bproj,
           L0_attn_a_scale, L0_attn_b_Wk, L0_attn_b_bk, L0_attn_b_Wq, L0_attn_b_bq,
           L0_attn_b_Wv, L0_attn_b_bv, L0_attn_b_Wbeta, L0_attn_b_bbeta,
           L0_attn_b_conv_k, L0_attn_b_conv_q, L0_attn_b_conv_v, L0_attn_b_rms_w,
           L0_attn_b_Wout, L0_attn_b_bout, L0_ffn_w1, L0_ffn_conv_w, L0_ffn_conv_b,
           L0_ffn_w2):
    nc = build_program(STAGE)
    consts = build_constants()
    shared = {
        "aWqkv": np.ascontiguousarray(L0_attn_a_Wqkv), "abqkv": np.ascontiguousarray(L0_attn_a_bqkv),
        "aWproj": np.ascontiguousarray(L0_attn_a_Wproj), "abproj": np.ascontiguousarray(L0_attn_a_bproj),
        "ascale": np.reshape(np.asarray(L0_attn_a_scale, dtype=np.float32), [1]),
        "bWk": np.ascontiguousarray(L0_attn_b_Wk), "bbk": np.ascontiguousarray(L0_attn_b_bk),
        "bWq": np.ascontiguousarray(L0_attn_b_Wq), "bbq": np.ascontiguousarray(L0_attn_b_bq),
        "bWv": np.ascontiguousarray(L0_attn_b_Wv), "bbv": np.ascontiguousarray(L0_attn_b_bv),
        "bWbeta": np.ascontiguousarray(L0_attn_b_Wbeta), "bbbeta": np.ascontiguousarray(L0_attn_b_bbeta),
        "bconvk": np.ascontiguousarray(L0_attn_b_conv_k), "bconvq": np.ascontiguousarray(L0_attn_b_conv_q),
        "bconvv": np.ascontiguousarray(L0_attn_b_conv_v), "brms": np.ascontiguousarray(L0_attn_b_rms_w),
        "bWout": np.ascontiguousarray(L0_attn_b_Wout), "bbout": np.ascontiguousarray(L0_attn_b_bout),
        "fw1": np.ascontiguousarray(L0_ffn_w1),
        "fconvw": np.ascontiguousarray(np.reshape(L0_ffn_conv_w, (DH, 3))),
        "fconvb": np.ascontiguousarray(L0_ffn_conv_b),
        "fw2": np.ascontiguousarray(L0_ffn_w2),
    }
    for k, v in consts.items():
        if k == "IDTB":
            shared["c_" + k] = np.ascontiguousarray(v.astype(ml_dtypes.bfloat16))
        else:
            shared["c_" + k] = np.ascontiguousarray(v.astype(np.float32))
    xin = np.ascontiguousarray(np.asarray(L0_attn_a_x, dtype=np.float32))
    in_maps = []
    for b in range(8):
        m = dict(shared)
        m["x_in"] = np.ascontiguousarray(xin[b])
        in_maps.append(m)
    res = run_bass_kernel_spmd(nc, in_maps, core_ids=list(range(8)))
    out = np.stack([res.results[b]["y_out"] for b in range(8)], axis=0)
    kernel.last_results = res
    return out


# version 05 (final)
# speedup vs baseline: 1.1176x; 1.0475x over previous
import contextlib
import numpy as np
import ml_dtypes
import concourse.bass as bass
import concourse.mybir as mybir
from concourse.bass_utils import run_bass_kernel_spmd

F32 = mybir.dt.float32
F16 = mybir.dt.float16
BF16 = mybir.dt.bfloat16
I32 = mybir.dt.int32
AF = mybir.ActivationFunctionType
OP = mybir.AluOpType
AX = mybir.AxisListType

T = 2048
D = 1024
NCH = 8
DH = 4096
HEADS = 16
HD = 64
CC = 64
NDC = 32
LN_EPS = 1e-5
WA = 256
WB = 256
WF = 128

STAGE = 4
DBG = False
DBGCH = 0
BSTRIPS = None
BCUT = None
DCUT = None
BAR = None


def _hilo(a64):
    hi = a64.astype(np.float32)
    lo = (a64 - hi.astype(np.float64)).astype(np.float32)
    return hi, lo


def build_constants():
    c = {}
    c["IDT"] = np.eye(128, dtype=np.float32)
    c["IDTB"] = np.eye(128, dtype=np.float32)
    c["ONES"] = np.ones((128, 128), dtype=np.float32)
    s16 = np.zeros((128, NCH * 16), dtype=np.float32)
    s16t = np.zeros((16, NCH * 128), dtype=np.float32)
    for ch in range(NCH):
        for p in range(128):
            h = 2 * ch + (1 if p >= 64 else 0)
            s16[p, ch * 16 + h] = 1.0
            s16t[h, ch * 128 + p] = 1.0
    c["S16"] = s16
    c["S16T"] = s16t
    tril = np.tril(np.ones((CC, CC)), -1)
    c["NMASK"] = np.tile(-tril, (1, 2)).astype(np.float32)
    c["NMASKT"] = np.tile(-tril.T, (1, 2)).astype(np.float32)
    c["I64S"] = np.tile(np.eye(CC), (1, 2)).astype(np.float32)
    a_all = np.zeros((NDC, CC))
    w_all = np.zeros((NDC, CC))
    c1_all = np.zeros(NDC)
    c2_all = np.zeros(NDC)
    tmT_all = np.zeros((NDC, CC, CC))
    Cf = float(CC)
    for i in range(NDC):
        ts = float(i * CC)
        idx = np.arange(1, CC + 1, dtype=np.float64) + ts
        cs = np.cumsum(idx)
        weighted_idx = idx + idx.sum() - cs
        w_all[i] = weighted_idx / ((1 + ts + Cf) * (ts + Cf) / 2)
        c1_all[i] = (ts / (ts + Cf)) * ((1 + ts) / (1 + ts + Cf)) if i > 0 else 0.0
        c2_all[i] = (Cf / (ts + Cf)) * ((ts * 2 + 1 + Cf) / (1 + ts + Cf))
        tri0 = (1 + ts) * ts / 2
        a_all[i] = tri0 / (tri0 + cs)
        r = cs[:, None]
        j_idx = (cs - idx)[None, :]
        tmT_all[i] = np.where(r >= j_idx, (r - j_idx) / (tri0 + r), 0.0).T
    ab = np.tile(a_all.reshape(1, NDC * CC), (128, 1))
    c["AB_hi"], c["AB_lo"] = _hilo(ab)
    wcol = np.concatenate([w_all.T, w_all.T], axis=0)
    c["WCOL_hi"], c["WCOL_lo"] = _hilo(wcol)
    c["C1_hi"], c["C1_lo"] = _hilo(np.tile(c1_all[None, :], (128, 1)))
    c["C2_hi"], c["C2_lo"] = _hilo(np.tile(c2_all[None, :], (128, 1)))
    tmt = np.zeros((64, NDC * CC))
    for i in range(NDC):
        tmt[:, i * CC:(i + 1) * CC] = tmT_all[i]
    c["TMT_hi"], c["TMT_lo"] = _hilo(tmt)
    return c



class Sem:
    CAP = 1_000_000_000

    def __init__(self, nc, stack, name):
        self.nc, self.stack, self.name = nc, stack, name
        self.idx = 0
        self.h = stack.enter_context(nc.semaphore(f"{name}0"))
        self.n = 0

    def add(self, k):
        if self.n + k > self.CAP:
            self.idx += 1
            self.h = self.stack.enter_context(self.nc.semaphore(f"{self.name}{self.idx}"))
            self.n = 0
        self.n += k
        return (self.h, self.n)

    def cur(self):
        return (self.h, self.n)


class Prog:
    ENGS = ("t", "v", "s", "g", "y")

    def __init__(self, nc, stack):
        self.nc = nc
        self.stack = stack
        self.ops = {e: [] for e in self.ENGS}
        self.sem = {e: Sem(nc, stack, f"c{e}") for e in self.ENGS}
        self.dmas = {q: Sem(nc, stack, f"d{q}") for q in ("y", "s", "g")}
        self.waited = {e: {} for e in self.ENGS}
        self._nsem = [0]

    def new_sem(self, name=None):
        self._nsem[0] += 1
        return Sem(self.nc, self.stack, name or f"x{self._nsem[0]}")

    def wait(self, e, hv):
        if hv is None:
            return
        h, v = hv
        seen = self.waited[e].get(id(h), -1)
        if v <= seen:
            return
        self.waited[e][id(h)] = v
        self.ops[e].append(lambda eng, h=h, v=v: eng.wait_ge(h, v))

    def op(self, e, fn, drain=True):
        hv = self.sem[e].add(1)
        def run(eng, fn=fn, hv=hv, drain=drain):
            fn(eng).then_inc(hv[0], 1)
            if drain:
                eng.drain()
        self.ops[e].append(run)
        return hv

    def drain(self, e):
        self.ops[e].append(lambda eng: eng.drain())

    def top(self, fn):
        hv = self.sem["t"].add(1)
        self.ops["t"].append(lambda eng, fn=fn, hv=hv: fn(eng).then_inc(hv[0], 1))
        return hv

    def dma(self, q, dst, src, slow=False, sem=None):
        hv = (sem if sem is not None else self.dmas[q]).add(16)
        if slow:
            self.ops[q].append(lambda eng, d=dst, s=src, hv=hv: eng.dma_start(
                d, s, allow_slow_non_contiguous=True).then_inc(hv[0], 16))
        else:
            self.ops[q].append(lambda eng, d=dst, s=src, hv=hv: eng.dma_start(d, s).then_inc(hv[0], 16))
        return hv

    def emit(self, block):
        decs = {"t": block.tensor, "v": block.vector, "s": block.scalar,
                "g": block.gpsimd, "y": block.sync}
        for e in self.ENGS:
            ops = self.ops[e]
            @decs[e]
            def _(eng, ops=ops):
                for fn in ops:
                    fn(eng)


class Banks:
    def __init__(self, nc, stack):
        self.b = [stack.enter_context(nc.psum_tensor(f"bank{i}", [128, 512], F32))
                  for i in range(8)]
        self.cons = [[] for _ in range(8)]
        self.prod = [None] * 8

    def acquire(self, p, i):
        for hv in self.cons[i]:
            p.wait("t", hv)
        self.cons[i] = []
        return self.b[i]

    def produced(self, p, i, hv):
        self.prod[i] = hv

    def consume(self, p, i, e):
        if self.prod[i] is not None:
            p.wait(e, self.prod[i])

    def consumed(self, p, i, hv):
        self.cons[i].append(hv)



def build_program(stage=None):
    if stage is None:
        stage = STAGE
    stack = contextlib.ExitStack()
    nc = bass.Bass("TRN2", target_bir_lowering=False)

    dx = nc.dram_tensor("x_in", [T, D], F32, kind="ExternalInput").ap()
    dy = nc.dram_tensor("y_out", [T, D], F32, kind="ExternalOutput").ap()
    ddbg = [nc.dram_tensor(f"dbg{i}", [128, 2048], F32, kind="ExternalOutput").ap()
            for i in range(6)]
    dbg_done = [False]
    def dump_dbg(aps):
        if dbg_done[0]:
            return
        dbg_done[0] = True
        p.wait("g", p.sem["v"].cur())
        p.wait("g", p.sem["s"].cur())
        hv = None
        sem_dbg = p.new_sem("dbg")
        for i, ap in aps:
            n = int(ap.ap[-1][1])
            np_ = int(ap.ap[0][1])
            hv = p.dma("g", ddbg[i][:np_, 0:n], ap, sem=sem_dbg)
        p.wait("v", hv)
        p.wait("s", hv)
        p.wait("t", hv)

    din = {}
    for name, shape in [
        ("aWqkv", [3 * D, D]), ("abqkv", [3 * D]), ("aWproj", [D, D]), ("abproj", [D]),
        ("ascale", [1]),
        ("bWk", [D, D]), ("bbk", [D]), ("bWq", [D, D]), ("bbq", [D]),
        ("bWv", [D, D]), ("bbv", [D]), ("bWbeta", [HEADS, D]), ("bbbeta", [HEADS]),
        ("bconvk", [D, 4]), ("bconvq", [D, 4]), ("bconvv", [D, 4]),
        ("brms", [HD]), ("bWout", [D, D]), ("bbout", [D]),
        ("fw1", [DH, D]), ("fconvw", [DH, 3]), ("fconvb", [DH]), ("fw2", [D, DH]),
    ]:
        din[name] = nc.dram_tensor(name, shape, F32, kind="ExternalInput").ap()

    CONSTS = build_constants()
    dcon = {}
    for k, v in CONSTS.items():
        dt = BF16 if k == "IDTB" else F32
        if k == "IDTB":
            dcon[k] = nc.dram_tensor("c_" + k, list(v.shape), BF16, kind="ExternalInput").ap()
        else:
            dcon[k] = nc.dram_tensor("c_" + k, list(v.shape), F32, kind="ExternalInput").ap()

    scr = {}
    scr["qkv"] = nc.dram_tensor("s_qkv", [24, 8, 128, 128], F32, kind="Internal").ap()
    scr["proj"] = nc.dram_tensor("s_proj", [8, 8, 128, 128], F32, kind="Internal").ap()
    for w in ("k", "q", "v", "out"):
        scr[w] = nc.dram_tensor(f"s_{w}", [8, 8, 128, 128], F32, kind="Internal").ap()
    scr["t1"] = nc.dram_tensor("s_t1", [32, 128, 1024], BF16, kind="Internal").ap()
    scr["t2"] = nc.dram_tensor("s_t2", [32, 8, 128, 128], BF16, kind="Internal").ap()

    cur = [16512]
    offs = {}

    def P_(name, shape, dtype=F32, at=None, alias=None):
        save = None
        if alias is not None:
            save = cur[0]
            cur[0] = offs[alias]
        elif at is not None:
            cur[0] = at
        nbytes = int(np.prod(shape[1:])) * mybir.dt.size(dtype)
        off = (cur[0] + 31) // 32 * 32
        h = nc.alloc_sbuf_tensor_at(name, shape, dtype, offset=off)
        offs[name] = off
        cur[0] = off + nbytes
        if save is not None and save > cur[0]:
            cur[0] = save
        return h

    xfm = P_("xfm", [128, NCH * T])
    cIDT = P_("cIDT", [128, 128])
    cIDTB = P_("cIDTB", [128, 128], BF16)
    cONES = P_("cONES", [128, 128])
    cS16 = P_("cS16", [128, NCH * 16])
    cS16T = P_("cS16T", [16, NCH * 128])
    cNMASK = P_("cNMASK", [64, 2 * CC])
    cNMASKT = P_("cNMASKT", [64, 2 * CC])
    cI64S = P_("cI64S", [64, 2 * CC])
    cAB = P_("cAB", [128, NDC * CC])
    cTMT = P_("cTMT", [64, NDC * CC])
    cWCOL = P_("cWCOL", [128, NDC])
    cC1 = P_("cC1", [128, NDC])
    cC2 = P_("cC2", [128, NDC])
    bias_qkv = P_("b_qkv", [128, 24])
    bias_proj = P_("b_proj", [128, 8])
    bias_k = P_("b_k", [128, 8]); bias_q = P_("b_q", [128, 8])
    bias_v = P_("b_v", [128, 8]); bias_out = P_("b_out", [128, 8])
    bias_beta = P_("b_beta", [16, 1])
    convk = P_("convk", [128, NCH * 4]); convq = P_("convq", [128, NCH * 4])
    convv = P_("convv", [128, NCH * 4])
    fcw = P_("fcw", [128, 32 * 3]); fcb = P_("fcb", [128, 32])
    rmsw = P_("rmsw", [128, 1])
    scale_a = P_("scale_a", [128, 1])
    cw1t = P_("cw1t", [128, 1]); cw1r = P_("cw1r", [128, 1])
    cw2t = P_("cw2t", [128, 1]); cw2r = P_("cw2r", [128, 1])
    cWbT = P_("cWbT", [128, NCH * 16])
    carry = P_("carry", [128, NCH * 2])
    PERM_END = cur[0]
    AR = (PERM_END + 63) // 64 * 64
    assert AR < 110500, f"perm too big: {AR}"

    wnat = P_("wnat", [128, 2 * DH], at=AR)
    pstage = P_("pstage", [128, 8 * 128])
    pstageb = P_("pstageb", [128, 8 * 128], BF16)
    pval = P_("pval", [128, 512])
    pint = P_("pint", [128, 512], I32)
    pf = P_("pf", [128, 512])
    pd = P_("pd", [128, 512])
    xld = P_("xld", [128, 2 * 1024])
    lnA = P_("lnA", [128, NCH * WA], at=AR)
    qkvA = P_("qkvA", [128, 24 * WA])
    kvA = P_("kvA", [128, NCH * WA])
    kcA = P_("kcA", [128, NCH * WA])
    wstrA = P_("wstrA", [128, 2 * 1024])
    smallA = P_("smallA", [128, 1024])
    stripA = P_("stripA", [128, 1024])
    mbA = P_("mbA", [128, WA])
    sbA = P_("sbA", [128, WA])
    czA = P_("czA", [128, WA])
    aintA = P_("aintA", [128, 512], I32)
    t1T = P_("t1T", [128, 32 * 1024], BF16, at=AR)
    hbuf = P_("hbuf", [128, 32 * 131])
    hc8 = P_("hc8", [128, 8 * 132])
    hsave = P_("hsave", [128, 32])
    m1b = P_("m1b", [128, 8 * WF], BF16)
    m2b = P_("m2b", [128, 32 * 256], BF16)
    t2buf = P_("t2buf", [128, 8 * 128], BF16)
    fsc = P_("fsc", [128, 8 * 128])
    fstats = P_("fstats", [128, 1024], alias="hc8")
    fval = P_("fval", [128, 512]); fint = P_("fint", [128, 512], I32)
    ff = P_("ff", [128, 512]); fd = P_("fd", [128, 512])
    famx = P_("famx", [128, 8 * 129], alias="fval")
    fsm = P_("fsm", [128, 384])
    FF_END = cur[0]
    assert FF_END <= 229344, f"ffn map too big: {FF_END}"
    lnB = P_("lnB", [128, NCH * WB], at=AR)
    bk0 = P_("bk0", [128, NCH * 264])
    bq0 = P_("bq0", [128, NCH * 264])
    bv0 = P_("bv0", [128, NCH * 264])
    bkh = P_("bkh", [128, NCH * WB])
    bfm = P_("bfmq", [128, NCH * WB])
    betab = P_("betab", [128, NCH * WB])
    nbfm = P_("nbfm", [128, NCH * WB])
    wstrB = P_("wstrB", [128, 2 * 1024])
    smallB = P_("smallB", [128, 1024])
    stripB = P_("stripB", [128, 1024])
    mbB = P_("mbB", [128, WB]); sbB = P_("sbB", [128, WB])
    leadsv = P_("leadsv", [128, NCH * 3 * 3])
    dW = P_("dW", [128, NCH * CC])
    dWavg = P_("dWavg", [128, NCH * CC])
    dPows = P_("dPows", [64, 12 * 2 * CC])
    dP = P_("dP", [64, 2 * CC])
    dG = P_("dG", [64, 2 * CC])
    dU = P_("dU", [64, 2 * CC])
    dUw = P_("dUw", [64, 2 * CC])
    dWmix = P_("dWmix", [128, CC])
    dT1 = P_("dT1", [64, 2 * CC])
    dT2 = P_("dT2", [128, CC])
    dKts = P_("dKts", [64, 2 * CC])
    dCmb = P_("dCmb", [128, CC])
    AB_END = cur[0]
    assert AB_END <= 229344, f"attn_b map too big: {AB_END}"

    p = Prog(nc, stack)
    bk = Banks(nc, stack)

    def xbar(pid):
        if BAR != pid:
            return
        hv_t, hv_v, hv_s = p.sem["t"].cur(), p.sem["v"].cur(), p.sem["s"].cur()
        for e in ("t", "v", "s", "g", "y"):
            p.wait(e, hv_t)
            p.wait(e, hv_v)
            p.wait(e, hv_s)

    def xc(c, t0, w):
        return xfm.ap()[:, c * T + t0: c * T + t0 + w]

    POW = {"N1": 0, "NS1": 1, "N2": 2, "NS2": 3, "N4": 4, "NS4": 5, "N8": 6,
           "NS8": 7, "N16": 8, "NS16": 9, "N32": 10, "PI": 11}

    def pw(nm):
        i = POW[nm]
        return dPows.ap()[:64, i * 2 * CC:(i + 1) * 2 * CC]

    def pwh(nm, h):
        i = POW[nm]
        return dPows.ap()[:64, i * 2 * CC + h * CC:i * 2 * CC + (h + 1) * CC]

    def transpose_group(tiles, bank_i, evac_fn, evac_engine="s", ident=None):
        idn = ident if ident is not None else cIDT.ap()[:, :]
        b = bk.acquire(p, bank_i)
        hv = None
        for j, src in enumerate(tiles):
            hv = p.top(lambda e, s=src, d=b[:, j * 128:(j + 1) * 128], idn=idn:
                       e.transpose(d, s, idn))
        bk.produced(p, bank_i, hv)
        bk.consume(p, bank_i, evac_engine)
        hv2 = evac_fn(b)
        bk.consumed(p, bank_i, hv2)
        return hv2

    def emit_round(val_ap, int_ap, f_ap, d_ap, out_ap):
        p.wait("s", p.sem["v"].cur())
        p.op("s", lambda e: e.copy(int_ap, val_ap))
        hv = p.op("s", lambda e: e.copy(f_ap, int_ap))
        p.wait("v", hv)
        p.op("v", lambda e: e.tensor_tensor(d_ap, val_ap, f_ap, OP.subtract))
        p.op("v", lambda e: e.scalar_tensor_tensor(f_ap, d_ap, 0.5, f_ap, OP.is_ge, OP.add))
        p.op("v", lambda e: e.tensor_scalar(d_ap, d_ap, -0.5, None, OP.is_le))
        return p.op("v", lambda e: e.tensor_tensor(out_ap, f_ap, d_ap, OP.subtract))

    def emit_rsqrt(out_ap, in_ap, eps, t1_ap, t2_ap, ln_ap):
        p.op("v", lambda e: e.tensor_scalar(t1_ap, in_ap, float(eps), None, OP.add))
        p.wait("s", p.sem["v"].cur())
        p.op("s", lambda e: e.activation(ln_ap, t1_ap, AF.Ln))
        hv = p.op("s", lambda e: e.activation(out_ap, ln_ap, AF.Exp, scale=-0.5))
        p.wait("v", hv)
        p.op("v", lambda e: e.tensor_tensor(t2_ap, out_ap, out_ap, OP.mult))
        p.op("v", lambda e: e.tensor_tensor(t2_ap, t2_ap, t1_ap, OP.mult))
        p.op("v", lambda e: e.tensor_scalar(t2_ap, t2_ap, -0.5, 1.5, OP.mult, OP.add))
        return p.op("v", lambda e: e.tensor_tensor(out_ap, out_ap, t2_ap, OP.mult))

    def emit_sqrt(out_ap, in_ap, t1_ap, t2_ap, ln_ap):
        p.op("v", lambda e: e.tensor_copy(t2_ap, in_ap))
        p.wait("s", p.sem["v"].cur())
        p.op("s", lambda e: e.activation(ln_ap, t2_ap, AF.Ln))
        hv = p.op("s", lambda e: e.activation(out_ap, ln_ap, AF.Exp, scale=0.5))
        p.wait("v", hv)
        p.op("v", lambda e: e.reciprocal(t1_ap, out_ap))
        p.op("v", lambda e: e.tensor_tensor(t1_ap, t1_ap, t2_ap, OP.mult))
        p.op("v", lambda e: e.tensor_tensor(t2_ap, out_ap, t1_ap, OP.add))
        return p.op("v", lambda e: e.tensor_scalar(out_ap, t2_ap, 0.5, None, OP.mult))

    sem_const = p.new_sem("const")
    for name, dst in [("IDT", cIDT), ("IDTB", cIDTB), ("ONES", cONES), ("S16", cS16),
                      ("S16T", cS16T), ("NMASK", cNMASK), ("NMASKT", cNMASKT),
                      ("I64S", cI64S)]:
        sh = dst.shape
        p.dma("y", dst.ap()[:sh[0], :], dcon[name], sem=sem_const)
    prev_add = None
    for hi, lo, dst in [("AB_hi", "AB_lo", cAB), ("WCOL_hi", "WCOL_lo", cWCOL),
                        ("C1_hi", "C1_lo", cC1), ("C2_hi", "C2_lo", cC2),
                        ("TMT_hi", "TMT_lo", cTMT)]:
        w = dst.shape[1]
        np_ = dst.shape[0]
        if prev_add is not None:
            p.wait("y", prev_add)
        hv0 = p.dma("y", dst.ap()[:, :], dcon[hi], sem=sem_const)
        hv = p.dma("y", wnat.ap()[:np_, :w], dcon[lo], sem=sem_const)
        p.wait("v", hv)
        prev_add = p.op("v", lambda e, d=dst, w=w, np_=np_: e.tensor_tensor(
            d.ap()[:, :], d.ap()[:, :], wnat.ap()[:np_, :w], OP.add))
    p.wait("y", prev_add)
    for nm, dst in [("abqkv", bias_qkv), ("abproj", bias_proj), ("bbk", bias_k),
                    ("bbq", bias_q), ("bbv", bias_v), ("bbout", bias_out)]:
        p.dma("y", dst.ap()[:, :], din[nm].rearrange("(c p) -> p c", p=128), slow=True, sem=sem_const)
    p.dma("y", bias_beta.ap()[:16, :], din["bbbeta"].rearrange("(h o) -> h o", o=1), slow=True, sem=sem_const)
    for nm, dst in [("bconvk", convk), ("bconvq", convq), ("bconvv", convv)]:
        p.dma("y", dst.ap()[:, :].rearrange("p (c k) -> p c k", k=4),
              din[nm].rearrange("(c p) k -> p c k", p=128), slow=True, sem=sem_const)
    p.dma("y", fcw.ap()[:, :].rearrange("p (c k) -> p c k", k=3),
          din["fconvw"].rearrange("(c p) k -> p c k", p=128), slow=True, sem=sem_const)
    p.dma("y", fcb.ap()[:, :], din["fconvb"].rearrange("(c p) -> p c", p=128), slow=True, sem=sem_const)
    p.dma("y", rmsw.ap()[:64, :], din["brms"].rearrange("(d o) -> d o", o=1), slow=True, sem=sem_const)
    p.dma("y", rmsw.ap()[64:128, :], din["brms"].rearrange("(d o) -> d o", o=1), slow=True, sem=sem_const)
    p.dma("y", scale_a.ap()[:1, :], din["ascale"].rearrange("(a b) -> a b", b=1), slow=True, sem=sem_const)
    p.wait("t", sem_const.cur())
    b0 = bk.acquire(p, 0)
    hv = p.top(lambda e: e.matmul(b0[:, 0:1], cONES.ap()[:1, :], scale_a.ap()[:1, :],
                                  start=True, stop=True))
    bk.produced(p, 0, hv)
    bk.consume(p, 0, "v")
    hv = p.op("v", lambda e: e.tensor_copy(scale_a.ap()[:, :], b0[:, 0:1]))
    bk.consumed(p, 0, hv)

    sem_wld = [p.new_sem("wld0"), p.new_sem("wld1")]
    sem_pst = p.new_sem("pst")
    last_tr = [None, None]
    last_wr = [None, None]

    def stage_fp32_weight(dsrc, dscr, MCH):
        for m in range(MCH):
            buf = wnat.ap()[:, (m % 2) * D:(m % 2) * D + D]
            if last_tr[m % 2] is not None:
                p.wait("y", last_tr[m % 2])
            if last_wr[m % 2] is not None:
                p.wait("y", last_wr[m % 2])
            hv = p.dma("y", buf, dsrc[m * 128:(m + 1) * 128, :], sem=sem_wld[m % 2])
            last_wr[m % 2] = hv
            p.wait("t", hv)
            for half in range(2):
                tiles = [buf[:, (half * 4 + k) * 128:(half * 4 + k + 1) * 128] for k in range(4)]
                def evac(b, half=half):
                    return p.op("v", lambda e: e.tensor_copy(
                        pstage.ap()[:, half * 512:(half + 1) * 512], b[:, 0:512]))
                transpose_group(tiles, 2 + half, evac, evac_engine="v")
            last_tr[m % 2] = p.sem["t"].cur()
            hvv = p.sem["v"].cur()
            p.wait("s", hvv)
            hv = p.dma("s", dscr[m, :, :, :].rearrange("k a b -> a k b"),
                       pstage.ap()[:, :].rearrange("a (k b) -> a k b", b=128), sem=sem_pst)
            p.wait("v", hv)

    stage_fp32_weight(din["aWqkv"], scr["qkv"], 24)
    stage_fp32_weight(din["aWproj"], scr["proj"], 8)
    stage_fp32_weight(din["bWk"], scr["k"], 8)
    stage_fp32_weight(din["bWq"], scr["q"], 8)
    stage_fp32_weight(din["bWv"], scr["v"], 8)
    stage_fp32_weight(din["bWout"], scr["out"], 8)

    if last_tr[0] is not None:
        p.wait("y", last_tr[0])
    if last_wr[0] is not None:
        p.wait("y", last_wr[0])
    hv = p.dma("y", wnat.ap()[:16, 0:D], din["bWbeta"], sem=sem_wld[0])
    last_wr[0] = hv
    p.wait("t", hv)
    b4 = bk.acquire(p, 4)
    hv = None
    for k in range(8):
        hv = p.top(lambda e, k=k: e.transpose(b4[:, k * 16:(k + 1) * 16],
                                              wnat.ap()[:16, k * 128:(k + 1) * 128],
                                              cIDT.ap()[:16, :16]))
    bk.produced(p, 4, hv)
    last_tr[0] = hv
    last_tr[1] = hv
    bk.consume(p, 4, "v")
    hv = p.op("v", lambda e: e.tensor_copy(cWbT.ap()[:, :], b4[:, 0:NCH * 16]))
    bk.consumed(p, 4, hv)

    def ffn_stage2(dsrc, MCH, KW, cwt, cwr, dscr, kmajor):
        sem_f = [p.new_sem(), p.new_sem()]
        sem_fs = p.new_sem()
        for g in (last_tr[0], last_tr[1], last_wr[0], last_wr[1]):
            if g is not None:
                p.wait("y", g)
        p.op("v", lambda e: e.memset(pval.ap()[:, 0:1], 0.0))
        prev_red = [last_tr[0], last_tr[1]]
        for m in range(MCH):
            buf = wnat.ap()[:, (m % 2) * KW:(m % 2) * KW + KW]
            if prev_red[m % 2] is not None:
                p.wait("y", prev_red[m % 2])
            if last_wr[m % 2] is not None:
                p.wait("y", last_wr[m % 2])
            hv = p.dma("y", buf, dsrc[m * 128:(m + 1) * 128, :], sem=sem_f[m % 2])
            last_wr[m % 2] = hv
            p.wait("v", hv)
            p.op("v", lambda e, buf=buf: e.tensor_reduce(
                pval.ap()[:, 1:2], buf, AX.X, OP.add, apply_absolute_value=True))
            hvv = p.op("v", lambda e: e.tensor_tensor(pval.ap()[:, 0:1], pval.ap()[:, 0:1],
                                                      pval.ap()[:, 1:2], OP.add))
            if m + 1 < MCH:
                p.wait("y", hvv)
        p.wait("t", p.sem["v"].cur())
        b5 = bk.acquire(p, 5)
        hv = p.top(lambda e: e.matmul(b5[:1, 0:1], cONES.ap()[:, 0:1], pval.ap()[:, 0:1],
                                      start=True, stop=True))
        bk.produced(p, 5, hv)
        bk.consume(p, 5, "v")
        p.op("v", lambda e: e.tensor_scalar(pval.ap()[:1, 2:3], b5[:1, 0:1],
                                            1.0 / (MCH * 128 * KW), 1e-5, OP.mult, OP.max))
        hv = p.op("v", lambda e: e.reciprocal(pval.ap()[:1, 3:4], pval.ap()[:1, 2:3]))
        bk.consumed(p, 5, hv)
        p.wait("t", hv)
        b5 = bk.acquire(p, 5)
        hv = p.top(lambda e: e.matmul(b5[:, 0:2], cONES.ap()[:1, :], pval.ap()[:1, 2:4],
                                      start=True, stop=True))
        bk.produced(p, 5, hv)
        bk.consume(p, 5, "v")
        p.op("v", lambda e: e.tensor_copy(cwt.ap()[:, :], b5[:, 0:1]))
        hv = p.op("v", lambda e: e.tensor_copy(cwr.ap()[:, :], b5[:, 1:2]))
        bk.consumed(p, 5, hv)
        last = None
        hvv_last = p.sem["v"].cur()
        qgate = [hvv_last, hvv_last]
        for m in range(MCH):
            buf = wnat.ap()[:, (m % 2) * KW:(m % 2) * KW + KW]
            if qgate[m % 2] is not None:
                p.wait("y", qgate[m % 2])
            if last_wr[m % 2] is not None:
                p.wait("y", last_wr[m % 2])
            hv = p.dma("y", buf, dsrc[m * 128:(m + 1) * 128, :], sem=sem_f[m % 2])
            last_wr[m % 2] = hv
            p.wait("v", hv)
            p.wait("s", hv)
            for grp in range(KW // 512):
                g0 = grp * 512
                qgate[m % 2] = p.op("v", lambda e, buf=buf, g0=g0: e.tensor_scalar(
                    pval.ap()[:, :], buf[:, g0:g0 + 512], cwr.ap()[:, 0:1], None, OP.mult))
                emit_round(pval.ap()[:, :], pint.ap()[:, :], pf.ap()[:, :], pd.ap()[:, :],
                           pf.ap()[:, :])
                p.op("v", lambda e, g0=g0: e.tensor_scalar(
                    pstage.ap()[:, (g0 % 1024):(g0 % 1024) + 512],
                    pf.ap()[:, :], 1.0, -1.0, OP.min, OP.max))
                if (g0 % 1024) == 512 or KW == 512:
                    blk = g0 // 1024
                    hvq = p.sem["v"].cur()
                    p.wait("t", hvq)
                    for half in range(2):
                        tiles = [pstage.ap()[:, (half * 4 + k) * 128:(half * 4 + k + 1) * 128]
                                 for k in range(4)]
                        def evac(b, half=half):
                            return p.op("v", lambda e, half=half: e.tensor_copy(
                                pstageb.ap()[:, half * 512:(half + 1) * 512], b[:, 0:512]))
                        transpose_group(tiles, 2 + half, evac, evac_engine="v")
                    hvv = p.sem["v"].cur()
                    p.wait("g", hvv)
                    if last is not None:
                        p.wait("g", last)
                    if kmajor:
                        kbase = blk * 8
                        last = p.dma("g", dscr[kbase:kbase + 8, m, :, :].rearrange("k a b -> a k b"),
                                     pstageb.ap()[:, :].rearrange("a (k b) -> a k b", b=128),
                                     sem=sem_fs)
                    else:
                        last = p.dma("g", dscr[m, :, :], pstageb.ap()[:, :], sem=sem_fs)
                    p.wait("v", last)
        last_tr[0] = qgate[0]
        last_tr[1] = qgate[1]
        return

    if stage >= 2:
        ffn_stage2(din["fw1"], 32, D, cw1t, cw1r, scr["t1"], kmajor=False)
        ffn_stage2(din["fw2"], 8, DH, cw2t, cw2r, scr["t2"], kmajor=True)

    sem_xld = [p.new_sem(), p.new_sem()]
    xld_free = [None, None]
    for j in range(16):
        buf = xld.ap()[:, (j % 2) * 1024:(j % 2) * 1024 + 1024]
        if xld_free[j % 2] is not None:
            p.wait("y", xld_free[j % 2])
        hv = p.dma("y", buf, dx[j * 128:(j + 1) * 128, :], sem=sem_xld[j % 2])
        p.wait("t", hv)
        for half in range(2):
            tiles = [buf[:, (half * 4 + c) * 128:(half * 4 + c + 1) * 128] for c in range(4)]
            def evac(b, half=half, j=j):
                hv2 = None
                for q in range(4):
                    c = half * 4 + q
                    hv2 = p.op("s", lambda e, c=c, q=q, b=b:
                               e.copy(xfm.ap()[:, c * T + j * 128: c * T + (j + 1) * 128],
                                      b[:, q * 128:(q + 1) * 128]))
                return hv2
            transpose_group(tiles, 6 + half, evac, evac_engine="s")
        xld_free[j % 2] = p.sem["t"].cur()

    def ln_stats(t0, W, mb_t, sb_t, small_t, strip_t, bankbase=6, amax_dst=None):
        ntile = W // 128
        for j in range(ntile):
            tj = t0 + j * 128
            xt = strip_t.ap()[:, 0:1024]
            for half in range(2):
                tiles = [xc(half * 4 + c, tj, 128) for c in range(4)]
                def evac(b, half=half, xt=xt):
                    hv2 = None
                    for q in range(4):
                        c = half * 4 + q
                        hv2 = p.op("s", lambda e, c=c, q=q, b=b, xt=xt:
                                   e.copy(xt[:, c * 128:(c + 1) * 128],
                                          b[:, q * 128:(q + 1) * 128]))
                    return hv2
                transpose_group(tiles, bankbase + half, evac, evac_engine="s")
            hvs = p.sem["s"].cur()
            p.wait("v", hvs)
            sm = small_t.ap()
            p.op("v", lambda e, xt=xt, sm=sm: e.tensor_reduce(sm[:, 0:1], xt[:, 0:D],
                                                              AX.X, OP.add))
            p.op("v", lambda e, sm=sm: e.tensor_scalar(sm[:, 0:1], sm[:, 0:1], 1.0 / D,
                                                       None, OP.mult))
            if amax_dst is not None:
                p.op("v", lambda e, xt=xt, sm=sm: e.tensor_scalar(
                    xt[:, 0:D], xt[:, 0:D], sm[:, 0:1], None, OP.subtract))
                hvr = p.op("v", lambda e, xt=xt, j=j: e.tensor_reduce(
                    amax_dst[:, j:j + 1], xt[:, 0:D], AX.X, OP.max, apply_absolute_value=True))
                p.wait("s", hvr)
                hv = p.op("s", lambda e, xt=xt: e.activation(xt[:, 0:D], xt[:, 0:D], AF.Square))
                p.wait("v", hv)
                p.op("v", lambda e, xt=xt, sm=sm: e.tensor_reduce(sm[:, 1:2], xt[:, 0:D],
                                                                  AX.X, OP.add))
                p.op("v", lambda e, sm=sm: e.tensor_scalar(sm[:, 1:2], sm[:, 1:2], 1.0 / D,
                                                           None, OP.mult))
            else:
                p.wait("s", p.sem["v"].cur())
                hv = p.op("s", lambda e, xt=xt: e.activation(xt[:, 0:D], xt[:, 0:D], AF.Square))
                p.wait("v", hv)
                p.op("v", lambda e, xt=xt, sm=sm: e.tensor_reduce(sm[:, 1:2], xt[:, 0:D],
                                                                  AX.X, OP.add))
                p.op("v", lambda e, sm=sm: e.tensor_scalar(sm[:, 1:2], sm[:, 1:2], 1.0 / D,
                                                           None, OP.mult))
                p.op("v", lambda e, sm=sm: e.tensor_tensor(sm[:, 2:3], sm[:, 0:1],
                                                           sm[:, 0:1], OP.mult))
                p.op("v", lambda e, sm=sm: e.tensor_tensor(sm[:, 1:2], sm[:, 1:2],
                                                           sm[:, 2:3], OP.subtract))
            emit_rsqrt(sm[:, 1:2], sm[:, 1:2], LN_EPS, sm[:, 2:3], sm[:, 3:4], sm[:, 4:5])
            p.op("v", lambda e, sm=sm: e.tensor_scalar(sm[:, 128:256], cONES.ap()[:, :],
                                                       sm[:, 0:1], None, OP.mult))
            hv = p.op("v", lambda e, sm=sm: e.tensor_scalar(sm[:, 256:384], cONES.ap()[:, :],
                                                            sm[:, 1:2], None, OP.mult))
            p.wait("t", hv)
            bb = bk.acquire(p, bankbase)
            p.top(lambda e, sm=sm: e.transpose(bb[:, 0:128], sm[:, 128:256], cIDT.ap()[:, :]))
            hv = p.top(lambda e, sm=sm: e.transpose(bb[:, 128:256], sm[:, 256:384],
                                                    cIDT.ap()[:, :]))
            bk.produced(p, bankbase, hv)
            bk.consume(p, bankbase, "s")
            p.op("s", lambda e, j=j: e.copy(mb_t.ap()[:, j * 128:(j + 1) * 128], bb[:, 0:128]))
            hv = p.op("s", lambda e, j=j: e.copy(sb_t.ap()[:, j * 128:(j + 1) * 128],
                                                 bb[:, 128:256]))
            bk.consumed(p, bankbase, hv)

    sem_wsA = [p.new_sem(), p.new_sem()]

    def emit_attn_a():
        p.op("v", lambda e: e.memset(carry.ap()[:, :], 0.0))
        p.op("v", lambda e: e.memset(czA.ap()[:, :], 0.0))
        p.wait("t", p.sem["s"].cur())
        p.wait("t", p.sem["v"].cur())
        p.wait("y", p.sem["t"].cur())
        p.wait("y", p.sem["s"].cur())
        p.wait("g", p.sem["t"].cur())
        wfree = [None, None]
        for s in range(T // WA):
            t0 = s * WA
            ln_stats(t0, WA, mbA, sbA, smallA, stripA, bankbase=6)
            p.wait("v", p.sem["s"].cur())
            for c in range(NCH):
                ln = lnA.ap()[:, c * WA:(c + 1) * WA]
                p.op("v", lambda e, c=c, ln=ln, t0=t0: e.tensor_tensor(
                    ln, xc(c, t0, WA), mbA.ap()[:, :], OP.subtract), drain=False)
            p.drain("v")
            for c in range(NCH):
                ln = lnA.ap()[:, c * WA:(c + 1) * WA]
                p.op("v", lambda e, ln=ln: e.tensor_tensor(ln, ln, sbA.ap()[:, :], OP.mult),
                     drain=False)
            hv_ln = p.sem["v"].cur()
            p.wait("t", hv_ln)
            if s == 0 and DBG == 1:
                dump_dbg([(0, lnA.ap()[:, 0:2048]), (1, mbA.ap()[:, 0:WA]),
                          (2, sbA.ap()[:, 0:WA]), (3, xfm.ap()[:, 0:2048])])
            for m in range(24):
                wb = wstrA.ap()[:, (m % 2) * 1024:(m % 2) * 1024 + 1024]
                if wfree[m % 2] is not None:
                    p.wait("y", wfree[m % 2])
                hvw = p.dma("y", wb.rearrange("a (k b) -> a k b", b=128),
                            scr["qkv"][m, :, :, :].rearrange("k a b -> a k b"),
                            sem=sem_wsA[m % 2])
                p.wait("t", hvw)
                bi = m % 2
                b = bk.acquire(p, bi)
                hv = None
                for k in range(8):
                    hv = p.top(lambda e, b=b, wb=wb, k=k:
                               e.matmul(b[:, 0:WA], wb[:, k * 128:(k + 1) * 128],
                                        lnA.ap()[:, k * WA:(k + 1) * WA],
                                        start=(k == 0), stop=(k == 7)))
                wfree[m % 2] = hv
                bk.produced(p, bi, hv)
                bk.consume(p, bi, "v")
                hv = p.op("v", lambda e, m=m, b=b:
                          e.tensor_scalar(qkvA.ap()[:, m * WA:(m + 1) * WA], b[:, 0:WA],
                                          bias_qkv.ap()[:, m:m + 1], None, OP.add),
                          drain=(m == 23))
                bk.consumed(p, bi, hv)
            for m in range(16):
                u = qkvA.ap()[:, m * WA:(m + 1) * WA]
                mn = smallA.ap()[:, 0:WA]
                mx = smallA.ap()[:, WA:2 * WA]
                p.op("v", lambda e, u=u, mn=mn: e.tensor_scalar(mn, u, 0.0, None, OP.min))
                hv = p.op("v", lambda e, u=u, mx=mx: e.tensor_scalar(mx, u, 0.0, None, OP.max))
                p.wait("s", hv)
                hv = p.op("s", lambda e, mn=mn: e.activation(mn, mn, AF.Exp,
                                                             scale=scale_a.ap()[:, 0:1]))
                p.wait("v", hv)
                hv = p.op("v", lambda e, u=u, mn=mn, mx=mx:
                          e.scalar_tensor_tensor(u, mx, scale_a.ap()[:, 0:1], mn,
                                                 OP.mult, OP.add))
                p.wait("s", hv)
            for c in range(NCH):
                kq = qkvA.ap()[:, (8 + c) * WA:(9 + c) * WA]
                vv = qkvA.ap()[:, (16 + c) * WA:(17 + c) * WA]
                p.op("v", lambda e, c=c, kq=kq, vv=vv:
                     e.tensor_tensor(kvA.ap()[:, c * WA:(c + 1) * WA], kq, vv, OP.mult))
            for c in range(NCH):
                kq = qkvA.ap()[:, (8 + c) * WA:(9 + c) * WA]
                kc = kcA.ap()[:, c * WA:(c + 1) * WA]
                p.op("v", lambda e, c=c: e.tensor_copy(czA.ap()[:, 0:1],
                                                       carry.ap()[:, 2 * c:2 * c + 1]))
                p.op("v", lambda e, kq=kq, kc=kc:
                     e.tensor_tensor_scan(kc, kq, czA.ap()[:, 0:WA], 0.0, OP.add, OP.add))
                p.op("v", lambda e, c=c, kc=kc:
                     e.tensor_copy(carry.ap()[:, 2 * c:2 * c + 1], kc[:, WA - 1:WA]))
                kv = kvA.ap()[:, c * WA:(c + 1) * WA]
                p.op("v", lambda e, c=c: e.tensor_copy(czA.ap()[:, 0:1],
                                                       carry.ap()[:, 2 * c + 1:2 * c + 2]))
                p.op("v", lambda e, kv=kv:
                     e.tensor_tensor_scan(kv, kv, czA.ap()[:, 0:WA], 0.0, OP.add, OP.add))
                p.op("v", lambda e, c=c, kv=kv:
                     e.tensor_copy(carry.ap()[:, 2 * c + 1:2 * c + 2], kv[:, WA - 1:WA]))
            b2 = bk.acquire(p, 2)
            prev_mm = None
            for c in range(NCH):
                qq = qkvA.ap()[:, c * WA:(c + 1) * WA]
                kc = kcA.ap()[:, c * WA:(c + 1) * WA]
                if prev_mm is not None:
                    p.wait("v", prev_mm)
                hv = p.op("v", lambda e, qq=qq, kc=kc:
                          e.tensor_tensor(smallA.ap()[:, 0:WA], qq, kc, OP.mult))
                p.wait("t", hv)
                prev_mm = hv = p.top(lambda e, c=c, b2=b2: e.matmul(
                    b2[:16, 0:WA], cS16.ap()[:, c * 16:(c + 1) * 16], smallA.ap()[:, 0:WA],
                    start=(c == 0), stop=(c == 7)))
            bk.produced(p, 2, hv)
            bk.consume(p, 2, "v")
            den = stripA.ap()[:16, 0:WA]
            p.op("v", lambda e, b2=b2: e.tensor_scalar(den, b2[:16, 0:WA], 1e-3, None, OP.max))
            hv = p.op("v", lambda e: e.reciprocal(den, den))
            bk.consumed(p, 2, hv)
            p.wait("t", hv)
            for c in range(NCH):
                b3 = bk.acquire(p, 3)
                hv = p.top(lambda e, c=c, b3=b3: e.matmul(
                    b3[:, 0:WA], cS16T.ap()[:16, c * 128:(c + 1) * 128], den,
                    start=True, stop=True))
                bk.produced(p, 3, hv)
                qq = qkvA.ap()[:, c * WA:(c + 1) * WA]
                kv = kvA.ap()[:, c * WA:(c + 1) * WA]
                yy = lnA.ap()[:, c * WA:(c + 1) * WA]
                p.op("v", lambda e, qq=qq, kv=kv, yy=yy: e.tensor_tensor(yy, qq, kv, OP.mult))
                bk.consume(p, 3, "v")
                hv = p.op("v", lambda e, yy=yy, b3=b3: e.tensor_tensor(yy, yy, b3[:, 0:WA],
                                                                       OP.mult))
                bk.consumed(p, 3, hv)
            hv_y = p.sem["v"].cur()
            p.wait("t", hv_y)
            if s == 0 and DBG == 2:
                dump_dbg([(0, lnA.ap()[:, 0:2048]), (1, qkvA.ap()[:, 0:2048]),
                          (2, qkvA.ap()[:, 8 * WA:8 * WA + 2048]),
                          (3, kcA.ap()[:, 0:2048]), (4, kvA.ap()[:, 0:2048]),
                          (5, stripA.ap()[:, 0:1024])])
            for m in range(8):
                wb = wstrA.ap()[:, (m % 2) * 1024:(m % 2) * 1024 + 1024]
                if wfree[m % 2] is not None:
                    p.wait("y", wfree[m % 2])
                hvw = p.dma("y", wb.rearrange("a (k b) -> a k b", b=128),
                            scr["proj"][m, :, :, :].rearrange("k a b -> a k b"),
                            sem=sem_wsA[m % 2])
                p.wait("t", hvw)
                bi = 4 + (m % 2)
                b = bk.acquire(p, bi)
                hv = None
                for k in range(8):
                    hv = p.top(lambda e, b=b, wb=wb, k=k:
                               e.matmul(b[:, 0:WA], wb[:, k * 128:(k + 1) * 128],
                                        lnA.ap()[:, k * WA:(k + 1) * WA],
                                        start=(k == 0), stop=(k == 7)))
                wfree[m % 2] = hv
                bk.produced(p, bi, hv)
                bk.consume(p, bi, "v")
                hv = p.op("v", lambda e, m=m, b=b: e.tensor_scalar(
                    smallA.ap()[:, 0:WA], b[:, 0:WA], bias_proj.ap()[:, m:m + 1], None, OP.add))
                bk.consumed(p, bi, hv)
                p.op("v", lambda e, m=m, t0=t0: e.tensor_tensor(
                    xc(m, t0, WA), xc(m, t0, WA), smallA.ap()[:, 0:WA], OP.add))

    if stage >= 1:
        emit_attn_a()

    def ln_stats_ffn(t0, sm):
        xt = fstats.ap()[:, 0:1024]
        for half in range(2):
            tiles = [xc(half * 4 + c, t0, 128) for c in range(4)]
            def evac(b, half=half):
                hv2 = None
                for q in range(4):
                    c = half * 4 + q
                    hv2 = p.op("s", lambda e, c=c, q=q, b=b:
                               e.copy(xt[:, c * 128:(c + 1) * 128], b[:, q * 128:(q + 1) * 128]))
                return hv2
            transpose_group(tiles, 6 + half, evac, evac_engine="s")
        p.wait("v", p.sem["s"].cur())
        p.op("v", lambda e: e.tensor_reduce(sm[:, 0:1], xt[:, 0:D], AX.X, OP.add))
        p.op("v", lambda e: e.tensor_scalar(sm[:, 0:1], sm[:, 0:1], 1.0 / D, None, OP.mult))
        p.op("v", lambda e: e.tensor_scalar(xt[:, 0:D], xt[:, 0:D], sm[:, 0:1], None,
                                            OP.subtract))
        hvr = p.op("v", lambda e: e.tensor_reduce(sm[:, 5:6], xt[:, 0:D], AX.X, OP.max,
                                            apply_absolute_value=True))
        p.wait("s", hvr)
        hv = p.op("s", lambda e: e.activation(xt[:, 0:D], xt[:, 0:D], AF.Square))
        p.wait("v", hv)
        p.op("v", lambda e: e.tensor_reduce(sm[:, 1:2], xt[:, 0:D], AX.X, OP.add))
        p.op("v", lambda e: e.tensor_scalar(sm[:, 1:2], sm[:, 1:2], 1.0 / D, None, OP.mult))
        emit_rsqrt(sm[:, 1:2], sm[:, 1:2], LN_EPS, sm[:, 2:3], sm[:, 3:4], sm[:, 4:5])

    bcast_gate = [None]

    def bcast_col(src_col_ap, dst_ap, sm, bank=6):
        if bcast_gate[0] is not None:
            p.wait("v", bcast_gate[0])
        hv = p.op("v", lambda e: e.tensor_scalar(sm[:, 128:256], cONES.ap()[:, :],
                                                 src_col_ap, None, OP.mult))
        p.wait("t", hv)
        bb = bk.acquire(p, bank)
        hv = p.top(lambda e, bb=bb: e.transpose(bb[:, 0:128], sm[:, 128:256], cIDT.ap()[:, :]))
        bcast_gate[0] = hv
        bk.produced(p, bank, hv)
        bk.consume(p, bank, "s")
        hv = p.op("s", lambda e, bb=bb, dst_ap=dst_ap: e.copy(dst_ap, bb[:, 0:128]))
        bk.consumed(p, bank, hv)
        return hv

    def emit_ffn():
        sem_t1 = p.new_sem()
        sem_t2 = p.new_sem()
        p.wait("y", p.sem["t"].cur())
        p.wait("y", p.sem["v"].cur())
        p.wait("y", p.sem["s"].cur())
        p.wait("g", p.sem["t"].cur())
        hv_t1 = p.dma("y", t1T.ap()[:, :].rearrange("a (m w) -> a m w", w=1024),
                      scr["t1"].rearrange("m a w -> a m w"), sem=sem_t1)
        p.wait("t", hv_t1)
        h3 = hbuf.ap()[:, :].rearrange("a (c w) -> a c w", w=131)
        m23 = m2b.ap()[:, :].rearrange("a (c w) -> a c w", w=256)
        p.op("v", lambda e: e.memset(hsave.ap()[:, :], 0.0))
        p.op("v", lambda e, h3=h3: e.memset(h3[:, :, 0:2], 0.0))
        p.op("v", lambda e, h3=h3: e.memset(h3[:, :, 130:131], 0.0))
        sm = fsm.ap()
        burst = {"tb0": -1, "bc": 0, "last_mm": None}

        def mm2_flush():
            if burst["bc"] == 0:
                return
            tb0 = burst["tb0"]
            sc = 1 if tb0 < 0 else 0
            ncol = burst["bc"] - sc
            out0 = tb0 + sc
            p.wait("t", p.sem["v"].cur())
            bs = [bk.acquire(p, i) for i in range(8)]
            hv = None
            t2gate = None
            for k in range(32):
                if t2gate is not None:
                    p.wait("g", t2gate)
                hvw = p.dma("g", t2buf.ap()[:, :].rearrange("a (m b) -> a m b", b=128),
                            scr["t2"][k, :, :, :].rearrange("m a b -> a m b"), sem=sem_t2)
                p.wait("t", hvw)
                for m in range(8):
                    b = bs[m]
                    hv = p.top(lambda e, b=b, m=m, k=k, ncol=ncol, sc=sc:
                               e.matmul(b[:, 0:ncol],
                                        t2buf.ap()[:, m * 128:(m + 1) * 128],
                                        m2b.ap()[:, k * 256 + sc:k * 256 + sc + ncol],
                                        start=(k == 0), stop=(k == 31)))
                t2gate = hv
            for i in range(8):
                bk.produced(p, i, hv)
            for m in range(8):
                b = bs[m]
                bk.consume(p, m, "v")
                p.op("v", lambda e, b=b, ncol=ncol, sc=sc:
                     e.tensor_tensor(fval.ap()[:, 0:ncol], b[:, 0:ncol],
                                     fsc.ap()[:, 512 + sc:512 + sc + ncol], OP.mult))
                hv2 = p.op("v", lambda e, m=m, out0=out0, ncol=ncol:
                           e.tensor_tensor(xc(m, out0, ncol), xc(m, out0, ncol),
                                           fval.ap()[:, 0:ncol], OP.add))
                bk.consumed(p, m, hv2)
            burst["last_mm"] = hv
            burst["bc"] = 0
            burst["tb0"] = out0 + ncol

        for s in range(T // WF):
            t0 = s * WF
            last_strip = (s == T // WF - 1)
            n = 129 if last_strip else 128
            if s > 0:
                p.op("v", lambda e, h3=h3: e.tensor_copy(
                    h3[:, :, 0:1], hsave.ap()[:, :].rearrange("a (c o) -> a c o", o=1)))
                p.op("v", lambda e, h3=h3: e.tensor_copy(h3[:, :, 1:2], h3[:, :, 129:130]))
            ln_stats_ffn(t0, sm)
            p.op("v", lambda e: e.tensor_tensor(sm[:, 6:7], sm[:, 5:6], sm[:, 1:2], OP.mult))
            p.op("v", lambda e: e.tensor_scalar(sm[:, 6:7], sm[:, 6:7], 1e-5, None, OP.max))
            p.op("v", lambda e: e.reciprocal(sm[:, 7:8], sm[:, 6:7]))
            p.op("v", lambda e: e.tensor_scalar(sm[:, 7:8], sm[:, 7:8], 127.0, None, OP.mult))
            p.op("v", lambda e: e.tensor_tensor(sm[:, 8:9], sm[:, 1:2], sm[:, 7:8], OP.mult))
            p.op("v", lambda e: e.tensor_scalar(sm[:, 9:10], sm[:, 6:7], 1.0 / 127.0,
                                                None, OP.mult))
            p.op("v", lambda e: e.tensor_tensor(sm[:, 9:10], sm[:, 9:10], cw1t.ap()[:, 0:1],
                                                OP.mult))
            bcast_col(sm[:, 0:1], fsc.ap()[:, 0:128], sm)
            bcast_col(sm[:, 8:9], fsc.ap()[:, 128:256], sm)
            bcast_col(sm[:, 9:10], fsc.ap()[:, 256:384], sm)
            p.wait("v", p.sem["s"].cur())
            if burst["last_mm"] is not None:
                p.wait("v", burst["last_mm"])
                p.wait("s", burst["last_mm"])
            if burst.get("last_mm1") is not None:
                p.wait("v", burst["last_mm1"])
                p.wait("s", burst["last_mm1"])
            for c in range(NCH):
                dst = fval.ap()[:, (c % 4) * 128:(c % 4) * 128 + 128]
                p.op("v", lambda e, c=c, dst=dst, t0=t0: e.tensor_tensor(
                    dst, xc(c, t0, WF), fsc.ap()[:, 0:128], OP.subtract))
                p.op("v", lambda e, dst=dst: e.tensor_tensor(
                    dst, dst, fsc.ap()[:, 128:256], OP.mult))
                if c % 4 == 3:
                    emit_round(fval.ap()[:, :], fint.ap()[:, :], ff.ap()[:, :],
                               fd.ap()[:, :],
                               m1b.ap()[:, (c - 3) * 128:(c + 1) * 128])
            hv_m1 = p.sem["v"].cur()
            p.wait("t", hv_m1)
            for m in range(32):
                bi = 4 + (m % 2)
                b = bk.acquire(p, bi)
                hv = None
                for k in range(8):
                    hv = p.top(lambda e, b=b, m=m, k=k: e.matmul(
                        b[:, 0:WF], t1T.ap()[:, m * 1024 + k * 128: m * 1024 + (k + 1) * 128],
                        m1b.ap()[:, k * WF:(k + 1) * WF], start=(k == 0), stop=(k == 7)))
                burst["last_mm1"] = hv
                bk.produced(p, bi, hv)
                bk.consume(p, bi, "v")
                hv = p.op("v", lambda e, b=b, m=m, h3=h3: e.tensor_tensor(
                    h3[:, m, 2:130], b[:, 0:WF], fsc.ap()[:, 256:384], OP.mult),
                    drain=(m == 31))
                bk.consumed(p, bi, hv)
            p.op("v", lambda e, h3=h3: e.tensor_copy(
                hsave.ap()[:, :].rearrange("a (c o) -> a c o", o=1), h3[:, :, 128:129]))
            hcv = hc8.ap()[:, :].rearrange("a (c w) -> a c w", w=132)
            for grp in range(4):
                for q in range(8):
                    m = grp * 8 + q
                    p.op("v", lambda e, m=m, q=q, n=n, h3=h3, hcv=hcv: e.tensor_scalar(
                        hcv[:, q, 0:n], h3[:, m, 1:1 + n], fcw.ap()[:, m * 3 + 1:m * 3 + 2],
                        fcb.ap()[:, m:m + 1], OP.mult, OP.add), drain=False)
                p.drain("v")
                for q in range(8):
                    m = grp * 8 + q
                    p.op("v", lambda e, m=m, q=q, n=n, h3=h3, hcv=hcv:
                         e.scalar_tensor_tensor(
                        hcv[:, q, 0:n], h3[:, m, 0:n], fcw.ap()[:, m * 3:m * 3 + 1],
                        hcv[:, q, 0:n], OP.mult, OP.add), drain=False)
                p.drain("v")
                for q in range(8):
                    m = grp * 8 + q
                    p.op("v", lambda e, m=m, q=q, n=n, h3=h3, hcv=hcv:
                         e.scalar_tensor_tensor(
                        hcv[:, q, 0:n], h3[:, m, 2:2 + n], fcw.ap()[:, m * 3 + 2:m * 3 + 3],
                        hcv[:, q, 0:n], OP.mult, OP.add), drain=False)
                hvv = p.sem["v"].cur()
                p.wait("s", hvv)
                hv = p.op("s", lambda e, grp=grp, n=n, h3=h3, hcv=hcv: e.activation(
                    h3[:, grp * 8:(grp + 1) * 8, 1:1 + n], hcv[:, 0:8, 0:n], AF.Silu))
                p.wait("v", hv)
                p.wait("g", hv)
            amx = hcv
            hperm = hbuf.ap()[:, :].rearrange("a (c w) -> a w c", w=131)
            p.op("v", lambda e, hperm=hperm, amx=amx, n=n: e.tensor_reduce(
                amx[:, 0, 0:n], hperm[:, 1:1 + n, :], AX.X, OP.max,
                apply_absolute_value=True))
            hv = p.sem["v"].cur()
            p.wait("t", hv)
            bb = bk.acquire(p, 6)
            hv = p.top(lambda e, bb=bb, amx=amx: e.transpose(bb[:, 0:128], amx[:, 0, 0:128],
                                                             cIDT.ap()[:, :]))
            bk.produced(p, 6, hv)
            bk.consume(p, 6, "v")
            hv2 = p.op("v", lambda e, bb=bb: e.tensor_reduce(
                sm[:, 20:21], bb[:, 0:128], AX.X, OP.max, apply_absolute_value=True))
            bk.consumed(p, 6, hv2)
            if last_strip:
                bb = bk.acquire(p, 6)
                hv = p.top(lambda e, bb=bb, amx=amx: e.transpose(
                    bb[:1, 0:128], amx[:, 0:1, 128:129], cIDT.ap()[:, :]))
                bk.produced(p, 6, hv)
                bk.consume(p, 6, "v")
                hv2 = p.op("v", lambda e, bb=bb: e.tensor_reduce(
                    sm[:1, 24:25], bb[:1, 0:128], AX.X, OP.max, apply_absolute_value=True))
                bk.consumed(p, 6, hv2)
                p.op("v", lambda e: e.tensor_scalar(sm[:1, 24:25], sm[:1, 24:25], 1e-5,
                                                    None, OP.max))
                p.op("v", lambda e: e.reciprocal(sm[:1, 25:26], sm[:1, 24:25]))
                p.op("v", lambda e: e.tensor_scalar(sm[:1, 25:26], sm[:1, 25:26], 127.0,
                                                    None, OP.mult))
                p.op("v", lambda e: e.tensor_scalar(sm[:1, 26:27], sm[:1, 24:25],
                                                    1.0 / 127.0, None, OP.mult))
                p.op("v", lambda e: e.tensor_tensor(sm[:1, 26:27], sm[:1, 26:27],
                                                    cw2t.ap()[:1, 0:1], OP.mult))
            p.op("v", lambda e: e.tensor_scalar(sm[:, 20:21], sm[:, 20:21], 1e-5, None, OP.max))
            p.op("v", lambda e: e.reciprocal(sm[:, 21:22], sm[:, 20:21]))
            p.op("v", lambda e: e.tensor_scalar(sm[:, 21:22], sm[:, 21:22], 127.0, None,
                                                OP.mult))
            p.op("v", lambda e: e.tensor_scalar(sm[:, 22:23], sm[:, 20:21], 1.0 / 127.0,
                                                None, OP.mult))
            p.op("v", lambda e: e.tensor_tensor(sm[:, 22:23], sm[:, 22:23], cw2t.ap()[:, 0:1],
                                                OP.mult))
            bcast_col(sm[:, 21:22], fsc.ap()[:, 384:512], sm)
            bc = burst["bc"]
            bcast_col(sm[:, 22:23], fsc.ap()[:, 512 + bc:512 + bc + 128], sm)
            if last_strip:
                p.wait("t", p.sem["v"].cur())
                bb = bk.acquire(p, 6)
                p.top(lambda e, bb=bb: e.matmul(bb[:, 0:1], cONES.ap()[:1, :], sm[:1, 25:26],
                                                start=True, stop=True))
                hv = p.top(lambda e, bb=bb: e.matmul(bb[:, 1:2], cONES.ap()[:1, :],
                                                     sm[:1, 26:27], start=True, stop=True,
                                                     skip_group_check=True))
                bk.produced(p, 6, hv)
                bk.consume(p, 6, "s")
                p.op("s", lambda e, bb=bb: e.copy(fsc.ap()[:, 1000:1001], bb[:, 0:1]))
                hv = p.op("s", lambda e, bb=bb, bc=bc: e.copy(
                    fsc.ap()[:, 512 + bc + 128:512 + bc + 129], bb[:, 1:2]))
                bk.consumed(p, 6, hv)
            p.wait("v", p.sem["s"].cur())
            for grp in range(8):
                for q in range(4):
                    c = grp * 4 + q
                    p.op("v", lambda e, c=c, q=q, h3=h3: e.tensor_tensor(
                        fval.ap()[:, q * 128:(q + 1) * 128], h3[:, c, 1:129],
                        fsc.ap()[:, 384:512], OP.mult))
                emit_round(fval.ap()[:, :], fint.ap()[:, :], ff.ap()[:, :], fd.ap()[:, :],
                           m23[:, grp * 4:(grp + 1) * 4, bc:bc + 128])
            if last_strip:
                for grp in range(8):
                    for q in range(4):
                        c = grp * 4 + q
                        p.op("v", lambda e, c=c, q=q, grp=grp, h3=h3: e.tensor_tensor(
                            fval.ap()[:, grp * 4 + q:grp * 4 + q + 1], h3[:, c, 129:130],
                            fsc.ap()[:, 1000:1001], OP.mult))
                emit_round(fval.ap()[:, 0:32], fint.ap()[:, 0:32], ff.ap()[:, 0:32],
                           fd.ap()[:, 0:32],
                           m23[:, 0:32, bc + 128:bc + 129].rearrange("a c o -> a (c o)"))
            if s == 15 and DBG == 3:
                dump_dbg([(0, hbuf.ap()[:, 0:2048]), (1, fsc.ap()[:, 0:512])])
            burst["bc"] = bc + n
            if (s % 2 == 1 and s < 14) or s >= 14:
                mm2_flush()

    def emit_attn_b():
        sem_wB = [p.new_sem(), p.new_sem()]
        p.wait("y", p.sem["t"].cur())
        p.wait("y", p.sem["v"].cur())
        p.wait("y", p.sem["s"].cur())
        p.wait("g", p.sem["t"].cur())
        xbar(0)
        p.op("v", lambda e: e.memset(dW.ap()[:, :], 0.0))
        p.op("v", lambda e: e.memset(dWavg.ap()[:, :], 0.0))
        for raw in (bk0, bq0, bv0):
            p.op("v", lambda e, raw=raw: e.memset(
                raw.ap()[:, :].rearrange("a (c w) -> a c w", w=264)[:, :, 0:3], 0.0))
        wfree = [None, None]

        def stream_mm(dscr_w, rhs_fn, out_fn, nbank0=0):
            for m in range(8):
                wb = wstrB.ap()[:, (m % 2) * 1024:(m % 2) * 1024 + 1024]
                if wfree[m % 2] is not None:
                    p.wait("y", wfree[m % 2])
                hvw = p.dma("y", wb.rearrange("a (k b) -> a k b", b=128),
                            dscr_w[m, :, :, :].rearrange("k a b -> a k b"),
                            sem=sem_wB[m % 2])
                p.wait("t", hvw)
                bi = nbank0 + (m % 2)
                b = bk.acquire(p, bi)
                hv = None
                for k in range(8):
                    hv = p.top(lambda e, b=b, wb=wb, k=k, m=m:
                               e.matmul(b[:, 0:WB], wb[:, k * 128:(k + 1) * 128],
                                        rhs_fn(k), start=(k == 0), stop=(k == 7)))
                wfree[m % 2] = hv
                bk.produced(p, bi, hv)
                bk.consume(p, bi, "v")
                hv2 = out_fn(m, b)
                bk.consumed(p, bi, hv2)

        def conv4(raw_t, dst_ap_fn, cw_t):
            r3 = raw_t.ap()[:, :].rearrange("a (c w) -> a c w", w=264)
            for c in range(NCH):
                dst = dst_ap_fn(c)
                p.op("v", lambda e, c=c, dst=dst, r3=r3, cw_t=cw_t: e.tensor_scalar(
                    dst, r3[:, c, 3:259], cw_t.ap()[:, c * 4 + 3:c * 4 + 4], None, OP.mult),
                    drain=False)
            for j, tap in ((2, 2), (1, 1), (0, 0)):
                p.drain("v")
                for c in range(NCH):
                    dst = dst_ap_fn(c)
                    p.op("v", lambda e, c=c, dst=dst, r3=r3, cw_t=cw_t, j=j:
                         e.scalar_tensor_tensor(
                             dst, r3[:, c, j:j + 256], cw_t.ap()[:, c * 4 + j:c * 4 + j + 1],
                             dst, OP.mult, OP.add), drain=False)
            p.drain("v")

        def headnorm(src_t, strip_off, is_rms):
            b2 = bk.acquire(p, 2)
            prev_mm = None
            for c in range(NCH):
                sl = src_t.ap()[:, c * WB:(c + 1) * WB]
                if prev_mm is not None:
                    p.wait("v", prev_mm)
                hv = p.op("v", lambda e, sl=sl: e.tensor_tensor(
                    smallB.ap()[:, 0:WB], sl, sl, OP.mult))
                p.wait("t", hv)
                prev_mm = hv = p.top(lambda e, c=c, b2=b2: e.matmul(
                    b2[:16, 0:WB], cS16.ap()[:, c * 16:(c + 1) * 16], smallB.ap()[:, 0:WB],
                    start=(c == 0), stop=(c == 7)))
            bk.produced(p, 2, hv)
            bk.consume(p, 2, "v")
            st_ = stripB.ap()[:16, strip_off:strip_off + WB]
            sc1 = smallB.ap()[:16, 256:257 + WB]
            if is_rms:
                hv = p.op("v", lambda e, b2=b2, st_=st_: e.tensor_scalar(
                    st_, b2[:16, 0:WB], 1.0 / HD, None, OP.mult))
                bk.consumed(p, 2, hv)
                emit_rsqrt(st_, st_, 1e-6, sc1[:, 0:WB], smallB.ap()[:16, 512:512 + WB],
                           smallB.ap()[:16, 768:768 + WB])
            else:
                hv = p.op("v", lambda e, b2=b2, st_=st_: e.tensor_copy(st_, b2[:16, 0:WB]))
                bk.consumed(p, 2, hv)
                emit_sqrt(st_, st_, sc1[:, 0:WB], smallB.ap()[:16, 512:512 + WB],
                          smallB.ap()[:16, 768:768 + WB])
                p.op("v", lambda e, st_=st_: e.tensor_scalar(st_, st_, 1e-6, None, OP.add))
                p.op("v", lambda e, st_=st_: e.reciprocal(st_, st_))
            hvr = p.sem["v"].cur()
            p.wait("t", hvr)
            for c in range(NCH):
                b3 = bk.acquire(p, 3)
                hv = p.top(lambda e, c=c, b3=b3, st_=st_: e.matmul(
                    b3[:, 0:WB], cS16T.ap()[:16, c * 128:(c + 1) * 128], st_,
                    start=True, stop=True))
                bk.produced(p, 3, hv)
                bk.consume(p, 3, "v")
                sl = src_t.ap()[:, c * WB:(c + 1) * WB]
                hv2 = p.op("v", lambda e, sl=sl, b3=b3: e.tensor_tensor(
                    sl, sl, b3[:, 0:WB], OP.mult))
                if is_rms:
                    hv2 = p.op("v", lambda e, sl=sl: e.tensor_scalar(
                        sl, sl, rmsw.ap()[:, 0:1], None, OP.mult))
                bk.consumed(p, 3, hv2)

        def delta_chunk(gi, ii, g):
            kf = bkh.ap()[:, g * WB + ii * 64: g * WB + ii * 64 + 64]
            qf = bfm.ap()[:, g * WB + ii * 64: g * WB + ii * 64 + 64]
            v3f = nbfm.ap()[:, g * WB + ii * 64: g * WB + ii * 64 + 64]
            bf = betab.ap()[:, g * WB + ii * 64: g * WB + ii * 64 + 64]
            nbf = bv0.ap()[:, :].rearrange("a (c w) -> a c w", w=264)[:, g,
                                                                     3 + ii * 64:3 + ii * 64 + 64]
            Wsl = dW.ap()[:, g * 64:(g + 1) * 64]
            Wasl = dWavg.ap()[:, g * 64:(g + 1) * 64]
            osl = lnB.ap()[:, g * WB + ii * 64: g * WB + ii * 64 + 64]
            tmt = cTMT.ap()[:64, gi * 64:(gi + 1) * 64]
            ab_ = cAB.ap()[:, gi * 64:(gi + 1) * 64]
            wcol = cWCOL.ap()[:64, gi:gi + 1]
            c1c = cC1.ap()[:, gi:gi + 1]
            c2c = cC2.ap()[:, gi:gi + 1]
            ID = cIDT.ap()

            def pairlow(bia, bib, lhsT_t, rhs_t):
                ba = bk.acquire(p, bia)
                hva = p.top(lambda e, ba=ba, lhsT_t=lhsT_t, rhs_t=rhs_t: e.matmul(
                    ba[0:64, 0:64], lhsT_t[0:64, :], rhs_t[0:64, :],
                    start=True, stop=True, tile_position=(0, 0)))
                bk.produced(p, bia, hva)
                bb2 = bk.acquire(p, bib)
                hvb = p.top(lambda e, bb2=bb2, lhsT_t=lhsT_t, rhs_t=rhs_t: e.matmul(
                    bb2[0:64, 0:64], lhsT_t[64:128, :], rhs_t[64:128, :],
                    start=True, stop=True, tile_position=(64, 0)))
                bk.produced(p, bib, hvb)
                return ba, bb2

            ba, bb2 = pairlow(0, 1, bf, kf)
            bk.consume(p, 0, "v")
            hv = p.op("v", lambda e, ba=ba: e.tensor_tensor(
                pwh("N1", 0), ba[0:64, 0:64], cNMASK.ap()[:64, 0:64], OP.mult), drain=False)
            bk.consumed(p, 0, hv)
            bk.consume(p, 1, "v")
            hv = p.op("v", lambda e, bb2=bb2: e.tensor_tensor(
                pwh("N1", 1), bb2[0:64, 0:64], cNMASK.ap()[:64, 64:128], OP.mult), drain=False)
            bk.consumed(p, 1, hv)
            ba, bb2 = pairlow(2, 3, kf, bf)
            bk.consume(p, 2, "v")
            hv = p.op("v", lambda e, ba=ba: e.tensor_tensor(
                pwh("NS1", 0), ba[0:64, 0:64], cNMASKT.ap()[:64, 0:64], OP.mult), drain=False)
            bk.consumed(p, 2, hv)
            bk.consume(p, 3, "v")
            hv = p.op("v", lambda e, bb2=bb2: e.tensor_tensor(
                pwh("NS1", 1), bb2[0:64, 0:64], cNMASKT.ap()[:64, 64:128], OP.mult), drain=False)
            bk.consumed(p, 3, hv)
            if DCUT is not None and DCUT <= 1: return
            b5 = bk.acquire(p, 5)
            hv = p.top(lambda e, b5=b5, kf=kf: e.transpose(b5[0:64, 0:128], kf, ID[:, :]))
            bk.produced(p, 5, hv)
            bk.consume(p, 5, "v")
            hv = p.op("v", lambda e, b5=b5: e.tensor_copy(dKts.ap()[:64, :], b5[0:64, 0:128]), drain=False)
            bk.consumed(p, 5, hv)
            if DCUT is not None and DCUT <= 2: return
            levels = [("N2", "NS1", "N1", "NS2", "N1", "NS1"),
                      ("N4", "NS2", "N2", "NS4", "N2", "NS2"),
                      ("N8", "NS4", "N4", "NS8", "N4", "NS4"),
                      ("N16", "NS8", "N8", "NS16", "N8", "NS8"),
                      ("N32", "NS16", "N16", None, None, None)]
            for dstA, laA, lbA, dstB, laB, lbB in levels:
                bA = bk.acquire(p, 2)
                bB = bk.acquire(p, 3) if dstB is not None else None
                wv = p.sem["v"].cur()
                p.wait("t", wv)
                for h in range(2):
                    hvA = p.top(lambda e, bA=bA, laA=laA, lbA=lbA, h=h: e.matmul(
                        bA[0:64, h * 64:h * 64 + 64], pw(laA)[:, h * 64:h * 64 + 64],
                        pw(lbA)[:, h * 64:h * 64 + 64],
                        start=(h == 0), stop=True, tile_position=(0, 0),
                        skip_group_check=(h == 1)))
                bk.produced(p, 2, hvA)
                if dstB is not None:
                    for h in range(2):
                        hvB = p.top(lambda e, bB=bB, laB=laB, lbB=lbB, h=h: e.matmul(
                            bB[0:64, h * 64:h * 64 + 64], pw(laB)[:, h * 64:h * 64 + 64],
                            pw(lbB)[:, h * 64:h * 64 + 64],
                            start=(h == 0), stop=True, tile_position=(0, 0),
                            skip_group_check=(h == 1)))
                    bk.produced(p, 3, hvB)
                bk.consume(p, 2, "v")
                hv = p.op("v", lambda e, bA=bA, dstA=dstA: e.tensor_copy(pw(dstA), bA[0:64, 0:128]),
                          drain=False)
                bk.consumed(p, 2, hv)
                if dstB is not None:
                    bk.consume(p, 3, "v")
                    hv = p.op("v", lambda e, bB=bB, dstB=dstB: e.tensor_copy(pw(dstB), bB[0:64, 0:128]),
                              drain=False)
                    bk.consumed(p, 3, hv)
            if DCUT is not None and DCUT <= 3: return
            p.op("v", lambda e: e.tensor_tensor(dP.ap()[:64, :], cI64S.ap()[:64, :],
                                                pw("NS1"), OP.add), drain=False)
            for k in (1, 2, 3, 4, 5):
                nm = f"N{2 ** k}"
                p.op("v", lambda e, nm=nm: e.tensor_tensor(pw("PI"), cI64S.ap()[:64, :],
                                                           pw(nm), OP.add), drain=False)
                b4 = bk.acquire(p, 4)
                wv = p.sem["v"].cur()
                p.wait("t", wv)
                for h in range(2):
                    hv = p.top(lambda e, b4=b4, h=h: e.matmul(
                        b4[0:64, h * 64:h * 64 + 64], pwh("PI", h),
                        dP.ap()[:64, h * 64:h * 64 + 64],
                        start=True, stop=True, tile_position=(0, 0),
                        skip_group_check=(h == 1)))
                bk.produced(p, 4, hv)
                bk.consume(p, 4, "v")
                hv = p.op("v", lambda e, b4=b4: e.tensor_copy(dP.ap()[:64, :], b4[0:64, 0:128]), drain=False)
                bk.consumed(p, 4, hv)
            if DCUT is not None and DCUT <= 4: return
            b0 = bk.acquire(p, 0)
            hv = p.top(lambda e, b0=b0, v3f=v3f: e.transpose(b0[0:64, 0:128], v3f, ID[:, :]))
            bk.produced(p, 0, hv)
            wv = p.sem["v"].cur()
            p.wait("t", wv)
            b1 = bk.acquire(p, 1)
            hv = p.top(lambda e, b1=b1, nbf=nbf, Wsl=Wsl: e.matmul(
                b1[0:64, 0:64], nbf[0:64, :], Wsl[0:64, :],
                start=True, stop=True, tile_position=(0, 0)))
            bk.produced(p, 1, hv)
            b5g = bk.acquire(p, 5)
            hv = p.top(lambda e, b5g=b5g, nbf=nbf, Wsl=Wsl: e.matmul(
                b5g[0:64, 0:64], nbf[64:128, :], Wsl[64:128, :],
                start=True, stop=True, tile_position=(64, 0)))
            bk.produced(p, 5, hv)
            bk.consume(p, 0, "v")
            hv = p.op("v", lambda e, b0=b0: e.tensor_copy(dG.ap()[:64, :], b0[0:64, 0:128]))
            bk.consumed(p, 0, hv)
            bk.consume(p, 1, "v")
            hv = p.op("v", lambda e, b1=b1: e.tensor_tensor(
                dG.ap()[:64, 0:64], dG.ap()[:64, 0:64], b1[0:64, 0:64], OP.add), drain=False)
            bk.consumed(p, 1, hv)
            bk.consume(p, 5, "v")
            hv = p.op("v", lambda e, b5g=b5g: e.tensor_tensor(
                dG.ap()[:64, 64:128], dG.ap()[:64, 64:128], b5g[0:64, 0:64], OP.add), drain=False)
            bk.consumed(p, 5, hv)
            if DCUT is not None and DCUT <= 5: return
            b1 = bk.acquire(p, 1)
            wv = p.sem["v"].cur()
            p.wait("t", wv)
            for h in range(2):
                hv = p.top(lambda e, b1=b1, h=h: e.matmul(
                    b1[0:64, h * 64:h * 64 + 64], dP.ap()[:64, h * 64:h * 64 + 64],
                    dG.ap()[:64, h * 64:h * 64 + 64],
                    start=(h == 0), stop=True, tile_position=(0, 0),
                    skip_group_check=(h == 1)))
            bk.produced(p, 1, hv)
            bk.consume(p, 1, "v")
            p.op("v", lambda e, b1=b1: e.tensor_copy(dU.ap()[:64, :], b1[0:64, 0:128]), drain=False)
            hv = p.op("v", lambda e, b1=b1, wcol=wcol: e.tensor_scalar(
                dUw.ap()[:64, :], b1[0:64, 0:128], wcol, None, OP.mult), drain=False)
            bk.consumed(p, 1, hv)
            if DCUT is not None and DCUT <= 6: return
            b2 = bk.acquire(p, 2)
            p.top(lambda e, b2=b2, Wasl=Wasl, qf=qf: e.matmul(
                b2[0:64, 0:64], Wasl[0:64, :], qf[0:64, :],
                start=True, stop=True, tile_position=(0, 0)))
            hv = p.top(lambda e, b2=b2, Wasl=Wasl, qf=qf: e.matmul(
                b2[64:128, 0:64], Wasl[64:128, :], qf[64:128, :],
                start=True, stop=True, tile_position=(64, 64), skip_group_check=True))
            bk.produced(p, 2, hv)
            b3 = bk.acquire(p, 3)
            p.top(lambda e, b3=b3, Wsl=Wsl, qf=qf: e.matmul(
                b3[0:64, 0:64], Wsl[0:64, :], qf[0:64, :],
                start=True, stop=True, tile_position=(0, 0)))
            hv = p.top(lambda e, b3=b3, Wsl=Wsl, qf=qf: e.matmul(
                b3[64:128, 0:64], Wsl[64:128, :], qf[64:128, :],
                start=True, stop=True, tile_position=(64, 64), skip_group_check=True))
            bk.produced(p, 3, hv)
            b4 = bk.acquire(p, 4)
            hv = p.top(lambda e, b4=b4, kf=kf, qf=qf: e.matmul(
                b4[0:64, 0:64], kf[0:64, :], qf[0:64, :],
                start=True, stop=True, tile_position=(0, 0)))
            bk.produced(p, 4, hv)
            b0 = bk.acquire(p, 0)
            hv = p.top(lambda e, b0=b0, kf=kf, qf=qf: e.matmul(
                b0[0:64, 0:64], kf[64:128, :], qf[64:128, :],
                start=True, stop=True, tile_position=(64, 0)))
            bk.produced(p, 0, hv)
            bk.consume(p, 4, "v")
            hv = p.op("v", lambda e, b4=b4, tmt=tmt: e.tensor_tensor(
                dT1.ap()[:64, 0:64], b4[0:64, 0:64], tmt, OP.mult), drain=False)
            bk.consumed(p, 4, hv)
            bk.consume(p, 0, "v")
            hv = p.op("v", lambda e, b0=b0, tmt=tmt: e.tensor_tensor(
                dT1.ap()[:64, 64:128], b0[0:64, 0:64], tmt, OP.mult), drain=False)
            bk.consumed(p, 0, hv)
            if DCUT is not None and DCUT <= 7: return
            b5o = bk.acquire(p, 5)
            wv = p.sem["v"].cur()
            p.wait("t", wv)
            p.top(lambda e, b5o=b5o: e.matmul(
                b5o[0:64, 0:64], dU.ap()[:64, 0:64], dT1.ap()[:64, 0:64],
                start=True, stop=True, tile_position=(0, 0)))
            hv = p.top(lambda e, b5o=b5o: e.matmul(
                b5o[64:128, 0:64], dU.ap()[:64, 64:128], dT1.ap()[:64, 64:128],
                start=True, stop=True, tile_position=(0, 64), skip_group_check=True))
            bk.produced(p, 5, hv)
            bk.consume(p, 3, "v")
            p.op("v", lambda e, b3=b3: e.tensor_copy(dT2.ap()[:, :], b3[:, 0:64]))
            bk.consume(p, 2, "v")
            hv = p.op("v", lambda e, b2=b2: e.scalar_tensor_tensor(
                dCmb.ap()[:, :], dT2.ap()[:, :], -1.0, b2[:, 0:64], OP.mult, OP.add))
            bk.consumed(p, 2, hv)
            p.op("v", lambda e, ab_=ab_: e.tensor_tensor(dCmb.ap()[:, :], dCmb.ap()[:, :],
                                                         ab_, OP.mult))
            hv = p.op("v", lambda e, b3=b3: e.tensor_tensor(
                dCmb.ap()[:, :], dCmb.ap()[:, :], b3[:, 0:64], OP.add))
            bk.consumed(p, 3, hv)
            bk.consume(p, 5, "v")
            hv = p.op("v", lambda e, b5o=b5o, osl=osl: e.tensor_tensor(
                osl, dCmb.ap()[:, :], b5o[:, 0:64], OP.add), drain=False)
            bk.consumed(p, 5, hv)
            if gi == 0 and g == 0:
                xbar(5)
            if gi == 0 and g == 0 and BAR == 6:
                p.wait("g", p.sem["v"].cur())
                p.wait("g", p.sem["s"].cur())
                semz = p.new_sem("bar6")
                hvz = p.dma("g", ddbg[0][:1, 0:1], dKts.ap()[:1, 0:1], slow=True, sem=semz)
                p.wait("v", hvz)
                p.wait("s", hvz)
                p.wait("t", hvz)
            if gi == 0 and g == 0 and BAR == 7:
                dump_dbg([(0, pw("N1")), (1, dP.ap()[:64, :]), (2, dG.ap()[:64, :]),
                          (3, dU.ap()[:64, :]), (4, osl), (5, dKts.ap()[:64, :])])
            if DBG == 4 and gi == DBGCH and g == 0:
                dump_dbg([(0, pw("N1")), (1, dP.ap()[:64, :]), (2, dG.ap()[:64, :]),
                          (3, dU.ap()[:64, :]), (4, osl), (5, dKts.ap()[:64, :])])
            if DBG == 6 and gi == DBGCH and g == 0:
                dump_dbg([(0, Wsl), (1, Wasl), (2, dU.ap()[:64, :]),
                          (3, dUw.ap()[:64, :]), (4, v3f), (5, bf)])
            if DCUT is not None and DCUT <= 8: return
            p.op("v", lambda e, Wsl=Wsl, c2c=c2c: e.tensor_scalar(
                dWmix.ap()[:, :], Wsl, c2c, None, OP.mult))
            p.op("v", lambda e, Wasl=Wasl, c1c=c1c: e.scalar_tensor_tensor(
                dWmix.ap()[:, :], Wasl, c1c, dWmix.ap()[:, :], OP.mult, OP.add))
            b1w = bk.acquire(p, 1)
            wv = p.sem["v"].cur()
            p.wait("t", wv)
            p.top(lambda e, b1w=b1w: e.matmul(
                b1w[0:64, 0:64], dKts.ap()[:64, 0:64], dU.ap()[:64, 0:64],
                start=True, stop=True, tile_position=(0, 0)))
            hv = p.top(lambda e, b1w=b1w: e.matmul(
                b1w[64:128, 0:64], dKts.ap()[:64, 64:128], dU.ap()[:64, 64:128],
                start=True, stop=True, tile_position=(0, 64), skip_group_check=True))
            bk.produced(p, 1, hv)
            b4w = bk.acquire(p, 4)
            p.top(lambda e, b4w=b4w: e.matmul(
                b4w[0:64, 0:64], dKts.ap()[:64, 0:64], dUw.ap()[:64, 0:64],
                start=True, stop=True, tile_position=(0, 0)))
            hv = p.top(lambda e, b4w=b4w: e.matmul(
                b4w[64:128, 0:64], dKts.ap()[:64, 64:128], dUw.ap()[:64, 64:128],
                start=True, stop=True, tile_position=(0, 64), skip_group_check=True))
            bk.produced(p, 4, hv)
            bk.consume(p, 1, "v")
            hv = p.op("v", lambda e, b1w=b1w, Wsl=Wsl: e.tensor_tensor(
                Wsl, Wsl, b1w[:, 0:64], OP.add), drain=False)
            bk.consumed(p, 1, hv)
            bk.consume(p, 4, "v")
            hv = p.op("v", lambda e, b4w=b4w, Wasl=Wasl: e.tensor_tensor(
                Wasl, dWmix.ap()[:, :], b4w[:, 0:64], OP.add), drain=False)
            bk.consumed(p, 4, hv)

        nstrip = T // WB if BSTRIPS is None else BSTRIPS
        for s in range(nstrip):
            t0 = s * WB
            p.wait("v", p.sem["t"].cur())
            p.wait("s", p.sem["t"].cur())
            ln_stats(t0, WB, mbB, sbB, smallB, stripB, bankbase=6)
            p.wait("v", p.sem["s"].cur())
            for c in range(NCH):
                ln = lnB.ap()[:, c * WB:(c + 1) * WB]
                p.op("v", lambda e, c=c, ln=ln, t0=t0: e.tensor_tensor(
                    ln, xc(c, t0, WB), mbB.ap()[:, :], OP.subtract), drain=False)
            p.drain("v")
            for c in range(NCH):
                ln = lnB.ap()[:, c * WB:(c + 1) * WB]
                p.op("v", lambda e, ln=ln: e.tensor_tensor(ln, ln, sbB.ap()[:, :], OP.mult),
                     drain=False)
            hv_ln = p.sem["v"].cur()
            p.wait("t", hv_ln)
            if BCUT is not None and BCUT <= 1: continue
            r3k = bk0.ap()[:, :].rearrange("a (c w) -> a c w", w=264)
            r3q = bq0.ap()[:, :].rearrange("a (c w) -> a c w", w=264)
            r3v = bv0.ap()[:, :].rearrange("a (c w) -> a c w", w=264)
            lsv = leadsv.ap()[:, 0:24].rearrange("a (c w) -> a c w", w=3)
            if s > 0:
                for r3 in (r3k, r3q):
                    p.op("v", lambda e, r3=r3: e.tensor_copy(r3[:, :, 0:3], r3[:, :, 256:259]))
                p.op("v", lambda e, lsv=lsv, r3v=r3v: e.tensor_copy(r3v[:, :, 0:3], lsv[:, :, :]))
            hv_lead = p.sem["v"].cur()

            def mk_out(r3, bias_t):
                def out_fn(m, b, r3=r3, bias_t=bias_t):
                    return p.op("v", lambda e, m=m, b=b, r3=r3, bias_t=bias_t:
                                e.tensor_scalar(r3[:, m, 3:259], b[:, 0:WB],
                                                bias_t.ap()[:, m:m + 1], None, OP.add))
                return out_fn
            lnf = lambda k: lnB.ap()[:, k * WB:(k + 1) * WB]
            p.wait("t", hv_lead)
            stream_mm(scr["k"], lnf, mk_out(r3k, bias_k), nbank0=0)
            stream_mm(scr["q"], lnf, mk_out(r3q, bias_q), nbank0=0)
            stream_mm(scr["v"], lnf, mk_out(r3v, bias_v), nbank0=0)
            if s == 0:
                xbar(1)
            if BCUT is not None and BCUT <= 2: continue
            b4 = bk.acquire(p, 4)
            hv = None
            for k in range(8):
                hv = p.top(lambda e, b4=b4, k=k: e.matmul(
                    b4[:16, 0:WB], cWbT.ap()[:, k * 16:(k + 1) * 16], lnf(k),
                    start=(k == 0), stop=(k == 7)))
            bk.produced(p, 4, hv)
            bk.consume(p, 4, "s")
            bstrip = stripB.ap()[:16, 0:WB]
            hv = p.op("s", lambda e, b4=b4, bstrip=bstrip: e.activation(
                bstrip, b4[:16, 0:WB], AF.Sigmoid, bias=bias_beta.ap()[:16, 0:1]))
            bk.consumed(p, 4, hv)
            p.wait("t", hv)
            for c in range(NCH):
                b3 = bk.acquire(p, 3)
                hv = p.top(lambda e, c=c, b3=b3, bstrip=bstrip: e.matmul(
                    b3[:, 0:WB], cS16T.ap()[:16, c * 128:(c + 1) * 128], bstrip,
                    start=True, stop=True))
                bk.produced(p, 3, hv)
                bk.consume(p, 3, "v")
                hv2 = p.op("v", lambda e, c=c, b3=b3: e.tensor_copy(
                    betab.ap()[:, c * WB:(c + 1) * WB], b3[:, 0:WB]))
                bk.consumed(p, 3, hv2)
            if BCUT is not None and BCUT <= 3: continue
            conv4(bk0, lambda c: bkh.ap()[:, c * WB:(c + 1) * WB], convk)
            conv4(bq0, lambda c: bfm.ap()[:, c * WB:(c + 1) * WB], convq)
            conv4(bv0, lambda c: lnB.ap()[:, c * WB:(c + 1) * WB], convv)
            if BCUT is not None and BCUT <= 4: continue
            headnorm(bkh, 256, is_rms=False)
            headnorm(bfm, 512, is_rms=False)
            if BCUT is not None and BCUT <= 5: continue
            hvv = p.sem["v"].cur()
            p.wait("s", hvv)
            for c in range(NCH):
                sl = lnB.ap()[:, c * WB:(c + 1) * WB]
                hv = p.op("s", lambda e, sl=sl: e.activation(sl, sl, AF.Silu))
            p.wait("v", p.sem["s"].cur())
            for c in range(NCH):
                p.op("v", lambda e, c=c: e.tensor_tensor(
                    nbfm.ap()[:, c * WB:(c + 1) * WB], betab.ap()[:, c * WB:(c + 1) * WB],
                    lnB.ap()[:, c * WB:(c + 1) * WB], OP.mult))
            p.op("v", lambda e, lsv=lsv, r3v=r3v: e.tensor_copy(lsv[:, :, :], r3v[:, :, 256:259]))
            for c in range(NCH):
                bb_ = betab.ap()[:, c * WB:(c + 1) * WB]
                p.op("v", lambda e, c=c, bb_=bb_: e.tensor_tensor(
                    bb_, bb_, bkh.ap()[:, c * WB:(c + 1) * WB], OP.mult))
                p.op("v", lambda e, c=c, bb_=bb_, r3v=r3v: e.tensor_scalar(
                    r3v[:, c, 3:259], bb_, -1.0, None, OP.mult))
            hv_pre = p.sem["v"].cur()
            p.wait("t", hv_pre)
            if BCUT is not None and BCUT <= 6: continue
            if DBG == 5 and s == DBGCH // 4:
                dump_dbg([(0, bkh.ap()[:, 0:2048]), (1, bfm.ap()[:, 0:2048]),
                          (2, betab.ap()[:, 0:2048]), (3, nbfm.ap()[:, 0:2048]),
                          (5, bk0.ap()[:, 0:2048])])
            for ii in range(4):
                gi = s * 4 + ii
                for g in range(NCH):
                    delta_chunk(gi, ii, g)
            if BCUT is not None and BCUT <= 7: continue
            if s == 0:
                xbar(3)
            headnorm(lnB, 768, is_rms=True)
            if BCUT is not None and BCUT <= 8: continue
            hv_o = p.sem["v"].cur()
            p.wait("t", hv_o)
            def out_res(m, b):
                hv = p.op("v", lambda e, m=m, b=b: e.tensor_scalar(
                    smallB.ap()[:, 0:WB], b[:, 0:WB], bias_out.ap()[:, m:m + 1], None, OP.add))
                p.op("v", lambda e, m=m, t0=t0: e.tensor_tensor(
                    xc(m, t0, WB), xc(m, t0, WB), smallB.ap()[:, 0:WB], OP.add))
                return hv
            stream_mm(scr["out"], lnf if False else (lambda k: lnB.ap()[:, k * WB:(k + 1) * WB]),
                      out_res, nbank0=0)
            if s == 0:
                xbar(4)

    if stage >= 2:
        emit_ffn()
    if stage >= 3:
        emit_attn_b()
    if stage >= 4:
        emit_ffn()

    p.wait("t", p.sem["v"].cur())
    p.wait("t", p.sem["s"].cur())
    sem_out = [p.new_sem("outs0"), p.new_sem("outs1")]
    out_free = [None, None]
    for j in range(16):
        buf = xld.ap()[:, (j % 2) * 1024:(j % 2) * 1024 + 1024]
        if out_free[j % 2] is not None:
            p.wait("s", out_free[j % 2])
        for half in range(2):
            tiles = [xc(half * 4 + c, j * 128, 128) for c in range(4)]
            def evac(b, half=half, buf=buf):
                hv2 = None
                for q in range(4):
                    c = half * 4 + q
                    hv2 = p.op("s", lambda e, c=c, q=q, b=b, buf=buf:
                               e.copy(buf[:, c * 128:(c + 1) * 128], b[:, q * 128:(q + 1) * 128]))
                return hv2
            transpose_group(tiles, 6 + half, evac, evac_engine="s")
        out_free[j % 2] = p.dma("s", dy[j * 128:(j + 1) * 128, :], buf, sem=sem_out[j % 2])
    for so in sem_out:
        p.wait("s", so.cur())
        p.wait("y", so.cur())
        p.wait("g", so.cur())

    with nc.Block() as block:
        p.emit(block)
    stack.close()
    return nc


def kernel(L0_attn_a_x, L0_attn_a_Wqkv, L0_attn_a_bqkv, L0_attn_a_Wproj, L0_attn_a_bproj,
           L0_attn_a_scale, L0_attn_b_Wk, L0_attn_b_bk, L0_attn_b_Wq, L0_attn_b_bq,
           L0_attn_b_Wv, L0_attn_b_bv, L0_attn_b_Wbeta, L0_attn_b_bbeta,
           L0_attn_b_conv_k, L0_attn_b_conv_q, L0_attn_b_conv_v, L0_attn_b_rms_w,
           L0_attn_b_Wout, L0_attn_b_bout, L0_ffn_w1, L0_ffn_conv_w, L0_ffn_conv_b,
           L0_ffn_w2):
    nc = build_program(STAGE)
    consts = build_constants()
    shared = {
        "aWqkv": np.ascontiguousarray(L0_attn_a_Wqkv), "abqkv": np.ascontiguousarray(L0_attn_a_bqkv),
        "aWproj": np.ascontiguousarray(L0_attn_a_Wproj), "abproj": np.ascontiguousarray(L0_attn_a_bproj),
        "ascale": np.reshape(np.asarray(L0_attn_a_scale, dtype=np.float32), [1]),
        "bWk": np.ascontiguousarray(L0_attn_b_Wk), "bbk": np.ascontiguousarray(L0_attn_b_bk),
        "bWq": np.ascontiguousarray(L0_attn_b_Wq), "bbq": np.ascontiguousarray(L0_attn_b_bq),
        "bWv": np.ascontiguousarray(L0_attn_b_Wv), "bbv": np.ascontiguousarray(L0_attn_b_bv),
        "bWbeta": np.ascontiguousarray(L0_attn_b_Wbeta), "bbbeta": np.ascontiguousarray(L0_attn_b_bbeta),
        "bconvk": np.ascontiguousarray(L0_attn_b_conv_k), "bconvq": np.ascontiguousarray(L0_attn_b_conv_q),
        "bconvv": np.ascontiguousarray(L0_attn_b_conv_v), "brms": np.ascontiguousarray(L0_attn_b_rms_w),
        "bWout": np.ascontiguousarray(L0_attn_b_Wout), "bbout": np.ascontiguousarray(L0_attn_b_bout),
        "fw1": np.ascontiguousarray(L0_ffn_w1),
        "fconvw": np.ascontiguousarray(np.reshape(L0_ffn_conv_w, (DH, 3))),
        "fconvb": np.ascontiguousarray(L0_ffn_conv_b),
        "fw2": np.ascontiguousarray(L0_ffn_w2),
    }
    for k, v in consts.items():
        if k == "IDTB":
            shared["c_" + k] = np.ascontiguousarray(v.astype(ml_dtypes.bfloat16))
        else:
            shared["c_" + k] = np.ascontiguousarray(v.astype(np.float32))
    xin = np.ascontiguousarray(np.asarray(L0_attn_a_x, dtype=np.float32))
    in_maps = []
    for b in range(8):
        m = dict(shared)
        m["x_in"] = np.ascontiguousarray(xin[b])
        in_maps.append(m)
    res = run_bass_kernel_spmd(nc, in_maps, core_ids=list(range(8)))
    out = np.stack([res.results[b]["y_out"] for b in range(8)], axis=0)
    kernel.last_results = res
    return out
